# version 23
# baseline (speedup 1.0000x reference)
import contextlib
import numpy as np

import concourse.bass as bass
import concourse.mybir as mybir
from concourse.bass_utils import run_bass_kernel_spmd

F32 = mybir.dt.float32
BF16 = mybir.dt.bfloat16
AF = mybir.ActivationFunctionType
ALU = mybir.AluOpType
AX = mybir.AxisListType

MAGIC = float(np.float32(3 * 2.0 ** 22))
D = 1024
NH = 16
DH = 64
HID = 4096
EPS = 1e-5

ENGS = ("sync", "tensor", "scalar", "vector", "gpsimd")
DBG = {}


class Ctr:
    def __init__(self, nc, es, name):
        self.sem = es.enter_context(nc.semaphore(name))
        self.v = 0

    def inc(self, h, k=1):
        h.then_inc(self.sem, k)
        self.v += k
        return self.v


class Handle:
    def __init__(self, entry):
        self.entry = entry

    def then_inc(self, sem, k):
        self.entry["incs"].append((sem, k))
        return self


class EngRec:
    def __init__(self, ops):
        self._ops = ops

    def __getattr__(self, name):
        ops = self._ops

        def call(*args, **kwargs):
            entry = {"name": name, "args": args, "kwargs": kwargs, "incs": []}
            ops.append(entry)
            return Handle(entry)

        return call


class Rec:
    def __init__(self, nc):
        self.nc = nc
        self._reset()

    def _reset(self):
        self.ops = {e: [] for e in ENGS}
        for e in ENGS:
            setattr(self, e, EngRec(self.ops[e]))

    def wait(self, engrec, ctr, v=None):
        if v is None:
            v = ctr.v
        assert v <= ctr.v, "wait on future value -> deadlock"
        if v > 0:
            engrec.wait_ge(ctr.sem, v)

    def emit(self):
        def replay(ops, eng):
            for en in ops:
                h = getattr(eng, en["name"])(*en["args"], **en["kwargs"])
                for sem, k in en["incs"]:
                    h.then_inc(sem, k)

        with self.nc.Block() as b:
            for e in ENGS:
                if self.ops[e]:
                    getattr(b, e)(
                        (lambda ops: (lambda eng: replay(ops, eng)))(self.ops[e]))
        self._reset()


def build_program(nseq, S):
    NP = S // 512
    NT = S // 128
    NCT = S // 512
    SP2 = S + 2

    nc = bass.Bass("TRN2", target_bir_lowering=False, debug=False)

    t_x = nc.dram_tensor("x", [nseq, S, D], F32, kind="ExternalInput").ap()
    t_qw = nc.dram_tensor("qw", [NH, D, DH], F32, kind="ExternalInput").ap()
    t_kw = nc.dram_tensor("kw", [NH, D, DH], F32, kind="ExternalInput").ap()
    t_vw = nc.dram_tensor("vw", [NH, D, DH], F32, kind="ExternalInput").ap()
    t_ow = nc.dram_tensor("ow", [NH, DH, D], F32, kind="ExternalInput").ap()
    t_qb = nc.dram_tensor("qbias", [NH, DH], F32, kind="ExternalInput").ap()
    t_kb = nc.dram_tensor("kbias", [NH, DH], F32, kind="ExternalInput").ap()
    t_vb = nc.dram_tensor("vbias", [NH, DH], F32, kind="ExternalInput").ap()
    t_ob = nc.dram_tensor("obias", [D], F32, kind="ExternalInput").ap()
    t_w1 = nc.dram_tensor("w1", [HID, D], F32, kind="ExternalInput").ap()
    t_cw = nc.dram_tensor("cw", [HID, 1, 3], F32, kind="ExternalInput").ap()
    t_cb = nc.dram_tensor("cb", [HID], F32, kind="ExternalInput").ap()
    t_w2 = nc.dram_tensor("w2", [D, HID], F32, kind="ExternalInput").ap()
    t_idb = nc.dram_tensor("ident_bf", [128, 128], BF16, kind="ExternalInput").ap()
    t_idf = nc.dram_tensor("ident_f32", [128, 128], F32, kind="ExternalInput").ap()
    t_tri = nc.dram_tensor("trimask", [128, 128], BF16, kind="ExternalInput").ap()
    t_o8 = nc.dram_tensor("ones8s", [8, 128, 8], BF16, kind="ExternalInput").ap()
    t_out = nc.dram_tensor("out", [nseq, S, D], F32, kind="ExternalOutput").ap()
    t_w1t = nc.dram_tensor("w1t_s", [32, 8, 128, 128], BF16, kind="Internal").ap()
    t_wq4 = nc.dram_tensor("wq4_s", [4, 128, 8192], BF16, kind="Internal").ap()
    t_w2t = nc.dram_tensor("w2t_s", [32, 128, D], BF16, kind="Internal").ap()

    es = contextlib.ExitStack()
    sb = lambda n, shape, dt: es.enter_context(nc.sbuf_tensor(n, shape, dt))

    ident_bf = sb("c_idb", [128, 128], BF16)
    ident_f32 = sb("c_idf", [128, 128], F32)
    trimask = sb("c_tri", [128, 128], BF16)
    ones8s = sb("c_o8", [128, 64], BF16)
    ones1b = sb("c_1b", [1, 128], BF16)
    ones1f = sb("c_1f", [1, 128], F32)
    onescol = sb("c_oc", [128, 1], F32)
    qb_t = sb("c_qb", [128, 8], F32)
    kb_t = sb("c_kb", [128, 8], F32)
    vb_row = sb("c_vb", [1, D], BF16)
    ob_row = sb("c_ob", [1, D], BF16)
    cw_t = sb("c_cw", [128, 96], F32)
    cb_t = sb("c_cb", [128, 32], F32)
    sw1_bc = sb("c_s1", [128, 1], F32)
    mw1_bc = sb("c_m1", [128, 1], F32)
    sw2_bc = sb("c_s2", [128, 1], F32)
    mw2_bc = sb("c_m2", [128, 1], F32)
    row1 = sb("c_r1", [1, 16], F32)
    magic_bc = sb("c_mg", [128, 1], F32)
    nmagic_bc = sb("c_nmg", [128, 1], F32)

    ctrs = {n: Ctr(nc, es, n) for n in
            ("bar", "dl", "dw", "dst", "tn", "sc", "ve", "gp",
             "dlx0", "dlx1", "dlx2", "dlx3", "dlb0", "dlb1",
             "dw1a", "dw1b", "dw1c", "dw2a", "dw2b")}
    bar, dl, dw, dst = ctrs["bar"], ctrs["dl"], ctrs["dw"], ctrs["dst"]
    tn, sc, ve, gp = ctrs["tn"], ctrs["sc"], ctrs["ve"], ctrs["gp"]
    dlx = (ctrs["dlx0"], ctrs["dlx1"], ctrs["dlx2"], ctrs["dlx3"])
    dlb = (ctrs["dlb0"], ctrs["dlb1"])
    dw1 = (ctrs["dw1a"], ctrs["dw1b"], ctrs["dw1c"])
    dw2 = (ctrs["dw2a"], ctrs["dw2b"])

    r = Rec(nc)

    def barrier():
        base = bar.v
        for e in ENGS:
            er = getattr(r, e)
            bar.inc(er.nop() if e in ("sync", "gpsimd") else er.drain())
        for e in ENGS:
            getattr(r, e).wait_ge(bar.sem, base + 5)

    dl.inc(r.sync.dma_start(out=ident_bf[:, :], in_=t_idb[:, :]), 16)
    dl.inc(r.sync.dma_start(out=ident_f32[:, :], in_=t_idf[:, :]), 16)
    dl.inc(r.sync.dma_start(out=trimask[:, :], in_=t_tri[:, :]), 16)
    dl.inc(r.sync.dma_start(
        out=ones8s.rearrange("p (hp j) -> p hp j", hp=8),
        in_=t_o8.rearrange("hp p j -> p hp j")), 16)
    dl.inc(r.sync.dma_start(
        out=qb_t[:, :], in_=t_qb.rearrange("(hp hh) d -> (hh d) hp", hh=2),
        allow_slow_non_contiguous=True), 16)
    dl.inc(r.sync.dma_start(
        out=kb_t[:, :], in_=t_kb.rearrange("(hp hh) d -> (hh d) hp", hh=2),
        allow_slow_non_contiguous=True), 16)
    dl.inc(r.sync.dma_start(
        out=cw_t.rearrange("p (j hc) -> p j hc", j=3),
        in_=t_cw.rearrange("(hc p) one j -> p j (one hc)", p=128),
        allow_slow_non_contiguous=True), 16)
    dl.inc(r.sync.dma_start(
        out=cb_t[:, :], in_=t_cb.rearrange("(hc p) -> p hc", p=128),
        allow_slow_non_contiguous=True), 16)

    with nc.sbuf_tensor("p_vst", [1, 2 * D], F32) as vbst:
        dl.inc(r.sync.dma_start(out=vbst[:, 0:D],
                                in_=t_vb.rearrange("h d -> (h d)")[None, :]), 16)
        dl.inc(r.sync.dma_start(out=vbst[:, D:2 * D], in_=t_ob[None, :]), 16)
        r.wait(r.vector, dl)
        ve.inc(r.vector.tensor_copy(vb_row[:, :], vbst[:, 0:D]))
        ve.inc(r.vector.tensor_copy(ob_row[:, :], vbst[:, D:2 * D]))
        ve.inc(r.vector.memset(ones1b[:, :], 1.0))
        ve.inc(r.vector.memset(ones1f[:, :], 1.0))
        ve.inc(r.vector.memset(onescol[:, :], 1.0))
        ve.inc(r.vector.memset(magic_bc[:, :], MAGIC))
        ve.inc(r.vector.memset(nmagic_bc[:, :], -MAGIC))
        ve.inc(r.vector.tensor_tensor(trimask[:, :], trimask[:, :],
                                      ident_bf[:, :], ALU.subtract))
        ve.inc(r.vector.drain())
        ve.inc(r.vector.tensor_single_scalar(trimask[:, :], trimask[:, :],
                                             -10000.0, ALU.mult))
        r.emit()
        barrier()
        r.emit()

    with nc.sbuf_tensor("p_wst", [128, 4096], F32) as wstg, \
         nc.sbuf_tensor("p_wbf", [128, 8192], BF16) as wbf:
        ve_cast = [0, 0]
        dw_w = 0
        for i4, t_w in enumerate((t_qw, t_kw, t_vw, t_ow)):
            if i4 < 3:
                wsrc = t_w.rearrange("h (mc p) d -> mc p h d", p=128)
            else:
                wsrc = t_ow.rearrange("(hp hh) d m -> hp (hh d) m", hh=2)
            for k4 in range(4):
                par = k4 % 2
                if i4 + k4 > 0:
                    r.wait(r.sync, ve, ve_cast[par])
                if i4 < 3:
                    for mm in range(2):
                        dlx[par].inc(r.sync.dma_start(
                            out=wstg[:, par * 2048 + mm * 1024:
                                par * 2048 + (mm + 1) * 1024].rearrange(
                                "p (h d) -> p h d", h=NH),
                            in_=wsrc[2 * k4 + mm]), 16)
                else:
                    dlx[par].inc(r.sync.dma_start(
                        out=wstg[:, par * 2048:(par + 1) * 2048].rearrange(
                            "p (hp m) -> p hp m", hp=2),
                        in_=wsrc[2 * k4:2 * k4 + 2].rearrange(
                            "hp pd m -> pd hp m")), 16)
                r.wait(r.vector, dlx[par])
                if i4 + k4 > 0:
                    r.wait(r.vector, dw, dw_w)
                ve_cast[par] = ve.inc(r.vector.tensor_copy(
                    wbf[:, k4 * 2048:(k4 + 1) * 2048],
                    wstg[:, par * 2048:(par + 1) * 2048]))
                ve.inc(r.vector.drain())
                r.wait(r.gpsimd, ve, ve_cast[par])
                dw_w = dw.inc(r.gpsimd.dma_start(
                    out=t_wq4[i4, :, k4 * 2048:(k4 + 1) * 2048],
                    in_=wbf[:, k4 * 2048:(k4 + 1) * 2048]), 16)
        r.wait(r.gpsimd, dw, dw.v)
        r.emit()
        barrier()
        r.emit()

    def wprep_quant(t_w, sw_bc, mw_bc, which):
        wn = which
        with nc.sbuf_tensor(f"wq_st{wn}", [128, 2 * 4096], F32) as stg2, \
             nc.sbuf_tensor(f"wq_f{wn}", [128, 4096], F32) as qf, \
             nc.sbuf_tensor(f"wq_b{wn}", [128, 4096], BF16) as qb8, \
             nc.sbuf_tensor(f"wq_sp{wn}", [128, 4096], BF16) as strip, \
             nc.psum_tensor(f"wq_tp{wn}", [128, 1024], BF16) as tpp:
            ve_q = 0
            dw_s = 0
            ve_g = 0
            sc_q = [0, 0]
            for blk in range(8):
                par = blk % 2
                stg = stg2[:, par * 4096:(par + 1) * 4096]
                if blk >= 2:
                    r.wait(r.sync, sc, sc_q[par])
                if which == "w1":
                    src = t_w.rearrange("(hc p) (fc f) -> fc p hc f",
                                        p=128, f=128)[blk]
                    dlx[par].inc(r.sync.dma_start(
                        out=stg.rearrange("p (hc f) -> p hc f", hc=32),
                        in_=src), 16)
                else:
                    src = t_w.rearrange("(oc p) (hg hq) -> hg p oc hq",
                                        p=128, hq=512)[blk]
                    dlx[par].inc(r.sync.dma_start(
                        out=stg.rearrange("p (oc hq) -> p oc hq", oc=8),
                        in_=src), 16)
                r.wait(r.scalar, dlx[par])
                if blk > 0:
                    r.wait(r.scalar, ve, ve_q)
                sc_q[par] = sc.inc(r.scalar.mul(qf[:, :], stg[:, :],
                                                sw_bc[:, 0:1]))
                r.wait(r.vector, sc)
                if blk > 0:
                    r.wait(r.vector, dw, dw_s)
                ve.inc(r.vector.tensor_scalar(qf[:, :], qf[:, :], -1.0, 1.0,
                                              ALU.max, ALU.min))
                ve.inc(r.vector.drain())
                ve_q = ve.inc(r.vector.tensor_scalar(
                    qb8[:, :], qf[:, :], MAGIC, -MAGIC, ALU.add, ALU.add))
                r.wait(r.tensor, ve, ve_q)
                for g in range(4):
                    if not (blk == 0 and g == 0):
                        r.wait(r.tensor, ve, ve_g)
                    for k in range(8):
                        if which == "w1":
                            src_t = qb8[:, (g * 8 + k) * 128:(g * 8 + k + 1) * 128]
                        else:
                            src_t = qb8[:, k * 512 + g * 128:
                                        k * 512 + (g + 1) * 128]
                        tn.inc(r.tensor.transpose(
                            tpp[:, k * 128:(k + 1) * 128], src_t,
                            ident_bf[:, :]))
                    r.wait(r.vector, tn)
                    ve_g = ve.inc(r.vector.tensor_copy(
                        strip[:, g * 1024:(g + 1) * 1024], tpp[:, :]))
                r.wait(r.gpsimd, ve, ve_g)
                if which == "w1":
                    dw_s = dw.inc(r.gpsimd.dma_start(
                        out=t_w1t[:, blk].rearrange("hc p h -> p hc h"),
                        in_=strip.rearrange("p (hc h) -> p hc h", hc=32)), 16)
                else:
                    dw_s = dw.inc(r.gpsimd.dma_start(
                        out=t_w2t[blk * 4:(blk + 1) * 4].rearrange(
                            "hl p o -> p hl o"),
                        in_=strip.rearrange("p (hl o) -> p hl o", hl=4)), 16)
            r.wait(r.gpsimd, dw, dw.v)
            r.emit()
            barrier()


    r.emit()

    for s in range(nseq):
        es_s = contextlib.ExitStack()
        sbs = lambda n, shape, dt: es_s.enter_context(
            nc.sbuf_tensor(f"{n}_{s}", shape, dt))

        es_qkv = contextlib.ExitStack()
        Q_sb = es_qkv.enter_context(
            nc.sbuf_tensor(f"Q_{s}", [128, 8 * S], BF16))
        K_sb = es_qkv.enter_context(
            nc.sbuf_tensor(f"K_{s}", [128, 8 * S], BF16))
        V_sb = es_qkv.enter_context(
            nc.sbuf_tensor(f"V_{s}", [128, NT * NH * 65], BF16))

        es_a = contextlib.ExitStack()
        QWb = es_a.enter_context(nc.sbuf_tensor(f"a_qw{s}", [128, 8192], BF16))
        KWb = es_a.enter_context(nc.sbuf_tensor(f"a_kw{s}", [128, 8192], BF16))
        VWb = es_a.enter_context(nc.sbuf_tensor(f"a_vw{s}", [128, 8192], BF16))
        xhat = es_a.enter_context(nc.sbuf_tensor(f"a_xh{s}", [128, 8 * S], BF16))
        xt2 = es_a.enter_context(nc.sbuf_tensor(f"a_xt{s}", [128, 4 * D], F32))
        xh2 = es_a.enter_context(nc.sbuf_tensor(f"a_xm{s}", [128, 2 * D], BF16))
        st24 = es_a.enter_context(nc.sbuf_tensor(f"a_st{s}", [128, 32], F32))
        bnsa = es_a.enter_context(nc.sbuf_tensor(f"a_bn{s}", [128, 48], F32))
        mva = es_a.enter_context(nc.sbuf_tensor(f"a_mv{s}", [128, 8], F32))
        tpp = es_a.enter_context(nc.psum_tensor(f"a_tp{s}", [128, 1024], BF16))
        qkps = es_a.enter_context(nc.psum_tensor(f"a_qk{s}", [128, 1024], F32))
        vps = es_a.enter_context(nc.psum_tensor(f"a_vp{s}", [128, 2048], F32))
        if True:
            if True:
                for i4, Wb in ((0, QWb), (1, KWb), (2, VWb)):
                    dlx[i4 % 2].inc(r.sync.dma_start(
                        out=Wb[:, :], in_=t_wq4[i4]), 16)
                for i4 in range(3):
                    r.wait(r.vector, dlx[i4 % 2])
                ve.inc(r.vector.memset(
                    V_sb.rearrange("p (kt h e) -> p kt h e", kt=NT, h=NH)
                    [:, :, :, 64:65], 1.0))
                r.emit()
                barrier()

            ve_xh = [0] * NT
            ve_vv = [0] * NT
            ve_ap = [0] * NT
            tn_xh = [0] * NT
            sc_qk = 0
            first_qk = True
            var_sA = mva.rearrange("p (j two) -> p two j", two=2)[:, 1:2, :]
            for p in range(NP):
                for j in range(4):
                    u = 4 * p + j
                    if u >= 4:
                        r.wait(r.sync, ve, ve_ap[u - 4])
                    dlx[j].inc(r.sync.dma_start(
                        out=xt2[:, j * D:(j + 1) * D],
                        in_=t_x[s, u * 128:(u + 1) * 128, :]), 16)
                for j in range(4):
                    r.wait(r.vector, dlx[j])
                    ve.inc(r.vector.bn_stats(bnsa[:, j * 12:j * 12 + 6],
                                             xt2[:, j * D:j * D + 512]))
                    ve.inc(r.vector.bn_stats(bnsa[:, j * 12 + 6:j * 12 + 12],
                                             xt2[:, j * D + 512:(j + 1) * D]))
                ve.inc(r.vector.drain())
                for j in range(4):
                    ve.inc(r.vector.bn_aggr(mva[:, j * 2:(j + 1) * 2],
                                            bnsa[:, j * 12:j * 12 + 12]))
                ve.inc(r.vector.drain())
                ve.inc(r.vector.tensor_scalar_add(
                    st24.rearrange("p (a b) -> p a b", a=1)[:, :, 0:4],
                    var_sA, EPS))
                r.wait(r.scalar, ve)
                sc.inc(r.scalar.activation(st24[:, 4:8], st24[:, 0:4],
                                           AF.Sqrt))
                r.wait(r.vector, sc)
                ve.inc(r.vector.reciprocal(st24[:, 8:12], st24[:, 4:8]))
                ve.inc(r.vector.drain())
                ve.inc(r.vector.tensor_mul(st24[:, 12:16], st24[:, 8:12],
                                           st24[:, 8:12]))
                ve.inc(r.vector.drain())
                ve.inc(r.vector.tensor_mul(st24[:, 12:16], st24[:, 12:16],
                                           st24[:, 0:4]))
                ve.inc(r.vector.drain())
                ve.inc(r.vector.tensor_scalar(st24[:, 12:16], st24[:, 12:16],
                                              -0.5, 1.5, ALU.mult, ALU.add))
                ve.inc(r.vector.drain())
                ve.inc(r.vector.tensor_mul(st24[:, 16:20], st24[:, 8:12],
                                           st24[:, 12:16]))
                ve.inc(r.vector.drain())
                for j in range(4):
                    u = 4 * p + j
                    ub = u % 2
                    xt = xt2[:, j * D:(j + 1) * D]
                    xh = xh2[:, ub * D:(ub + 1) * D]
                    if u >= 2:
                        r.wait(r.vector, tn, tn_xh[u - 2])
                    ve_ap[u] = ve.inc(r.vector.tensor_scalar(
                        xh[:, :], xt[:, :], mva[:, 2 * j:2 * j + 1],
                        st24[:, 16 + j:17 + j], ALU.subtract, ALU.mult))
                    r.wait(r.tensor, ve, ve_ap[u])
                    if u >= 1:
                        r.wait(r.tensor, ve, ve_xh[u - 1])
                    for c in range(8):
                        tn.inc(r.tensor.transpose(
                            tpp[:, c * 128:(c + 1) * 128],
                            xh[:, c * 128:(c + 1) * 128], ident_bf[:, :]))
                    tn_xh[u] = tn.v
                    r.wait(r.vector, tn, tn_xh[u])
                    ve_xh[u] = ve.inc(r.vector.tensor_copy(
                        xhat.rearrange("p (c t) -> p c t", c=8)
                        [:, :, u * 128:(u + 1) * 128],
                        tpp.rearrange("p (c t) -> p c t", c=8)))
                    r.wait(r.tensor, ve, ve_xh[u])
                    if u >= 2:
                        r.wait(r.tensor, ve, ve_vv[u - 2])
                    vpb = ub * 1024
                    for half in range(2):
                        for mc in range(8):
                            tn.inc(r.tensor.matmul(
                                vps[:, vpb + half * 512:vpb + (half + 1) * 512],
                                xhat[:, mc * S + u * 128:mc * S + (u + 1) * 128],
                                VWb[:, mc * 1024 + half * 512:
                                    mc * 1024 + (half + 1) * 512],
                                start=(mc == 0), stop=False))
                        tn.inc(r.tensor.matmul(
                            vps[:, vpb + half * 512:vpb + (half + 1) * 512],
                            ones1b[0:1, :],
                            vb_row[0:1, half * 512:(half + 1) * 512],
                            start=False, stop=True))
                    r.wait(r.vector, tn)
                    ve_vv[u] = ve.inc(r.vector.tensor_copy(
                        V_sb.rearrange("p (kt h e) -> p kt h e", kt=NT, h=NH)
                        [:, u, :, 0:64],
                        vps[:, vpb:vpb + 1024].rearrange("p (h d) -> p h d",
                                                         h=NH)))
                for W, bias, dstb in ((QWb, qb_t, Q_sb), (KWb, kb_t, K_sb)):
                    for oc in range(8):
                        off = (oc % 2) * 512
                        if first_qk and W is QWb and oc < 2 and p == 0:
                            pass
                        else:
                            r.wait(r.tensor, sc, sc_qk)
                        for mc in range(8):
                            tn.inc(r.tensor.matmul(
                                qkps[:, off:off + 512],
                                W[:, mc * 1024 + oc * 128:
                                  mc * 1024 + (oc + 1) * 128],
                                xhat[:, mc * S + p * 512:mc * S + (p + 1) * 512],
                                start=(mc == 0), stop=(mc == 7)))
                        r.wait(r.scalar, tn)
                        sc_qk = sc.inc(r.scalar.activation(
                            dstb[:, oc * S + p * 512:oc * S + (p + 1) * 512],
                            qkps[:, off:off + 512], AF.Identity,
                            bias=bias[:, oc:oc + 1]))
                first_qk = False
            r.emit()
            barrier()

        es_a.close()

        DBG.update({f"xhat{s}": xhat, f"Q{s}": Q_sb, f"K{s}": K_sb,
                    f"V{s}": V_sb, f"st24{s}": st24})
        x2 = es_s.enter_context(
            nc.sbuf_tensor(f"x2_{s}", [128, NT * D], BF16, side="right"))
        es_pw = contextlib.ExitStack()
        if s == 0:
            pstg = es_pw.enter_context(
                nc.sbuf_tensor("pw_stg", [128, 2 * 2048], F32, side="right"))
            pasum = es_pw.enter_context(
                nc.sbuf_tensor("pw_as", [128, 36], F32, side="right"))
            pw_flat = [t_w.rearrange("a b -> (a b)").rearrange(
                "(ci p a) -> ci p a", ci=16, p=128)
                for t_w in (t_w1, t_w2)]
            pw_abs = [0] * 32

            def stats_load(n):
                if n >= 32:
                    return
                par = n % 2
                if n >= 2:
                    r.wait(r.sync, ve, pw_abs[n - 2])
                dlx[2 + par].inc(r.sync.dma_start(
                    out=pstg[:, par * 2048:(par + 1) * 2048],
                    in_=pw_flat[n // 16][n % 16]), 16)

            def stats_step(n):
                par = n % 2
                r.wait(r.vector, dlx[2 + par])
                pw_abs[n] = ve.inc(r.vector.reduce_sum(
                    pasum[:, n:n + 1], pstg[:, par * 2048:(par + 1) * 2048],
                    axis=AX.X, apply_absolute_value=True))
                stats_load(n + 2)
        es_b = contextlib.ExitStack()
        OWb = es_b.enter_context(nc.sbuf_tensor(f"b_ow{s}", [128, 8192], BF16))
        Cbias = es_b.enter_context(nc.sbuf_tensor(f"b_cb{s}", [128, 8], F32))
        qsq = es_b.enter_context(nc.sbuf_tensor(f"b_sq{s}", [128, 512], BF16))
        smf = es_b.enter_context(nc.sbuf_tensor(f"b_sm{s}", [128, 16], F32))
        ptil = es_b.enter_context(nc.sbuf_tensor(f"b_pt{s}", [128, 2048], BF16))
        atok = es_b.enter_context(nc.sbuf_tensor(f"b_at{s}", [128, 4 * 1024], F32))
        attT = es_b.enter_context(nc.sbuf_tensor(f"b_tt{s}", [128, 1024], BF16))
        zrec = es_b.enter_context(nc.sbuf_tensor(f"b_zr{s}", [128, 8], F32))
        xre = es_b.enter_context(nc.sbuf_tensor(f"b_xr{s}", [128, 2 * D], F32))
        if True:
            with nc.psum_tensor(f"b_ut{s}", [128, 512], F32) as util:
                dlx[0].inc(r.sync.dma_start(out=OWb[:, :], in_=t_wq4[3]), 16)
                r.wait(r.vector, dlx[0])
                ve.inc(r.vector.engine_nop())

                ve_red = 0
                for ti, Xb in ((0, Q_sb), (1, K_sb)):
                    for p in range(NP):
                        for hp in range(8):
                            if not (ti == 0 and p == 0 and hp == 0):
                                r.wait(r.vector, tn, tn.v)
                            ve.inc(r.vector.tensor_mul(
                                qsq[:, :],
                                Xb[:, hp * S + p * 512:hp * S + (p + 1) * 512],
                                Xb[:, hp * S + p * 512:hp * S + (p + 1) * 512]))
                            r.wait(r.tensor, ve)
                            if hp == 0 and not (ti == 0 and p == 0):
                                r.wait(r.tensor, ve, ve_red)
                            tn.inc(r.tensor.matmul(
                                util[0:8, :], ones8s[:, hp * 8:(hp + 1) * 8],
                                qsq[:, :], start=(hp == 0), stop=(hp == 7)))
                        r.wait(r.vector, tn)
                        ve_red = ve.inc(r.vector.reduce_max(
                            smf[0:8, 8 + p:9 + p], util[0:8, :], axis=AX.X))
                    ve.inc(r.vector.drain())
                    ve_red = ve.inc(r.vector.reduce_max(
                        smf[0:8, ti:ti + 1], smf[0:8, 8:8 + NP], axis=AX.X))
                    ve.inc(r.vector.drain())
                ve.inc(r.vector.tensor_mul(smf[0:8, 2:3], smf[0:8, 0:1],
                                           smf[0:8, 1:2]))
                r.wait(r.scalar, ve)
                sc.inc(r.scalar.activation(smf[0:8, 3:4], smf[0:8, 2:3],
                                           AF.Sqrt))
                r.wait(r.vector, sc)
                ve.inc(r.vector.tensor_single_scalar(smf[0:8, 4:5],
                                                     smf[0:8, 3:4],
                                                     -1.02 * 0.125, ALU.mult))
                r.wait(r.tensor, ve)
                tn.inc(r.tensor.transpose(util[0:1, 0:8], smf[0:8, 4:5],
                                          ident_f32[0:8, 0:8]))
                r.wait(r.vector, tn)
                ve.inc(r.vector.tensor_copy(row1[:, 0:8], util[0:1, 0:8]))
                r.wait(r.tensor, ve)
                tn.inc(r.tensor.matmul(util[:, 0:8], ones1f[0:1, :],
                                       row1[0:1, 0:8], start=True, stop=True))
                r.wait(r.vector, tn)
                ve.inc(r.vector.tensor_copy(Cbias[:, :], util[:, 0:8]))
                r.emit()
                barrier()

            DBG.update({f"Cb{s}": Cbias, f"smf{s}": smf, f"pt{s}": ptil,
                        f"zr{s}": zrec, f"at{s}": atok})
            Lps = es_b.enter_context(
                nc.psum_tensor(f"b_lp{s}", [128, 2048], F32))
            AVps2 = es_b.enter_context(
                nc.psum_tensor(f"b_av{s}", [128, 2048], F32))
            sc_exp = [0, 0]
            ve_pn = 0
            ve_pn2 = [0, 0]
            ve_aT2 = 0
            ve_x2 = 0
            tn_tT = 0
            dl_x = [0] * NT
            for u in (0, 1):
                dl_x[u] = dlb[u % 2].inc(r.sync.dma_start(
                    out=xre[:, (u % 2) * D:(u % 2 + 1) * D],
                    in_=t_x[s, u * 128:(u + 1) * 128, :]), 16)
            if s == 0:
                stats_load(0)
                stats_load(1)
            for p in range(NP):
                kcmax = 4 * (p + 1)
                for hp in range(8):
                    tn_log = [0, 0]

                    def emit_logits(kc, hp=hp, p=p):
                        lb = kc % 2
                        Lp = Lps[:, lb * 1024:(lb + 1) * 1024]
                        r.wait(r.tensor, sc, sc_exp[lb])
                        if p > 0 and hp == 0 and kc < 2:
                            r.wait(r.tensor, ve, ve_x2)
                        tn.inc(r.tensor.matmul(
                            Lp[:, 0:512],
                            K_sb[0:64,
                                 hp * S + kc * 128:hp * S + (kc + 1) * 128],
                            Q_sb[0:64,
                                 hp * S + p * 512:hp * S + (p + 1) * 512],
                            start=True, stop=True, tile_position=(0, 0)))
                        tn.inc(r.tensor.matmul(
                            Lp[:, 512:1024],
                            K_sb[64:128,
                                 hp * S + kc * 128:hp * S + (kc + 1) * 128],
                            Q_sb[64:128,
                                 hp * S + p * 512:hp * S + (p + 1) * 512],
                            start=True, stop=True, tile_position=(64, 0)))
                        if kc >= 4 * p:
                            ob = (kc - 4 * p) * 128
                            for hh in range(2):
                                tn.inc(r.tensor.matmul(
                                    Lp[:, hh * 512 + ob:hh * 512 + ob + 128],
                                    trimask[:, :], ident_bf[:, :],
                                    start=False, stop=True,
                                    skip_group_check=True))
                        tn_log[lb] = tn.v

                    emit_logits(0)
                    for kc in range(kcmax):
                        lb = kc % 2
                        if kc + 1 < kcmax:
                            emit_logits(kc + 1)
                        Lp = Lps[:, lb * 1024:(lb + 1) * 1024]
                        pb = ptil[:, lb * 1024:(lb + 1) * 1024]
                        o = (kc - 4 * p) * 128
                        r.wait(r.scalar, tn, tn_log[lb])
                        if kc < 4 * p:
                            sc_exp[lb] = sc.inc(r.scalar.activation(
                                pb, Lp, AF.Exp, scale=0.125,
                                bias=Cbias[:, hp:hp + 1]))
                        else:
                            if o > 0:
                                r.wait(r.vector, sc, sc_exp[lb])
                                r.wait(r.vector, tn, tn_log[lb])
                                ve.inc(r.vector.memset(
                                    pb.rearrange("p (h q) -> p h q", h=2)
                                    [:, :, 0:o], 0.0))
                                r.wait(r.scalar, ve)
                            sc_exp[lb] = sc.inc(r.scalar.activation(
                                pb.rearrange("p (h q) -> p h q", h=2)
                                [:, :, o:512],
                                Lp.rearrange("p (h q) -> p h q", h=2)
                                [:, :, o:512],
                                AF.Exp, scale=0.125, bias=Cbias[:, hp:hp + 1]))
                        r.wait(r.tensor, sc, sc_exp[lb])
                        if kc == 0:
                            r.wait(r.tensor, ve, ve_pn2[hp % 2])
                            if hp < 2 and p > 0:
                                r.wait(r.tensor, ve, ve_aT2)
                        ab = (hp % 2) * 1024
                        for hh in range(2):
                            for qs in range(4):
                                tn.inc(r.tensor.matmul(
                                    AVps2[:, ab + hh * 512 + qs * 65:
                                          ab + hh * 512 + qs * 65 + 65],
                                    pb[:, hh * 512 + qs * 128:
                                       hh * 512 + (qs + 1) * 128],
                                    V_sb[:, (kc * NH + 2 * hp + hh) * 65:
                                         (kc * NH + 2 * hp + hh) * 65 + 65],
                                    start=(kc == 0 and qs == 0),
                                    stop=(kc == kcmax - 1),
                                    skip_group_check=True))
                    tn_av = tn.v
                    r.wait(r.vector, tn, tn_av)
                    if p > 0 and hp == 0:
                        r.wait(r.vector, tn, tn_tT)
                    ve.inc(r.vector.reciprocal(
                        zrec.rearrange("p (hh q o) -> p hh q o", hh=2, o=1),
                        bass.AP(AVps2[:, :].tensor, ab + 64,
                                [[2048, 128], [512, 2], [65, 4], [1, 1]])))
                    ve.inc(r.vector.drain())
                    for hh in range(2):
                        for qs in range(4):
                            ve_pn = ve.inc(r.vector.tensor_single_scalar(
                                atok[:, qs * 1024 + (2 * hp + hh) * 64:
                                     qs * 1024 + (2 * hp + hh + 1) * 64],
                                AVps2[:, ab + hh * 512 + qs * 65:
                                      ab + hh * 512 + qs * 65 + 64],
                                zrec[:, hh * 4 + qs:hh * 4 + qs + 1], ALU.mult))
                    ve_pn2[hp % 2] = ve_pn
                    if s == 0:
                        stats_step(p * 8 + hp)
                for v in range(4):
                    u = 4 * p + v
                    r.wait(r.tensor, ve, ve_pn)
                    r.wait(r.tensor, sc, sc.v)
                    if v == 0:
                        r.wait(r.tensor, ve, ve_x2)
                    for c in range(8):
                        tn.inc(r.tensor.transpose(
                            AVps2[:, c * 128:(c + 1) * 128],
                            atok[:, v * 1024 + c * 128:v * 1024 + (c + 1) * 128],
                            ident_f32[:, :]))
                    tn_tT = tn.v
                    r.wait(r.vector, tn, tn_tT)
                    ve_aT = ve.inc(r.vector.tensor_copy(attT[:, :],
                                                        AVps2[:, 0:1024]))
                    ve_aT2 = ve_aT
                    r.wait(r.tensor, ve, ve_aT)
                    for half in range(2):
                        hs = half * 512
                        for c in range(8):
                            tn.inc(r.tensor.matmul(
                                Lps[:, hs:hs + 512],
                                attT[:, c * 128:(c + 1) * 128],
                                OWb[:, c * 1024 + hs:c * 1024 + hs + 512],
                                start=(c == 0), stop=False))
                        tn.inc(r.tensor.matmul(
                            Lps[:, hs:hs + 512], ones1b[0:1, :],
                            ob_row[0:1, hs:hs + 512], start=False, stop=True))
                    tn_o = tn.v
                    r.wait(r.vector, tn, tn_o)
                    r.wait(r.vector, dlb[u % 2], dl_x[u])
                    ve_x2 = ve.inc(r.vector.tensor_add(
                        x2[:, u * D:(u + 1) * D],
                        xre[:, (u % 2) * D:(u % 2 + 1) * D], Lps[:, 0:1024]))
                    nu = u + 2
                    if nu < NT:
                        r.wait(r.sync, ve, ve_x2)
                        dl_x[nu] = dlb[nu % 2].inc(r.sync.dma_start(
                            out=xre[:, (nu % 2) * D:(nu % 2 + 1) * D],
                            in_=t_x[s, nu * 128:(nu + 1) * 128, :]), 16)
            if s == 0:
                for n in range(NP * 8, 32):
                    stats_step(n)
            r.emit()
            barrier()

        es_b.close()
        es_qkv.close()

        if s == 0:
            with nc.sbuf_tensor("pw_s1", [1, 8], F32) as s1s, \
                 nc.psum_tensor("pw_ps", [128, 512], F32) as ups:
                for wi, (sw_bc, mw_bc) in enumerate(
                        ((sw1_bc, mw1_bc), (sw2_bc, mw2_bc))):
                    nel = HID * D
                    ve.inc(r.vector.reduce_sum(
                        pasum[:, 32 + wi:33 + wi],
                        pasum[:, wi * 16:wi * 16 + 16], axis=AX.X))
                    ve.inc(r.vector.drain())
                    r.wait(r.tensor, ve, ve.v)
                    tn.inc(r.tensor.matmul(
                        ups[0:1, wi:wi + 1], pasum[:, 32 + wi:33 + wi],
                        onescol[:, :], start=True, stop=True))
                    r.wait(r.vector, tn, tn.v)
                    ve.inc(r.vector.tensor_scalar(
                        s1s[:, 4 * wi:4 * wi + 1], ups[0:1, wi:wi + 1],
                        1.0 / nel, 1e-5, ALU.mult, ALU.max))
                    ve.inc(r.vector.drain())
                    ve.inc(r.vector.reciprocal(s1s[:, 4 * wi + 1:4 * wi + 2],
                                               s1s[:, 4 * wi:4 * wi + 1]))
                    ve.inc(r.vector.drain())
                    r.wait(r.tensor, ve, ve.v)
                    tn.inc(r.tensor.matmul(
                        ups[:, 0:1], ones1f[0:1, :],
                        s1s[0:1, 4 * wi:4 * wi + 1], start=True, stop=True))
                    tn.inc(r.tensor.matmul(
                        ups[:, 1:2], ones1f[0:1, :],
                        s1s[0:1, 4 * wi + 1:4 * wi + 2],
                        start=True, stop=True))
                    r.wait(r.vector, tn, tn.v)
                    ve.inc(r.vector.tensor_copy(mw_bc[:, :], ups[:, 0:1]))
                    ve.inc(r.vector.tensor_copy(sw_bc[:, :], ups[:, 1:2]))
                    ve.inc(r.vector.drain())
                r.emit()
                barrier()
            wprep_quant(t_w1, sw1_bc, mw1_bc, "w1")
            wprep_quant(t_w2, sw2_bc, mw2_bc, "w2")
            r.emit()
        es_pw.close()

        es_c = contextlib.ExitStack()
        xq2 = es_c.enter_context(nc.sbuf_tensor(f"c_xq{s}", [128, 8 * SP2], BF16))
        g1col = es_c.enter_context(nc.sbuf_tensor(f"c_gc{s}", [128, NT], F32))
        cst = es_c.enter_context(nc.sbuf_tensor(f"c_st{s}", [128, 64], F32))
        bnsc = es_c.enter_context(nc.sbuf_tensor(f"c_bn{s}", [128, 48], F32))
        mvc = es_c.enter_context(nc.sbuf_tensor(f"c_mv{s}", [128, 8], F32))
        if True:
            es_cp = contextlib.ExitStack()
            tppC = es_cp.enter_context(
                nc.psum_tensor(f"c_tp{s}", [128, 1024], BF16))
            cxt = es_cp.enter_context(
                nc.sbuf_tensor(f"c_xv{s}", [128, 4 * D], F32, side="right"))
            cxm = es_cp.enter_context(
                nc.sbuf_tensor(f"c_xm{s}", [128, 4 * D], BF16, side="right"))
            cpb = es_cp.enter_context(
                nc.sbuf_tensor(f"c_pb{s}", [128, 192], F32, side="right"))
            mv16 = es_cp.enter_context(
                nc.sbuf_tensor(f"c_mw{s}", [128, 32], F32, side="right"))
            cs2 = es_cp.enter_context(
                nc.sbuf_tensor(f"c_cq{s}", [128, 128], F32, side="right"))
            ve.inc(r.vector.memset(
                xq2.rearrange("p (c t) -> p c t", c=8)[:, :, 0:1], 0.0))
            ve.inc(r.vector.memset(
                xq2.rearrange("p (c t) -> p c t", c=8)[:, :, S + 1:S + 2], 0.0))
            tn_pq = [0] * NT
            ve_pq = [0] * NT
            sc_r2 = [0] * NT
            sc_r1 = [0] * NT
            for u in range(NT):
                ve.inc(r.vector.bn_stats(cpb[:, u * 12:u * 12 + 6],
                                         x2[:, u * D:u * D + 512]))
                ve.inc(r.vector.bn_stats(cpb[:, u * 12 + 6:u * 12 + 12],
                                         x2[:, u * D + 512:(u + 1) * D]))
            ve.inc(r.vector.drain())
            for u in range(NT):
                ve.inc(r.vector.bn_aggr(mv16[:, u * 2:(u + 1) * 2],
                                        cpb[:, u * 12:u * 12 + 12]))
            ve.inc(r.vector.drain())
            var16 = mv16.rearrange("p (j two) -> p two j", two=2)[:, 1:2, :]
            ve.inc(r.vector.tensor_scalar_add(
                cs2.rearrange("p (a b) -> p a b", a=1)[:, :, 0:16], var16,
                EPS))
            r.wait(r.scalar, ve)
            sc.inc(r.scalar.activation(cs2[:, 16:32], cs2[:, 0:16], AF.Sqrt))
            r.wait(r.vector, sc)
            ve.inc(r.vector.reciprocal(cs2[:, 32:48], cs2[:, 16:32]))
            ve.inc(r.vector.drain())
            ve.inc(r.vector.tensor_mul(cs2[:, 48:64], cs2[:, 32:48],
                                       cs2[:, 32:48]))
            ve.inc(r.vector.drain())
            ve.inc(r.vector.tensor_mul(cs2[:, 48:64], cs2[:, 48:64],
                                       cs2[:, 0:16]))
            ve.inc(r.vector.drain())
            ve.inc(r.vector.tensor_scalar(cs2[:, 48:64], cs2[:, 48:64],
                                          -0.5, 1.5, ALU.mult, ALU.add))
            ve.inc(r.vector.drain())
            ve.inc(r.vector.tensor_mul(cs2[:, 64:80], cs2[:, 32:48],
                                       cs2[:, 48:64]))
            ve.inc(r.vector.drain())
            for tq in range(NT // 4):
                q0 = 4 * tq
                if tq >= 1:
                    r.wait(r.vector, tn, tn_pq[q0 - 1])
                    r.wait(r.vector, sc, sc_r1[q0 - 1])
                for j in range(4):
                    u = q0 + j
                    ve.inc(r.vector.tensor_scalar(
                        cxt[:, j * D:(j + 1) * D], x2[:, u * D:(u + 1) * D],
                        mv16[:, 2 * u:2 * u + 1], cs2[:, 64 + u:65 + u],
                        ALU.subtract, ALU.mult))
                ve.inc(r.vector.drain())
                for j in range(4):
                    u = q0 + j
                    ve.inc(r.vector.reduce_max(cs2[:, 80 + u:81 + u],
                                               cxt[:, j * D:(j + 1) * D],
                                               axis=AX.X,
                                               apply_absolute_value=True))
                ve.inc(r.vector.drain())
                ve.inc(r.vector.tensor_scalar_max(
                    cs2[:, 96 + q0:100 + q0], cs2[:, 80 + q0:84 + q0], 1e-5))
                ve.inc(r.vector.drain())
                ve.inc(r.vector.reciprocal(cs2[:, 112 + q0:116 + q0],
                                           cs2[:, 96 + q0:100 + q0]))
                ve.inc(r.vector.drain())
                ve.inc(r.vector.tensor_single_scalar(
                    cs2[:, 112 + q0:116 + q0], cs2[:, 112 + q0:116 + q0],
                    127.0, ALU.mult))
                ve.inc(r.vector.tensor_scalar(
                    g1col[:, q0:q0 + 4], cs2[:, 96 + q0:100 + q0],
                    mw1_bc[:, 0:1], 1.0 / 127.0, ALU.mult, ALU.mult))
                ve_q = ve.inc(r.vector.drain())
                r.wait(r.scalar, ve, ve_q)
                if tq >= 1:
                    r.wait(r.scalar, tn, tn_pq[q0 - 1])
                for pj in range(2):
                    for j in (2 * pj, 2 * pj + 1):
                        u = q0 + j
                        sc_r1[u] = sc.inc(r.scalar.activation(
                            cxt[:, j * D:(j + 1) * D],
                            cxt[:, j * D:(j + 1) * D], AF.Identity,
                            scale=cs2[:, 112 + u:113 + u],
                            bias=magic_bc[:, 0:1]))
                    sc.inc(r.scalar.drain())
                    for j in (2 * pj, 2 * pj + 1):
                        u = q0 + j
                        sc_r2[u] = sc.inc(r.scalar.activation(
                            cxm[:, j * D:(j + 1) * D],
                            cxt[:, j * D:(j + 1) * D], AF.Identity,
                            bias=nmagic_bc[:, 0:1]))
                for j in range(4):
                    u = q0 + j
                    r.wait(r.tensor, sc, sc_r2[u])
                    if not (tq == 0 and j == 0):
                        r.wait(r.tensor, ve, ve_pq[u - 1])
                    for c in range(8):
                        tn.inc(r.tensor.transpose(
                            tppC[:, c * 128:(c + 1) * 128],
                            cxm[:, j * D + c * 128:j * D + (c + 1) * 128],
                            ident_bf[:, :]))
                    tn_pq[u] = tn.v
                    r.wait(r.vector, tn, tn_pq[u])
                    ve_pq[u] = ve.inc(r.vector.tensor_copy(
                        xq2.rearrange("p (c t) -> p c t", c=8)
                        [:, :, 1 + u * 128:1 + (u + 1) * 128],
                        tppC.rearrange("p (c t) -> p c t", c=8)))
            r.emit()
            barrier()
            es_cp.close()

            hps = es_c.enter_context(nc.psum_tensor(f"c_hp{s}",
                                                    [128, 2048], F32))
            ops = es_c.enter_context(nc.psum_tensor(f"c_op{s}",
                                                    [128, 2048], F32))
            Hbq = es_c.enter_context(
                nc.sbuf_tensor(f"c_H{s}", [128, 32 * 514], BF16))
            Hb2 = es_c.enter_context(
                nc.sbuf_tensor(f"c_H2{s}", [128, 32 * 512], BF16))
            G1sb = es_c.enter_context(
                nc.sbuf_tensor(f"c_g1b{s}", [128, 514], F32))
            w1b = es_c.enter_context(
                nc.sbuf_tensor(f"c_w1b{s}", [128, 3 * 2048], BF16))
            w2b = es_c.enter_context(
                nc.sbuf_tensor(f"c_w2b{s}", [128, 2 * 2048], BF16))
            dgb = es_c.enter_context(
                nc.sbuf_tensor(f"c_dg{s}", [128, 3 * 384], BF16))
            ct1 = es_c.enter_context(
                nc.sbuf_tensor(f"c_t1{s}", [128, 2 * 512], F32))
            ct2 = es_c.enter_context(
                nc.sbuf_tensor(f"c_t2{s}", [128, 512], F32))
            mxt = es_c.enter_context(
                nc.sbuf_tensor(f"c_mx{s}", [128, 2 * 512], BF16))
            mxn = es_c.enter_context(
                nc.sbuf_tensor(f"c_mn{s}", [128, 2 * 512], BF16))
            hqg = es_c.enter_context(
                nc.sbuf_tensor(f"c_hq{s}", [128, 32 * 512], BF16))
            S2sb = es_c.enter_context(
                nc.sbuf_tensor(f"c_ss{s}", [128, 512], F32))
            g2buf = es_c.enter_context(
                nc.sbuf_tensor(f"c_g2{s}", [128, 8], F32))
            obuf = es_c.enter_context(
                nc.sbuf_tensor(f"c_obf{s}", [128, D], F32, side="right"))
            ve_h = [0] * 32
            sc_sil = [0] * 32
            ve_hq = [0] * 32
            ve_hqm = [0] * 32
            tn_cv = [0] * 32
            ve_dgb = [0] * 32
            slot_free = [0, 0, 0]
            ve_mx = 0
            ve_ev = 0
            ve_g1 = 0
            gp_ob = 0
            dl_w1 = [0] * 16
            st_ch = {}
            w2_done = {}
            g2s = {}
            w2b_cons = [0, 0]

            dl_w2 = [0, 0]

            def emit_w2_load(glob):
                wb2 = glob % 2
                g2c = 2 * glob
                r.wait(r.sync, tn, w2b_cons[wb2])
                dl_w2[wb2] = dw2[wb2].inc(r.sync.dma_start(
                    out=w2b[:, wb2 * 2048:(wb2 + 1) * 2048].rearrange(
                        "p (l m) -> p l m", l=2),
                    in_=t_w2t[g2c:g2c + 2].rearrange("l p m -> p l m")), 16)

            def emit_w2_sub(t, half, glob):
                vp = 0 if half == 0 else 2
                wb2 = glob % 2
                g2c = 2 * glob
                if glob == 0:
                    r.wait(r.tensor, ve, ve_ev)
                r.wait(r.tensor, sc, ve_hq[2 * glob + 1])
                r.wait(r.tensor, dw2[wb2], dl_w2[wb2])
                for l in range(2):
                    hl2 = g2c + l
                    mv = w2b[:, (wb2 * 2 + l) * 1024:
                             (wb2 * 2 + l + 1) * 1024]
                    first = (glob == 0 and l == 0)
                    last = (glob == 15 and l == 1)
                    for v in (vp, vp + 1):
                        vv = v - vp
                        tn.inc(r.tensor.matmul(
                            ops[:, (2 * vv) * 512:(2 * vv + 1) * 512],
                            hqg[:, hl2 * 512 + v * 128:
                                hl2 * 512 + (v + 1) * 128],
                            mv[:, 0:512], start=first, stop=last,
                            skip_group_check=True))
                        tn.inc(r.tensor.matmul(
                            ops[:, (2 * vv + 1) * 512:(2 * vv + 2) * 512],
                            hqg[:, hl2 * 512 + v * 128:
                                hl2 * 512 + (v + 1) * 128],
                            mv[:, 512:1024], start=first, stop=last,
                            skip_group_check=True))
                w2b_cons[wb2] = tn.v

            def emit_final_pair(t, vp):
                nonlocal ve_ev, gp_ob
                tn_w2 = tn.v
                for vv in range(2):
                    v = vp + vv
                    u = 4 * t + v
                    r.wait(r.vector, tn, tn_w2)
                    r.wait(r.vector, dst, gp_ob)
                    ve.inc(r.vector.scalar_tensor_tensor(
                        obuf[:, 0:512], ops[:, (2 * vv) * 512:
                                            (2 * vv + 1) * 512],
                        g2s[t][:, v:v + 1],
                        x2[:, u * D:u * D + 512], ALU.mult, ALU.add))
                    ve_evl = ve.inc(r.vector.scalar_tensor_tensor(
                        obuf[:, 512:1024], ops[:, (2 * vv + 1) * 512:
                                               (2 * vv + 2) * 512],
                        g2s[t][:, v:v + 1],
                        x2[:, u * D + 512:(u + 1) * D], ALU.mult, ALU.add))
                    ve.inc(r.vector.drain())
                    r.wait(r.gpsimd, ve, ve_evl)
                    gp_ob = dst.inc(r.gpsimd.dma_start(
                        out=t_out[s, u * 128:(u + 1) * 128, :],
                        in_=obuf[:, :]), 16)
                    ve_ev = ve_evl

            def emit_conv_silu(t, hc):
                par = hc % 2
                dbase = (hc % 3) * 384
                r.wait(r.tensor, ve, ve_h[hc])
                r.wait(r.tensor, ve, ve_dgb[hc])
                for j in range(3):
                    tn.inc(r.tensor.matmul(
                        hps[:, par * 1024:par * 1024 + 512],
                        dgb[:, dbase + j * 128:dbase + (j + 1) * 128],
                        Hbq[:, hc * 514 + j:hc * 514 + j + 512],
                        start=(j == 0), stop=(j == 2)))
                tn_cv[hc] = tn.v
                r.wait(r.scalar, tn, tn_cv[hc])
                r.wait(r.scalar, ve, ve_hqm[hc])
                sc_sil[hc] = sc.inc(r.scalar.activation(
                    Hb2[:, hc * 512:(hc + 1) * 512],
                    hps[:, par * 1024:par * 1024 + 512],
                    AF.Silu, bias=cb_t[:, hc:hc + 1]))

            def emit_mx(gq):
                r.wait(r.vector, sc, sc_sil[4 * gq + 3])
                for j in range(4):
                    c = 4 * gq + j
                    ja = (j % 2) * 512
                    hsl = Hb2[:, c * 512:(c + 1) * 512]
                    if j == 2:
                        ve.inc(r.vector.drain())
                    if gq == 0 and j < 2:
                        ve.inc(r.vector.tensor_copy(mxt[:, ja:ja + 512], hsl))
                        ve.inc(r.vector.tensor_copy(mxn[:, ja:ja + 512], hsl))
                    else:
                        ve.inc(r.vector.tensor_tensor(
                            mxt[:, ja:ja + 512], mxt[:, ja:ja + 512],
                            hsl, ALU.max))
                        ve.inc(r.vector.tensor_tensor(
                            mxn[:, ja:ja + 512], mxn[:, ja:ja + 512],
                            hsl, ALU.min))
                ve.inc(r.vector.drain())

            def emit_hq_batch(t, c0):
                r.wait(r.vector, sc, sc_sil[c0 + 3])
                r.wait(r.vector, sc, ve_hq[(c0 + 31) % 32])
                for pair in range(2):
                    if pair == 1:
                        r.wait(r.vector, sc, ve_hq[c0 + 1])
                    mv_h = 0
                    for k in range(2):
                        c = c0 + 2 * pair + k
                        mv_h = ve.inc(r.vector.tensor_mul(
                            ct1[:, k * 512:(k + 1) * 512],
                            Hb2[:, c * 512:(c + 1) * 512], S2sb[:, :]))
                        ve_hqm[c] = mv_h
                    r.wait(r.scalar, ve, mv_h)
                    for k in range(2):
                        c = c0 + 2 * pair + k
                        sc.inc(r.scalar.activation(
                            ct1[:, k * 512:(k + 1) * 512],
                            ct1[:, k * 512:(k + 1) * 512],
                            AF.Identity, bias=magic_bc[:, 0:1]))
                    sc.inc(r.scalar.drain())
                    for k in range(2):
                        c = c0 + 2 * pair + k
                        ve_hq[c] = sc.inc(r.scalar.activation(
                            hqg[:, c * 512:(c + 1) * 512],
                            ct1[:, k * 512:(k + 1) * 512],
                            AF.Identity, bias=nmagic_bc[:, 0:1]))

            def emit_g1(t):
                nonlocal ve_g1
                if t > 0:
                    r.wait(r.tensor, ve, st_ch[t - 1])
                for b in range(4):
                    tn.inc(r.tensor.transpose(
                        hps[0:1, 1024 + b * 128:1024 + (b + 1) * 128],
                        g1col[:, 4 * t + b:4 * t + b + 1], ident_f32[:, :]))
                if t + 1 < NCT:
                    tn.inc(r.tensor.transpose(
                        hps[0:1, 1536:1664],
                        g1col[:, 4 * t + 4:4 * t + 5], ident_f32[:, :]))
                tn_g = tn.v
                r.wait(r.vector, tn, tn_g)
                if t == 0:
                    ve.inc(r.vector.memset(G1sb[0:1, 0:1], 0.0))
                else:
                    ve.inc(r.vector.tensor_copy(G1sb[0:1, 0:1],
                                                G1sb[0:1, 512:513]))
                ve.inc(r.vector.drain())
                ve.inc(r.vector.tensor_copy(G1sb[0:1, 1:513],
                                            hps[0:1, 1024:1536]))
                if t + 1 < NCT:
                    ve.inc(r.vector.tensor_copy(G1sb[0:1, 513:514],
                                                hps[0:1, 1536:1537]))
                else:
                    ve.inc(r.vector.memset(G1sb[0:1, 513:514], 0.0))
                ve_row = ve.inc(r.vector.drain())
                r.wait(r.tensor, ve, ve_row)
                tn.inc(r.tensor.matmul(hps[:, 1024:1536], ones1f[0:1, :],
                                       G1sb[0:1, 0:512], start=True,
                                       stop=True))
                tn.inc(r.tensor.matmul(hps[:, 1536:1538], ones1f[0:1, :],
                                       G1sb[0:1, 512:514], start=True,
                                       stop=True))
                r.wait(r.vector, tn, tn.v)
                ve.inc(r.vector.tensor_copy(G1sb[:, :], hps[:, 1024:1538]))
                ve_g1 = ve.inc(r.vector.drain())

            def emit_g1_w1_conv(t, w2t=None):
                nonlocal ve_mx

                def w1_load(k):
                    sl = k % 3
                    r.wait(r.sync, tn, slot_free[sl])
                    dl_w1[k] = dw1[sl].inc(r.sync.dma_start(
                        out=w1b[:, sl * 2048:(sl + 1) * 2048].rearrange(
                            "p (l f h) -> p l f h", l=2, f=8),
                        in_=t_w1t[2 * k:2 * k + 2].rearrange(
                            "l f p h -> p l f h")), 16)

                def dg_build(hc):
                    r.wait(r.vector, tn, tn_cv[(hc - 3) % 32])
                    for j in range(3):
                        ve_dgb[hc] = ve.inc(r.vector.tensor_scalar(
                            dgb[:, (hc % 3) * 384 + j * 128:
                                (hc % 3) * 384 + (j + 1) * 128],
                            ident_f32[:, :],
                            cw_t[:, j * 32 + hc:j * 32 + hc + 1], 0.0,
                            ALU.mult, ALU.add))

                def w1_chunk(hc, sl):
                    par = hc % 2
                    if t > 0 or hc >= 2:
                        r.wait(r.tensor, sc, sc_sil[(hc - 2) % 32])
                    if t > 0 and hc == 1:
                        r.wait(r.tensor, ve, ve_g1)
                    lbase = sl * 2048 + (hc % 2) * 1024
                    for fc in range(8):
                        wsl = w1b[:, lbase + fc * 128:lbase + (fc + 1) * 128]
                        tn.inc(r.tensor.matmul(
                            hps[:, par * 1024:par * 1024 + 257], wsl,
                            xq2[:, fc * SP2 + t * 512:
                                fc * SP2 + t * 512 + 257],
                            start=(fc == 0), stop=(fc == 7)))
                        tn.inc(r.tensor.matmul(
                            hps[:, par * 1024 + 512:par * 1024 + 769], wsl,
                            xq2[:, fc * SP2 + t * 512 + 257:
                                fc * SP2 + t * 512 + 514],
                            start=(fc == 0), stop=(fc == 7)))
                    tn_h = tn.v
                    r.wait(r.vector, tn, tn_h)
                    ve_h[hc] = ve.inc(r.vector.tensor_mul(
                        Hbq[:, hc * 514:hc * 514 + 514].rearrange(
                            "p (b c) -> p b c", b=2),
                        bass.AP(hps[:, :].tensor, par * 1024,
                                [[2048, 128], [512, 2], [1, 257]]),
                        G1sb.rearrange("p (b c) -> p b c", b=2)))

                dg_build(0)
                w1_load(0)
                w1_load(1)
                for g in range(16):
                    sl = g % 3
                    if g + 2 <= 15:
                        w1_load(g + 2)
                    if w2t is not None:
                        emit_w2_load(2 * (g % 8))
                        emit_w2_load(2 * (g % 8) + 1)
                    dg_build(2 * g + 1)
                    if g == 0:
                        dg_build(2)
                    r.wait(r.tensor, dw1[sl], dl_w1[g])
                    if g == 0:
                        r.wait(r.tensor, ve, ve_g1)
                    w1_chunk(2 * g, sl)
                    if g > 0:
                        emit_conv_silu(t, 2 * g - 1)
                        if g < 15:
                            dg_build(2 * g + 2)
                    if w2t is not None:
                        emit_w2_sub(w2t, g // 8, 2 * (g % 8))
                    w1_chunk(2 * g + 1, sl)
                    slot_free[sl] = tn.v
                    emit_conv_silu(t, 2 * g)
                    if w2t is not None:
                        emit_w2_sub(w2t, g // 8, 2 * (g % 8) + 1)
                        if g == 7:
                            emit_final_pair(w2t, 0)
                        if g == 15:
                            emit_final_pair(w2t, 2)
                            w2_done[w2t] = tn.v
                    if g % 2 == 0 and g >= 2:
                        emit_mx(g // 2 - 1)
                    if w2t is not None and g <= 6:
                        emit_hq_batch(w2t, 4 * (g + 1))
                emit_conv_silu(t, 31)
                emit_mx(6)
                emit_mx(7)
                ve.inc(r.vector.tensor_tensor(mxt[:, 0:512], mxt[:, 0:512],
                                              mxt[:, 512:1024], ALU.max))
                ve.inc(r.vector.tensor_tensor(mxn[:, 0:512], mxn[:, 0:512],
                                              mxn[:, 512:1024], ALU.min))
                ve.inc(r.vector.drain())
                ve.inc(r.vector.tensor_single_scalar(mxn[:, 0:512],
                                                     mxn[:, 0:512],
                                                     -1.0, ALU.mult))
                ve.inc(r.vector.drain())
                ve.inc(r.vector.tensor_tensor(
                    mxt[:, 0:512], mxt[:, 0:512], mxn[:, 0:512], ALU.max))
                ve.inc(r.vector.drain())
                ve_mx = ve.inc(r.vector.tensor_copy(ct2[:, 0:512],
                                                    mxt[:, 0:512]))
                ve.inc(r.vector.drain())

            def emit_chain(t):
                r.wait(r.tensor, ve, ve_mx)
                for b in range(4):
                    tn.inc(r.tensor.transpose(
                        hps[:, 1024 + b * 128:1024 + (b + 1) * 128],
                        ct2[:, b * 128:(b + 1) * 128], ident_f32[:, :]))
                tn_mx = tn.v
                r.wait(r.vector, tn, tn_mx)
                for b in range(4):
                    ve.inc(r.vector.reduce_max(
                        cst[:, 32 + b:33 + b],
                        hps[:, 1024 + b * 128:1024 + (b + 1) * 128],
                        axis=AX.X))
                ve.inc(r.vector.drain())
                ve.inc(r.vector.tensor_scalar_max(cst[:, 36:40], cst[:, 32:36],
                                                  1e-5))
                ve.inc(r.vector.drain())
                ve.inc(r.vector.reciprocal(cst[:, 40:44], cst[:, 36:40]))
                ve.inc(r.vector.drain())
                ve.inc(r.vector.tensor_single_scalar(cst[:, 44:48],
                                                     cst[:, 40:44],
                                                     127.0, ALU.mult))
                ve_g2 = ve.inc(r.vector.tensor_scalar(
                    g2buf[:, (t % 2) * 4:(t % 2) * 4 + 4], cst[:, 36:40],
                    mw2_bc[:, 0:1], 1.0 / 127.0, ALU.mult, ALU.mult))
                g2s[t] = g2buf[:, (t % 2) * 4:(t % 2) * 4 + 4]
                r.wait(r.tensor, ve, ve_g2)
                for b in range(4):
                    tn.inc(r.tensor.transpose(
                        hps[0:1, 1024 + b * 128:1024 + (b + 1) * 128],
                        cst[:, 44 + b:45 + b], ident_f32[:, :]))
                r.wait(r.vector, tn, tn.v)
                ve_s2 = ve.inc(r.vector.tensor_copy(S2sb[0:1, 0:512],
                                                    hps[0:1, 1024:1536]))
                r.wait(r.tensor, ve, ve_s2)
                tn_s2 = tn.inc(r.tensor.matmul(hps[:, 1024:1536],
                                               ones1f[0:1, :],
                                               S2sb[0:1, 0:512],
                                               start=True, stop=True))
                r.wait(r.vector, tn, tn_s2)
                ve.inc(r.vector.tensor_copy(S2sb[:, :], hps[:, 1024:1536]))
                ve.inc(r.vector.drain())
                st_ch[t] = ve.v

            emit_g1(0)
            for t in range(NCT):
                emit_g1_w1_conv(t, w2t=(t - 1 if t > 0 else None))
                emit_chain(t)
                if t + 1 < NCT:
                    emit_g1(t + 1)
                if t > 0:
                    r.wait(r.scalar, tn, w2_done[t - 1])
                emit_hq_batch(t, 0)
            tl = NCT - 1
            for g in range(16):
                emit_w2_load(2 * (g % 8))
                emit_w2_load(2 * (g % 8) + 1)
                emit_w2_sub(tl, g // 8, 2 * (g % 8))
                emit_w2_sub(tl, g // 8, 2 * (g % 8) + 1)
                if g == 7:
                    emit_final_pair(tl, 0)
                if g == 15:
                    emit_final_pair(tl, 2)
                    w2_done[tl] = tn.v
                if g <= 6:
                    emit_hq_batch(tl, 4 * (g + 1))
            r.wait(r.gpsimd, dst, dst.v)
            r.emit()
            barrier()
        es_c.close()
        es_s.close()
        r.emit()

    r.wait(r.gpsimd, dst, dst.v)
    gp.inc(r.gpsimd.engine_nop())
    r.emit()
    es.close()
    nc.finalize()
    return nc


_CACHE = {}


def _get_nc(nseq, S):
    key = (nseq, S)
    if key not in _CACHE:
        _CACHE[key] = build_program(nseq, S)
    return _CACHE[key]


def kernel(L0_attn_x, L0_attn_Q_w, L0_attn_Q_b, L0_attn_K_w, L0_attn_K_b,
           L0_attn_V_w, L0_attn_V_b, L0_attn_O_w, L0_attn_O_b,
           L0_ffn_w1, L0_ffn_conv_w, L0_ffn_conv_b, L0_ffn_w2):
    import ml_dtypes
    x = np.ascontiguousarray(np.asarray(L0_attn_x, dtype=np.float32))
    B, S, _ = x.shape
    ncores = 8
    nseq = B // ncores
    nc = _get_nc(nseq, S)

    tri = np.triu(np.ones((128, 128))).astype(ml_dtypes.bfloat16)
    o8 = np.zeros((8, 128, 8), dtype=ml_dtypes.bfloat16)
    for hp in range(8):
        o8[hp, :, hp] = 1.0
    asf = lambda a: np.ascontiguousarray(np.asarray(a, dtype=np.float32))
    common = {
        "qw": asf(L0_attn_Q_w), "kw": asf(L0_attn_K_w), "vw": asf(L0_attn_V_w),
        "ow": asf(L0_attn_O_w), "qbias": asf(L0_attn_Q_b),
        "kbias": asf(L0_attn_K_b), "vbias": asf(L0_attn_V_b),
        "obias": asf(L0_attn_O_b), "w1": asf(L0_ffn_w1),
        "cw": asf(L0_ffn_conv_w), "cb": asf(L0_ffn_conv_b),
        "w2": asf(L0_ffn_w2),
        "ident_bf": np.eye(128, dtype=ml_dtypes.bfloat16),
        "ident_f32": np.eye(128, dtype=np.float32),
        "trimask": tri,
        "ones8s": o8,
    }
    in_maps = []
    for i in range(ncores):
        m = dict(common)
        m["x"] = np.ascontiguousarray(x[i * nseq:(i + 1) * nseq])
        in_maps.append(m)
    res = run_bass_kernel_spmd(nc, in_maps, core_ids=list(range(ncores)))
    out = np.concatenate([np.asarray(res.results[i]["out"])
                          for i in range(ncores)], axis=0)
    return out.astype(np.float32)


# version 24
# speedup vs baseline: 1.0356x; 1.0356x over previous
import contextlib
import numpy as np

import concourse.bass as bass
import concourse.mybir as mybir
from concourse.bass_utils import run_bass_kernel_spmd

F32 = mybir.dt.float32
BF16 = mybir.dt.bfloat16
AF = mybir.ActivationFunctionType
ALU = mybir.AluOpType
AX = mybir.AxisListType

MAGIC = float(np.float32(3 * 2.0 ** 22))
D = 1024
NH = 16
DH = 64
HID = 4096
EPS = 1e-5

ENGS = ("sync", "tensor", "scalar", "vector", "gpsimd")
DBG = {}


class Ctr:
    def __init__(self, nc, es, name):
        self.sem = es.enter_context(nc.semaphore(name))
        self.v = 0

    def inc(self, h, k=1):
        h.then_inc(self.sem, k)
        self.v += k
        return self.v


class Handle:
    def __init__(self, entry):
        self.entry = entry

    def then_inc(self, sem, k):
        self.entry["incs"].append((sem, k))
        return self


class EngRec:
    def __init__(self, ops):
        self._ops = ops

    def __getattr__(self, name):
        ops = self._ops

        def call(*args, **kwargs):
            entry = {"name": name, "args": args, "kwargs": kwargs, "incs": []}
            ops.append(entry)
            return Handle(entry)

        return call


class Rec:
    def __init__(self, nc):
        self.nc = nc
        self._reset()

    def _reset(self):
        self.ops = {e: [] for e in ENGS}
        for e in ENGS:
            setattr(self, e, EngRec(self.ops[e]))

    def wait(self, engrec, ctr, v=None):
        if v is None:
            v = ctr.v
        assert v <= ctr.v, "wait on future value -> deadlock"
        if v > 0:
            engrec.wait_ge(ctr.sem, v)

    def emit(self):
        def replay(ops, eng):
            for en in ops:
                h = getattr(eng, en["name"])(*en["args"], **en["kwargs"])
                for sem, k in en["incs"]:
                    h.then_inc(sem, k)

        with self.nc.Block() as b:
            for e in ENGS:
                if self.ops[e]:
                    getattr(b, e)(
                        (lambda ops: (lambda eng: replay(ops, eng)))(self.ops[e]))
        self._reset()


def build_program(nseq, S):
    NP = S // 512
    NT = S // 128
    NCT = S // 512
    SP2 = S + 2

    nc = bass.Bass("TRN2", target_bir_lowering=False, debug=False)

    t_x = nc.dram_tensor("x", [nseq, S, D], F32, kind="ExternalInput").ap()
    t_qw = nc.dram_tensor("qw", [NH, D, DH], F32, kind="ExternalInput").ap()
    t_kw = nc.dram_tensor("kw", [NH, D, DH], F32, kind="ExternalInput").ap()
    t_vw = nc.dram_tensor("vw", [NH, D, DH], F32, kind="ExternalInput").ap()
    t_ow = nc.dram_tensor("ow", [NH, DH, D], F32, kind="ExternalInput").ap()
    t_qb = nc.dram_tensor("qbias", [NH, DH], F32, kind="ExternalInput").ap()
    t_kb = nc.dram_tensor("kbias", [NH, DH], F32, kind="ExternalInput").ap()
    t_vb = nc.dram_tensor("vbias", [NH, DH], F32, kind="ExternalInput").ap()
    t_ob = nc.dram_tensor("obias", [D], F32, kind="ExternalInput").ap()
    t_w1 = nc.dram_tensor("w1", [HID, D], F32, kind="ExternalInput").ap()
    t_cw = nc.dram_tensor("cw", [HID, 1, 3], F32, kind="ExternalInput").ap()
    t_cb = nc.dram_tensor("cb", [HID], F32, kind="ExternalInput").ap()
    t_w2 = nc.dram_tensor("w2", [D, HID], F32, kind="ExternalInput").ap()
    t_idb = nc.dram_tensor("ident_bf", [128, 128], BF16, kind="ExternalInput").ap()
    t_idf = nc.dram_tensor("ident_f32", [128, 128], F32, kind="ExternalInput").ap()
    t_tri = nc.dram_tensor("trimask", [128, 128], BF16, kind="ExternalInput").ap()
    t_o8 = nc.dram_tensor("ones8s", [8, 128, 8], BF16, kind="ExternalInput").ap()
    t_out = nc.dram_tensor("out", [nseq, S, D], F32, kind="ExternalOutput").ap()
    t_w1t = nc.dram_tensor("w1t_s", [32, 8, 128, 128], BF16, kind="Internal").ap()
    t_wq4 = nc.dram_tensor("wq4_s", [4, 128, 8192], BF16, kind="Internal").ap()
    t_w2t = nc.dram_tensor("w2t_s", [32, 128, D], BF16, kind="Internal").ap()

    es = contextlib.ExitStack()
    sb = lambda n, shape, dt: es.enter_context(nc.sbuf_tensor(n, shape, dt))

    ident_bf = sb("c_idb", [128, 128], BF16)
    ident_f32 = sb("c_idf", [128, 128], F32)
    trimask = sb("c_tri", [128, 128], BF16)
    ones8s = sb("c_o8", [128, 64], BF16)
    ones1b = sb("c_1b", [1, 128], BF16)
    ones1f = sb("c_1f", [1, 128], F32)
    onescol = sb("c_oc", [128, 1], F32)
    qb_t = sb("c_qb", [128, 8], F32)
    kb_t = sb("c_kb", [128, 8], F32)
    vb_row = sb("c_vb", [1, D], BF16)
    ob_row = sb("c_ob", [1, D], BF16)
    cw_t = sb("c_cw", [128, 96], F32)
    cb_t = sb("c_cb", [128, 32], F32)
    sw1_bc = sb("c_s1", [128, 1], F32)
    mw1_bc = sb("c_m1", [128, 1], F32)
    sw2_bc = sb("c_s2", [128, 1], F32)
    mw2_bc = sb("c_m2", [128, 1], F32)
    row1 = sb("c_r1", [1, 16], F32)
    magic_bc = sb("c_mg", [128, 1], F32)
    nmagic_bc = sb("c_nmg", [128, 1], F32)

    ctrs = {n: Ctr(nc, es, n) for n in
            ("bar", "dl", "dw", "dst", "tn", "sc", "ve", "gp",
             "dlx0", "dlx1", "dlx2", "dlx3", "dlb0", "dlb1",
             "dw1a", "dw1b", "dw1c", "dw2a", "dw2b")}
    bar, dl, dw, dst = ctrs["bar"], ctrs["dl"], ctrs["dw"], ctrs["dst"]
    tn, sc, ve, gp = ctrs["tn"], ctrs["sc"], ctrs["ve"], ctrs["gp"]
    dlx = (ctrs["dlx0"], ctrs["dlx1"], ctrs["dlx2"], ctrs["dlx3"])
    dlb = (ctrs["dlb0"], ctrs["dlb1"])
    dw1 = (ctrs["dw1a"], ctrs["dw1b"], ctrs["dw1c"])
    dw2 = (ctrs["dw2a"], ctrs["dw2b"])

    r = Rec(nc)

    def barrier():
        base = bar.v
        for e in ENGS:
            er = getattr(r, e)
            bar.inc(er.nop() if e in ("sync", "gpsimd") else er.drain())
        for e in ENGS:
            getattr(r, e).wait_ge(bar.sem, base + 5)

    dl.inc(r.sync.dma_start(out=ident_bf[:, :], in_=t_idb[:, :]), 16)
    dl.inc(r.sync.dma_start(out=ident_f32[:, :], in_=t_idf[:, :]), 16)
    dl.inc(r.sync.dma_start(out=trimask[:, :], in_=t_tri[:, :]), 16)
    dl.inc(r.sync.dma_start(
        out=ones8s.rearrange("p (hp j) -> p hp j", hp=8),
        in_=t_o8.rearrange("hp p j -> p hp j")), 16)
    dl.inc(r.sync.dma_start(
        out=qb_t[:, :], in_=t_qb.rearrange("(hp hh) d -> (hh d) hp", hh=2),
        allow_slow_non_contiguous=True), 16)
    dl.inc(r.sync.dma_start(
        out=kb_t[:, :], in_=t_kb.rearrange("(hp hh) d -> (hh d) hp", hh=2),
        allow_slow_non_contiguous=True), 16)
    dl.inc(r.sync.dma_start(
        out=cw_t.rearrange("p (j hc) -> p j hc", j=3),
        in_=t_cw.rearrange("(hc p) one j -> p j (one hc)", p=128),
        allow_slow_non_contiguous=True), 16)
    dl.inc(r.sync.dma_start(
        out=cb_t[:, :], in_=t_cb.rearrange("(hc p) -> p hc", p=128),
        allow_slow_non_contiguous=True), 16)

    with nc.sbuf_tensor("p_vst", [1, 2 * D], F32) as vbst:
        dl.inc(r.sync.dma_start(out=vbst[:, 0:D],
                                in_=t_vb.rearrange("h d -> (h d)")[None, :]), 16)
        dl.inc(r.sync.dma_start(out=vbst[:, D:2 * D], in_=t_ob[None, :]), 16)
        r.wait(r.vector, dl)
        ve.inc(r.vector.tensor_copy(vb_row[:, :], vbst[:, 0:D]))
        ve.inc(r.vector.tensor_copy(ob_row[:, :], vbst[:, D:2 * D]))
        ve.inc(r.vector.memset(ones1b[:, :], 1.0))
        ve.inc(r.vector.memset(ones1f[:, :], 1.0))
        ve.inc(r.vector.memset(onescol[:, :], 1.0))
        ve.inc(r.vector.memset(magic_bc[:, :], MAGIC))
        ve.inc(r.vector.memset(nmagic_bc[:, :], -MAGIC))
        ve.inc(r.vector.tensor_tensor(trimask[:, :], trimask[:, :],
                                      ident_bf[:, :], ALU.subtract))
        ve.inc(r.vector.drain())
        ve.inc(r.vector.tensor_single_scalar(trimask[:, :], trimask[:, :],
                                             -10000.0, ALU.mult))
        r.emit()
        barrier()
        r.emit()

    with nc.sbuf_tensor("p_wst", [128, 4096], F32) as wstg, \
         nc.sbuf_tensor("p_wbf", [128, 8192], BF16) as wbf:
        ve_cast = [0, 0]
        dw_w = 0
        for i4, t_w in enumerate((t_qw, t_kw, t_vw, t_ow)):
            if i4 < 3:
                wsrc = t_w.rearrange("h (mc p) d -> mc p h d", p=128)
            else:
                wsrc = t_ow.rearrange("(hp hh) d m -> hp (hh d) m", hh=2)
            for k4 in range(4):
                par = k4 % 2
                if i4 + k4 > 0:
                    r.wait(r.sync, ve, ve_cast[par])
                if i4 < 3:
                    for mm in range(2):
                        dlx[par].inc(r.sync.dma_start(
                            out=wstg[:, par * 2048 + mm * 1024:
                                par * 2048 + (mm + 1) * 1024].rearrange(
                                "p (h d) -> p h d", h=NH),
                            in_=wsrc[2 * k4 + mm]), 16)
                else:
                    dlx[par].inc(r.sync.dma_start(
                        out=wstg[:, par * 2048:(par + 1) * 2048].rearrange(
                            "p (hp m) -> p hp m", hp=2),
                        in_=wsrc[2 * k4:2 * k4 + 2].rearrange(
                            "hp pd m -> pd hp m")), 16)
                r.wait(r.vector, dlx[par])
                if i4 + k4 > 0:
                    r.wait(r.vector, dw, dw_w)
                ve_cast[par] = ve.inc(r.vector.tensor_copy(
                    wbf[:, k4 * 2048:(k4 + 1) * 2048],
                    wstg[:, par * 2048:(par + 1) * 2048]))
                ve.inc(r.vector.drain())
                r.wait(r.gpsimd, ve, ve_cast[par])
                dw_w = dw.inc(r.gpsimd.dma_start(
                    out=t_wq4[i4, :, k4 * 2048:(k4 + 1) * 2048],
                    in_=wbf[:, k4 * 2048:(k4 + 1) * 2048]), 16)
        r.wait(r.gpsimd, dw, dw.v)
        r.emit()
        barrier()
        r.emit()

    def wprep_quant_all():
        with nc.sbuf_tensor("wq_st", [128, 2 * 4096], F32) as stg2, \
             nc.sbuf_tensor("wq_f", [128, 4096], F32) as qf, \
             nc.sbuf_tensor("wq_b", [128, 4096], BF16) as qb8, \
             nc.sbuf_tensor("wq_sp", [128, 4096], BF16) as strip, \
             nc.psum_tensor("wq_tp", [128, 1024], BF16) as tpp:
            ve_q = 0
            dw_s = 0
            ve_g = 0
            sc_q = [0, 0]
            for j16 in range(16):
                which = "w1" if j16 < 8 else "w2"
                t_w = t_w1 if j16 < 8 else t_w2
                sw_bc = sw1_bc if j16 < 8 else sw2_bc
                blk = j16 % 8
                par = j16 % 2
                stg = stg2[:, par * 4096:(par + 1) * 4096]
                if j16 >= 2:
                    r.wait(r.sync, sc, sc_q[par])
                if which == "w1":
                    src_w = t_w.rearrange("(hc p) (fc f) -> fc p hc f",
                                          p=128, f=128)[blk]
                    dlx[par].inc(r.sync.dma_start(
                        out=stg.rearrange("p (hc f) -> p hc f", hc=32),
                        in_=src_w), 16)
                else:
                    src_w = t_w.rearrange("(oc p) (hg hq) -> hg p oc hq",
                                          p=128, hq=512)[blk]
                    dlx[par].inc(r.sync.dma_start(
                        out=stg.rearrange("p (oc hq) -> p oc hq", oc=8),
                        in_=src_w), 16)
                r.wait(r.scalar, dlx[par])
                if j16 > 0:
                    r.wait(r.scalar, ve, ve_q)
                sc_q[par] = sc.inc(r.scalar.mul(qf[:, :], stg[:, :],
                                                sw_bc[:, 0:1]))
                r.wait(r.vector, sc)
                if j16 > 0:
                    r.wait(r.vector, dw, dw_s)
                ve.inc(r.vector.tensor_scalar(qf[:, :], qf[:, :], -1.0, 1.0,
                                              ALU.max, ALU.min))
                ve.inc(r.vector.drain())
                ve_q = ve.inc(r.vector.tensor_scalar(
                    qb8[:, :], qf[:, :], MAGIC, -MAGIC, ALU.add, ALU.add))
                r.wait(r.tensor, ve, ve_q)
                for g in range(4):
                    if not (j16 == 0 and g == 0):
                        r.wait(r.tensor, ve, ve_g)
                    for k in range(8):
                        if which == "w1":
                            src_t = qb8[:, (g * 8 + k) * 128:
                                        (g * 8 + k + 1) * 128]
                        else:
                            src_t = qb8[:, k * 512 + g * 128:
                                        k * 512 + (g + 1) * 128]
                        tn.inc(r.tensor.transpose(
                            tpp[:, k * 128:(k + 1) * 128], src_t,
                            ident_bf[:, :]))
                    r.wait(r.vector, tn)
                    ve_g = ve.inc(r.vector.tensor_copy(
                        strip[:, g * 1024:(g + 1) * 1024], tpp[:, :]))
                r.wait(r.gpsimd, ve, ve_g)
                if which == "w1":
                    dw_s = dw.inc(r.gpsimd.dma_start(
                        out=t_w1t[:, blk].rearrange("hc p h -> p hc h"),
                        in_=strip.rearrange("p (hc h) -> p hc h", hc=32)), 16)
                else:
                    dw_s = dw.inc(r.gpsimd.dma_start(
                        out=t_w2t[blk * 4:(blk + 1) * 4].rearrange(
                            "hl p o -> p hl o"),
                        in_=strip.rearrange("p (hl o) -> p hl o", hl=4)), 16)
            r.wait(r.gpsimd, dw, dw.v)
            r.emit()
            barrier()

    r.emit()

    for s in range(nseq):
        es_s = contextlib.ExitStack()
        sbs = lambda n, shape, dt: es_s.enter_context(
            nc.sbuf_tensor(f"{n}_{s}", shape, dt))

        es_qkv = contextlib.ExitStack()
        Q_sb = es_qkv.enter_context(
            nc.sbuf_tensor(f"Q_{s}", [128, 8 * S], BF16))
        K_sb = es_qkv.enter_context(
            nc.sbuf_tensor(f"K_{s}", [128, 8 * S], BF16))
        V_sb = es_qkv.enter_context(
            nc.sbuf_tensor(f"V_{s}", [128, NT * NH * 65], BF16))

        es_a = contextlib.ExitStack()
        QWb = es_a.enter_context(nc.sbuf_tensor(f"a_qw{s}", [128, 8192], BF16))
        KWb = es_a.enter_context(nc.sbuf_tensor(f"a_kw{s}", [128, 8192], BF16))
        VWb = es_a.enter_context(nc.sbuf_tensor(f"a_vw{s}", [128, 8192], BF16))
        xhat = es_a.enter_context(nc.sbuf_tensor(f"a_xh{s}", [128, 8 * S], BF16))
        xt2 = es_a.enter_context(nc.sbuf_tensor(f"a_xt{s}", [128, 4 * D], F32))
        xh2 = es_a.enter_context(nc.sbuf_tensor(f"a_xm{s}", [128, 2 * D], BF16))
        st24 = es_a.enter_context(nc.sbuf_tensor(f"a_st{s}", [128, 32], F32))
        bnsa = es_a.enter_context(nc.sbuf_tensor(f"a_bn{s}", [128, 48], F32))
        mva = es_a.enter_context(nc.sbuf_tensor(f"a_mv{s}", [128, 8], F32))
        tpp = es_a.enter_context(nc.psum_tensor(f"a_tp{s}", [128, 1024], BF16))
        qkps = es_a.enter_context(nc.psum_tensor(f"a_qk{s}", [128, 1024], F32))
        vps = es_a.enter_context(nc.psum_tensor(f"a_vp{s}", [128, 2048], F32))
        if True:
            if True:
                for i4, Wb in ((0, QWb), (1, KWb), (2, VWb)):
                    dlx[i4 % 2].inc(r.sync.dma_start(
                        out=Wb[:, :], in_=t_wq4[i4]), 16)
                for i4 in range(3):
                    r.wait(r.vector, dlx[i4 % 2])
                ve.inc(r.vector.memset(
                    V_sb.rearrange("p (kt h e) -> p kt h e", kt=NT, h=NH)
                    [:, :, :, 64:65], 1.0))
                r.emit()
                barrier()

            ve_xh = [0] * NT
            ve_vv = [0] * NT
            ve_ap = [0] * NT
            tn_xh = [0] * NT
            sc_qk = 0
            first_qk = True
            var_sA = mva.rearrange("p (j two) -> p two j", two=2)[:, 1:2, :]
            for p in range(NP):
                for j in range(4):
                    u = 4 * p + j
                    if u >= 4:
                        r.wait(r.sync, ve, ve_ap[u - 4])
                    dlx[j].inc(r.sync.dma_start(
                        out=xt2[:, j * D:(j + 1) * D],
                        in_=t_x[s, u * 128:(u + 1) * 128, :]), 16)
                for j in range(4):
                    r.wait(r.vector, dlx[j])
                    ve.inc(r.vector.bn_stats(bnsa[:, j * 12:j * 12 + 6],
                                             xt2[:, j * D:j * D + 512]))
                    ve.inc(r.vector.bn_stats(bnsa[:, j * 12 + 6:j * 12 + 12],
                                             xt2[:, j * D + 512:(j + 1) * D]))
                ve.inc(r.vector.drain())
                for j in range(4):
                    ve.inc(r.vector.bn_aggr(mva[:, j * 2:(j + 1) * 2],
                                            bnsa[:, j * 12:j * 12 + 12]))
                ve.inc(r.vector.drain())
                ve.inc(r.vector.tensor_scalar_add(
                    st24.rearrange("p (a b) -> p a b", a=1)[:, :, 0:4],
                    var_sA, EPS))
                r.wait(r.scalar, ve)
                sc.inc(r.scalar.activation(st24[:, 4:8], st24[:, 0:4],
                                           AF.Sqrt))
                r.wait(r.vector, sc)
                ve.inc(r.vector.reciprocal(st24[:, 8:12], st24[:, 4:8]))
                ve.inc(r.vector.drain())
                ve.inc(r.vector.tensor_mul(st24[:, 12:16], st24[:, 8:12],
                                           st24[:, 8:12]))
                ve.inc(r.vector.drain())
                ve.inc(r.vector.tensor_mul(st24[:, 12:16], st24[:, 12:16],
                                           st24[:, 0:4]))
                ve.inc(r.vector.drain())
                ve.inc(r.vector.tensor_scalar(st24[:, 12:16], st24[:, 12:16],
                                              -0.5, 1.5, ALU.mult, ALU.add))
                ve.inc(r.vector.drain())
                ve.inc(r.vector.tensor_mul(st24[:, 16:20], st24[:, 8:12],
                                           st24[:, 12:16]))
                ve.inc(r.vector.drain())
                for j in range(4):
                    u = 4 * p + j
                    ub = u % 2
                    xt = xt2[:, j * D:(j + 1) * D]
                    xh = xh2[:, ub * D:(ub + 1) * D]
                    if u >= 2:
                        r.wait(r.vector, tn, tn_xh[u - 2])
                    ve_ap[u] = ve.inc(r.vector.tensor_scalar(
                        xh[:, :], xt[:, :], mva[:, 2 * j:2 * j + 1],
                        st24[:, 16 + j:17 + j], ALU.subtract, ALU.mult))
                    r.wait(r.tensor, ve, ve_ap[u])
                    if u >= 1:
                        r.wait(r.tensor, ve, ve_xh[u - 1])
                    for c in range(8):
                        tn.inc(r.tensor.transpose(
                            tpp[:, c * 128:(c + 1) * 128],
                            xh[:, c * 128:(c + 1) * 128], ident_bf[:, :]))
                    tn_xh[u] = tn.v
                    r.wait(r.vector, tn, tn_xh[u])
                    ve_xh[u] = ve.inc(r.vector.tensor_copy(
                        xhat.rearrange("p (c t) -> p c t", c=8)
                        [:, :, u * 128:(u + 1) * 128],
                        tpp.rearrange("p (c t) -> p c t", c=8)))
                    r.wait(r.tensor, ve, ve_xh[u])
                    if u >= 2:
                        r.wait(r.tensor, ve, ve_vv[u - 2])
                    vpb = ub * 1024
                    for half in range(2):
                        for mc in range(8):
                            tn.inc(r.tensor.matmul(
                                vps[:, vpb + half * 512:vpb + (half + 1) * 512],
                                xhat[:, mc * S + u * 128:mc * S + (u + 1) * 128],
                                VWb[:, mc * 1024 + half * 512:
                                    mc * 1024 + (half + 1) * 512],
                                start=(mc == 0), stop=False))
                        tn.inc(r.tensor.matmul(
                            vps[:, vpb + half * 512:vpb + (half + 1) * 512],
                            ones1b[0:1, :],
                            vb_row[0:1, half * 512:(half + 1) * 512],
                            start=False, stop=True))
                    r.wait(r.vector, tn)
                    ve_vv[u] = ve.inc(r.vector.tensor_copy(
                        V_sb.rearrange("p (kt h e) -> p kt h e", kt=NT, h=NH)
                        [:, u, :, 0:64],
                        vps[:, vpb:vpb + 1024].rearrange("p (h d) -> p h d",
                                                         h=NH)))
                for W, bias, dstb in ((QWb, qb_t, Q_sb), (KWb, kb_t, K_sb)):
                    for oc in range(8):
                        off = (oc % 2) * 512
                        if first_qk and W is QWb and oc < 2 and p == 0:
                            pass
                        else:
                            r.wait(r.tensor, sc, sc_qk)
                        for mc in range(8):
                            tn.inc(r.tensor.matmul(
                                qkps[:, off:off + 512],
                                W[:, mc * 1024 + oc * 128:
                                  mc * 1024 + (oc + 1) * 128],
                                xhat[:, mc * S + p * 512:mc * S + (p + 1) * 512],
                                start=(mc == 0), stop=(mc == 7)))
                        r.wait(r.scalar, tn)
                        sc_qk = sc.inc(r.scalar.activation(
                            dstb[:, oc * S + p * 512:oc * S + (p + 1) * 512],
                            qkps[:, off:off + 512], AF.Identity,
                            bias=bias[:, oc:oc + 1]))
                first_qk = False
            r.emit()
            barrier()

        es_a.close()

        DBG.update({f"xhat{s}": xhat, f"Q{s}": Q_sb, f"K{s}": K_sb,
                    f"V{s}": V_sb, f"st24{s}": st24})
        x2 = es_s.enter_context(
            nc.sbuf_tensor(f"x2_{s}", [128, NT * D], BF16, side="right"))
        es_pw = contextlib.ExitStack()
        if s == 0:
            pstg = es_pw.enter_context(
                nc.sbuf_tensor("pw_stg", [128, 2 * 2048], F32, side="right"))
            pasum = es_pw.enter_context(
                nc.sbuf_tensor("pw_as", [128, 36], F32, side="right"))
            pw_flat = [t_w.rearrange("a b -> (a b)").rearrange(
                "(ci p a) -> ci p a", ci=16, p=128)
                for t_w in (t_w1, t_w2)]
            pw_abs = [0] * 32

            def stats_load(n):
                if n >= 32:
                    return
                par = n % 2
                if n >= 2:
                    r.wait(r.sync, ve, pw_abs[n - 2])
                dlx[2 + par].inc(r.sync.dma_start(
                    out=pstg[:, par * 2048:(par + 1) * 2048],
                    in_=pw_flat[n // 16][n % 16]), 16)

            def stats_step(n):
                par = n % 2
                r.wait(r.vector, dlx[2 + par])
                pw_abs[n] = ve.inc(r.vector.reduce_sum(
                    pasum[:, n:n + 1], pstg[:, par * 2048:(par + 1) * 2048],
                    axis=AX.X, apply_absolute_value=True))
                stats_load(n + 2)
        es_b = contextlib.ExitStack()
        OWb = es_b.enter_context(nc.sbuf_tensor(f"b_ow{s}", [128, 8192], BF16))
        Cbias = es_b.enter_context(nc.sbuf_tensor(f"b_cb{s}", [128, 8], F32))
        qsq = es_b.enter_context(nc.sbuf_tensor(f"b_sq{s}", [128, 512], BF16))
        smf = es_b.enter_context(nc.sbuf_tensor(f"b_sm{s}", [128, 16], F32))
        ptil = es_b.enter_context(nc.sbuf_tensor(f"b_pt{s}", [128, 2048], BF16))
        atok = es_b.enter_context(nc.sbuf_tensor(f"b_at{s}", [128, 4 * 1024], F32))
        attT = es_b.enter_context(nc.sbuf_tensor(f"b_tt{s}", [128, 1024], BF16))
        zrec = es_b.enter_context(nc.sbuf_tensor(f"b_zr{s}", [128, 8], F32))
        xre = es_b.enter_context(nc.sbuf_tensor(f"b_xr{s}", [128, 2 * D], F32))
        if True:
            with nc.psum_tensor(f"b_ut{s}", [128, 512], F32) as util:
                dlx[0].inc(r.sync.dma_start(out=OWb[:, :], in_=t_wq4[3]), 16)
                r.wait(r.vector, dlx[0])
                ve.inc(r.vector.engine_nop())

                ve_red = 0
                for ti, Xb in ((0, Q_sb), (1, K_sb)):
                    for p in range(NP):
                        for hp in range(8):
                            if not (ti == 0 and p == 0 and hp == 0):
                                r.wait(r.vector, tn, tn.v)
                            ve.inc(r.vector.tensor_mul(
                                qsq[:, :],
                                Xb[:, hp * S + p * 512:hp * S + (p + 1) * 512],
                                Xb[:, hp * S + p * 512:hp * S + (p + 1) * 512]))
                            r.wait(r.tensor, ve)
                            if hp == 0 and not (ti == 0 and p == 0):
                                r.wait(r.tensor, ve, ve_red)
                            tn.inc(r.tensor.matmul(
                                util[0:8, :], ones8s[:, hp * 8:(hp + 1) * 8],
                                qsq[:, :], start=(hp == 0), stop=(hp == 7)))
                        r.wait(r.vector, tn)
                        ve_red = ve.inc(r.vector.reduce_max(
                            smf[0:8, 8 + p:9 + p], util[0:8, :], axis=AX.X))
                    ve.inc(r.vector.drain())
                    ve_red = ve.inc(r.vector.reduce_max(
                        smf[0:8, ti:ti + 1], smf[0:8, 8:8 + NP], axis=AX.X))
                    ve.inc(r.vector.drain())
                ve.inc(r.vector.tensor_mul(smf[0:8, 2:3], smf[0:8, 0:1],
                                           smf[0:8, 1:2]))
                r.wait(r.scalar, ve)
                sc.inc(r.scalar.activation(smf[0:8, 3:4], smf[0:8, 2:3],
                                           AF.Sqrt))
                r.wait(r.vector, sc)
                ve.inc(r.vector.tensor_single_scalar(smf[0:8, 4:5],
                                                     smf[0:8, 3:4],
                                                     -1.02 * 0.125, ALU.mult))
                r.wait(r.tensor, ve)
                tn.inc(r.tensor.transpose(util[0:1, 0:8], smf[0:8, 4:5],
                                          ident_f32[0:8, 0:8]))
                r.wait(r.vector, tn)
                ve.inc(r.vector.tensor_copy(row1[:, 0:8], util[0:1, 0:8]))
                r.wait(r.tensor, ve)
                tn.inc(r.tensor.matmul(util[:, 0:8], ones1f[0:1, :],
                                       row1[0:1, 0:8], start=True, stop=True))
                r.wait(r.vector, tn)
                ve.inc(r.vector.tensor_copy(Cbias[:, :], util[:, 0:8]))
                r.emit()
                barrier()

            DBG.update({f"Cb{s}": Cbias, f"smf{s}": smf, f"pt{s}": ptil,
                        f"zr{s}": zrec, f"at{s}": atok})
            Lps = es_b.enter_context(
                nc.psum_tensor(f"b_lp{s}", [128, 2048], F32))
            AVps2 = es_b.enter_context(
                nc.psum_tensor(f"b_av{s}", [128, 2048], F32))
            sc_exp = [0, 0]
            ve_pn = 0
            ve_pn2 = [0, 0]
            ve_aT2 = 0
            ve_x2 = 0
            tn_tT = 0
            dl_x = [0] * NT
            for u in (0, 1):
                dl_x[u] = dlb[u % 2].inc(r.sync.dma_start(
                    out=xre[:, (u % 2) * D:(u % 2 + 1) * D],
                    in_=t_x[s, u * 128:(u + 1) * 128, :]), 16)
            if s == 0:
                stats_load(0)
                stats_load(1)
            for p in range(NP):
                kcmax = 4 * (p + 1)
                for hp in range(8):
                    tn_log = [0, 0]

                    def emit_logits(kc, hp=hp, p=p):
                        lb = kc % 2
                        Lp = Lps[:, lb * 1024:(lb + 1) * 1024]
                        r.wait(r.tensor, sc, sc_exp[lb])
                        if p > 0 and hp == 0 and kc < 2:
                            r.wait(r.tensor, ve, ve_x2)
                        tn.inc(r.tensor.matmul(
                            Lp[:, 0:512],
                            K_sb[0:64,
                                 hp * S + kc * 128:hp * S + (kc + 1) * 128],
                            Q_sb[0:64,
                                 hp * S + p * 512:hp * S + (p + 1) * 512],
                            start=True, stop=True, tile_position=(0, 0)))
                        tn.inc(r.tensor.matmul(
                            Lp[:, 512:1024],
                            K_sb[64:128,
                                 hp * S + kc * 128:hp * S + (kc + 1) * 128],
                            Q_sb[64:128,
                                 hp * S + p * 512:hp * S + (p + 1) * 512],
                            start=True, stop=True, tile_position=(64, 0)))
                        if kc >= 4 * p:
                            ob = (kc - 4 * p) * 128
                            for hh in range(2):
                                tn.inc(r.tensor.matmul(
                                    Lp[:, hh * 512 + ob:hh * 512 + ob + 128],
                                    trimask[:, :], ident_bf[:, :],
                                    start=False, stop=True,
                                    skip_group_check=True))
                        tn_log[lb] = tn.v

                    emit_logits(0)
                    for kc in range(kcmax):
                        lb = kc % 2
                        if kc + 1 < kcmax:
                            emit_logits(kc + 1)
                        Lp = Lps[:, lb * 1024:(lb + 1) * 1024]
                        pb = ptil[:, lb * 1024:(lb + 1) * 1024]
                        o = (kc - 4 * p) * 128
                        r.wait(r.scalar, tn, tn_log[lb])
                        if kc < 4 * p:
                            sc_exp[lb] = sc.inc(r.scalar.activation(
                                pb, Lp, AF.Exp, scale=0.125,
                                bias=Cbias[:, hp:hp + 1]))
                        else:
                            if o > 0:
                                r.wait(r.vector, sc, sc_exp[lb])
                                r.wait(r.vector, tn, tn_log[lb])
                                ve.inc(r.vector.memset(
                                    pb.rearrange("p (h q) -> p h q", h=2)
                                    [:, :, 0:o], 0.0))
                                r.wait(r.scalar, ve)
                            sc_exp[lb] = sc.inc(r.scalar.activation(
                                pb.rearrange("p (h q) -> p h q", h=2)
                                [:, :, o:512],
                                Lp.rearrange("p (h q) -> p h q", h=2)
                                [:, :, o:512],
                                AF.Exp, scale=0.125, bias=Cbias[:, hp:hp + 1]))
                        r.wait(r.tensor, sc, sc_exp[lb])
                        if kc == 0:
                            r.wait(r.tensor, ve, ve_pn2[hp % 2])
                            if hp < 2 and p > 0:
                                r.wait(r.tensor, ve, ve_aT2)
                        ab = (hp % 2) * 1024
                        for hh in range(2):
                            for qs in range(4):
                                tn.inc(r.tensor.matmul(
                                    AVps2[:, ab + hh * 512 + qs * 65:
                                          ab + hh * 512 + qs * 65 + 65],
                                    pb[:, hh * 512 + qs * 128:
                                       hh * 512 + (qs + 1) * 128],
                                    V_sb[:, (kc * NH + 2 * hp + hh) * 65:
                                         (kc * NH + 2 * hp + hh) * 65 + 65],
                                    start=(kc == 0 and qs == 0),
                                    stop=(kc == kcmax - 1),
                                    skip_group_check=True))
                    tn_av = tn.v
                    r.wait(r.vector, tn, tn_av)
                    if p > 0 and hp == 0:
                        r.wait(r.vector, tn, tn_tT)
                    ve.inc(r.vector.reciprocal(
                        zrec.rearrange("p (hh q o) -> p hh q o", hh=2, o=1),
                        bass.AP(AVps2[:, :].tensor, ab + 64,
                                [[2048, 128], [512, 2], [65, 4], [1, 1]])))
                    ve.inc(r.vector.drain())
                    for hh in range(2):
                        for qs in range(4):
                            ve_pn = ve.inc(r.vector.tensor_single_scalar(
                                atok[:, qs * 1024 + (2 * hp + hh) * 64:
                                     qs * 1024 + (2 * hp + hh + 1) * 64],
                                AVps2[:, ab + hh * 512 + qs * 65:
                                      ab + hh * 512 + qs * 65 + 64],
                                zrec[:, hh * 4 + qs:hh * 4 + qs + 1], ALU.mult))
                    ve_pn2[hp % 2] = ve_pn
                    if s == 0:
                        stats_step(p * 8 + hp)
                for v in range(4):
                    u = 4 * p + v
                    r.wait(r.tensor, ve, ve_pn)
                    r.wait(r.tensor, sc, sc.v)
                    if v == 0:
                        r.wait(r.tensor, ve, ve_x2)
                    for c in range(8):
                        tn.inc(r.tensor.transpose(
                            AVps2[:, c * 128:(c + 1) * 128],
                            atok[:, v * 1024 + c * 128:v * 1024 + (c + 1) * 128],
                            ident_f32[:, :]))
                    tn_tT = tn.v
                    r.wait(r.vector, tn, tn_tT)
                    ve_aT = ve.inc(r.vector.tensor_copy(attT[:, :],
                                                        AVps2[:, 0:1024]))
                    ve_aT2 = ve_aT
                    r.wait(r.tensor, ve, ve_aT)
                    for half in range(2):
                        hs = half * 512
                        for c in range(8):
                            tn.inc(r.tensor.matmul(
                                Lps[:, hs:hs + 512],
                                attT[:, c * 128:(c + 1) * 128],
                                OWb[:, c * 1024 + hs:c * 1024 + hs + 512],
                                start=(c == 0), stop=False))
                        tn.inc(r.tensor.matmul(
                            Lps[:, hs:hs + 512], ones1b[0:1, :],
                            ob_row[0:1, hs:hs + 512], start=False, stop=True))
                    tn_o = tn.v
                    r.wait(r.vector, tn, tn_o)
                    r.wait(r.vector, dlb[u % 2], dl_x[u])
                    ve_x2 = ve.inc(r.vector.tensor_add(
                        x2[:, u * D:(u + 1) * D],
                        xre[:, (u % 2) * D:(u % 2 + 1) * D], Lps[:, 0:1024]))
                    nu = u + 2
                    if nu < NT:
                        r.wait(r.sync, ve, ve_x2)
                        dl_x[nu] = dlb[nu % 2].inc(r.sync.dma_start(
                            out=xre[:, (nu % 2) * D:(nu % 2 + 1) * D],
                            in_=t_x[s, nu * 128:(nu + 1) * 128, :]), 16)
            if s == 0:
                for n in range(NP * 8, 32):
                    stats_step(n)
            r.emit()
            barrier()

        es_b.close()
        es_qkv.close()

        if s == 0:
            with nc.sbuf_tensor("pw_s1", [1, 8], F32) as s1s, \
                 nc.psum_tensor("pw_ps", [128, 512], F32) as ups:
                for wi, (sw_bc, mw_bc) in enumerate(
                        ((sw1_bc, mw1_bc), (sw2_bc, mw2_bc))):
                    nel = HID * D
                    ve.inc(r.vector.reduce_sum(
                        pasum[:, 32 + wi:33 + wi],
                        pasum[:, wi * 16:wi * 16 + 16], axis=AX.X))
                    ve.inc(r.vector.drain())
                    r.wait(r.tensor, ve, ve.v)
                    tn.inc(r.tensor.matmul(
                        ups[0:1, wi:wi + 1], pasum[:, 32 + wi:33 + wi],
                        onescol[:, :], start=True, stop=True))
                    r.wait(r.vector, tn, tn.v)
                    ve.inc(r.vector.tensor_scalar(
                        s1s[:, 4 * wi:4 * wi + 1], ups[0:1, wi:wi + 1],
                        1.0 / nel, 1e-5, ALU.mult, ALU.max))
                    ve.inc(r.vector.drain())
                    ve.inc(r.vector.reciprocal(s1s[:, 4 * wi + 1:4 * wi + 2],
                                               s1s[:, 4 * wi:4 * wi + 1]))
                    ve.inc(r.vector.drain())
                    r.wait(r.tensor, ve, ve.v)
                    tn.inc(r.tensor.matmul(
                        ups[:, 0:1], ones1f[0:1, :],
                        s1s[0:1, 4 * wi:4 * wi + 1], start=True, stop=True))
                    tn.inc(r.tensor.matmul(
                        ups[:, 1:2], ones1f[0:1, :],
                        s1s[0:1, 4 * wi + 1:4 * wi + 2],
                        start=True, stop=True))
                    r.wait(r.vector, tn, tn.v)
                    ve.inc(r.vector.tensor_copy(mw_bc[:, :], ups[:, 0:1]))
                    ve.inc(r.vector.tensor_copy(sw_bc[:, :], ups[:, 1:2]))
                    ve.inc(r.vector.drain())
                r.emit()
                barrier()
            wprep_quant_all()
            r.emit()
        es_pw.close()

        es_c = contextlib.ExitStack()
        xq2 = es_c.enter_context(nc.sbuf_tensor(f"c_xq{s}", [128, 8 * SP2], BF16))
        g1col = es_c.enter_context(nc.sbuf_tensor(f"c_gc{s}", [128, NT], F32))
        cst = es_c.enter_context(nc.sbuf_tensor(f"c_st{s}", [128, 64], F32))
        bnsc = es_c.enter_context(nc.sbuf_tensor(f"c_bn{s}", [128, 48], F32))
        mvc = es_c.enter_context(nc.sbuf_tensor(f"c_mv{s}", [128, 8], F32))
        if True:
            es_cp = contextlib.ExitStack()
            tppC = es_cp.enter_context(
                nc.psum_tensor(f"c_tp{s}", [128, 1024], BF16))
            cxt = es_cp.enter_context(
                nc.sbuf_tensor(f"c_xv{s}", [128, 4 * D], F32, side="right"))
            cxm = es_cp.enter_context(
                nc.sbuf_tensor(f"c_xm{s}", [128, 4 * D], BF16, side="right"))
            cpb = es_cp.enter_context(
                nc.sbuf_tensor(f"c_pb{s}", [128, 192], F32, side="right"))
            mv16 = es_cp.enter_context(
                nc.sbuf_tensor(f"c_mw{s}", [128, 32], F32, side="right"))
            cs2 = es_cp.enter_context(
                nc.sbuf_tensor(f"c_cq{s}", [128, 128], F32, side="right"))
            ve.inc(r.vector.memset(
                xq2.rearrange("p (c t) -> p c t", c=8)[:, :, 0:1], 0.0))
            ve.inc(r.vector.memset(
                xq2.rearrange("p (c t) -> p c t", c=8)[:, :, S + 1:S + 2], 0.0))
            tn_pq = [0] * NT
            ve_pq = [0] * NT
            sc_r2 = [0] * NT
            sc_r1 = [0] * NT
            for u in range(NT):
                ve.inc(r.vector.bn_stats(cpb[:, u * 12:u * 12 + 6],
                                         x2[:, u * D:u * D + 512]))
                ve.inc(r.vector.bn_stats(cpb[:, u * 12 + 6:u * 12 + 12],
                                         x2[:, u * D + 512:(u + 1) * D]))
            ve.inc(r.vector.drain())
            for u in range(NT):
                ve.inc(r.vector.bn_aggr(mv16[:, u * 2:(u + 1) * 2],
                                        cpb[:, u * 12:u * 12 + 12]))
            ve.inc(r.vector.drain())
            var16 = mv16.rearrange("p (j two) -> p two j", two=2)[:, 1:2, :]
            ve.inc(r.vector.tensor_scalar_add(
                cs2.rearrange("p (a b) -> p a b", a=1)[:, :, 0:16], var16,
                EPS))
            r.wait(r.scalar, ve)
            sc.inc(r.scalar.activation(cs2[:, 16:32], cs2[:, 0:16], AF.Sqrt))
            r.wait(r.vector, sc)
            ve.inc(r.vector.reciprocal(cs2[:, 32:48], cs2[:, 16:32]))
            ve.inc(r.vector.drain())
            ve.inc(r.vector.tensor_mul(cs2[:, 48:64], cs2[:, 32:48],
                                       cs2[:, 32:48]))
            ve.inc(r.vector.drain())
            ve.inc(r.vector.tensor_mul(cs2[:, 48:64], cs2[:, 48:64],
                                       cs2[:, 0:16]))
            ve.inc(r.vector.drain())
            ve.inc(r.vector.tensor_scalar(cs2[:, 48:64], cs2[:, 48:64],
                                          -0.5, 1.5, ALU.mult, ALU.add))
            ve.inc(r.vector.drain())
            ve.inc(r.vector.tensor_mul(cs2[:, 64:80], cs2[:, 32:48],
                                       cs2[:, 48:64]))
            ve.inc(r.vector.drain())
            for tq in range(NT // 4):
                q0 = 4 * tq
                if tq >= 1:
                    r.wait(r.vector, tn, tn_pq[q0 - 1])
                    r.wait(r.vector, sc, sc_r1[q0 - 1])
                for j in range(4):
                    u = q0 + j
                    ve.inc(r.vector.tensor_scalar(
                        cxt[:, j * D:(j + 1) * D], x2[:, u * D:(u + 1) * D],
                        mv16[:, 2 * u:2 * u + 1], cs2[:, 64 + u:65 + u],
                        ALU.subtract, ALU.mult))
                ve.inc(r.vector.drain())
                for j in range(4):
                    u = q0 + j
                    ve.inc(r.vector.reduce_max(cs2[:, 80 + u:81 + u],
                                               cxt[:, j * D:(j + 1) * D],
                                               axis=AX.X,
                                               apply_absolute_value=True))
                ve.inc(r.vector.drain())
                ve.inc(r.vector.tensor_scalar_max(
                    cs2[:, 96 + q0:100 + q0], cs2[:, 80 + q0:84 + q0], 1e-5))
                ve.inc(r.vector.drain())
                ve.inc(r.vector.reciprocal(cs2[:, 112 + q0:116 + q0],
                                           cs2[:, 96 + q0:100 + q0]))
                ve.inc(r.vector.drain())
                ve.inc(r.vector.tensor_single_scalar(
                    cs2[:, 112 + q0:116 + q0], cs2[:, 112 + q0:116 + q0],
                    127.0, ALU.mult))
                ve.inc(r.vector.tensor_scalar(
                    g1col[:, q0:q0 + 4], cs2[:, 96 + q0:100 + q0],
                    mw1_bc[:, 0:1], 1.0 / 127.0, ALU.mult, ALU.mult))
                ve_q = ve.inc(r.vector.drain())
                r.wait(r.scalar, ve, ve_q)
                if tq >= 1:
                    r.wait(r.scalar, tn, tn_pq[q0 - 1])
                for pj in range(2):
                    for j in (2 * pj, 2 * pj + 1):
                        u = q0 + j
                        sc_r1[u] = sc.inc(r.scalar.activation(
                            cxt[:, j * D:(j + 1) * D],
                            cxt[:, j * D:(j + 1) * D], AF.Identity,
                            scale=cs2[:, 112 + u:113 + u],
                            bias=magic_bc[:, 0:1]))
                    sc.inc(r.scalar.drain())
                    for j in (2 * pj, 2 * pj + 1):
                        u = q0 + j
                        sc_r2[u] = sc.inc(r.scalar.activation(
                            cxm[:, j * D:(j + 1) * D],
                            cxt[:, j * D:(j + 1) * D], AF.Identity,
                            bias=nmagic_bc[:, 0:1]))
                for j in range(4):
                    u = q0 + j
                    r.wait(r.tensor, sc, sc_r2[u])
                    if not (tq == 0 and j == 0):
                        r.wait(r.tensor, ve, ve_pq[u - 1])
                    for c in range(8):
                        tn.inc(r.tensor.transpose(
                            tppC[:, c * 128:(c + 1) * 128],
                            cxm[:, j * D + c * 128:j * D + (c + 1) * 128],
                            ident_bf[:, :]))
                    tn_pq[u] = tn.v
                    r.wait(r.vector, tn, tn_pq[u])
                    ve_pq[u] = ve.inc(r.vector.tensor_copy(
                        xq2.rearrange("p (c t) -> p c t", c=8)
                        [:, :, 1 + u * 128:1 + (u + 1) * 128],
                        tppC.rearrange("p (c t) -> p c t", c=8)))
            r.emit()
            barrier()
            es_cp.close()

            hps = es_c.enter_context(nc.psum_tensor(f"c_hp{s}",
                                                    [128, 2048], F32))
            ops = es_c.enter_context(nc.psum_tensor(f"c_op{s}",
                                                    [128, 2048], F32))
            Hbq = es_c.enter_context(
                nc.sbuf_tensor(f"c_H{s}", [128, 32 * 514], BF16))
            Hb2 = es_c.enter_context(
                nc.sbuf_tensor(f"c_H2{s}", [128, 32 * 512], BF16))
            G1sb = es_c.enter_context(
                nc.sbuf_tensor(f"c_g1b{s}", [128, 514], F32))
            w1b = es_c.enter_context(
                nc.sbuf_tensor(f"c_w1b{s}", [128, 3 * 2048], BF16))
            w2b = es_c.enter_context(
                nc.sbuf_tensor(f"c_w2b{s}", [128, 2 * 2048], BF16))
            dgb = es_c.enter_context(
                nc.sbuf_tensor(f"c_dg{s}", [128, 3 * 384], BF16))
            ct1 = es_c.enter_context(
                nc.sbuf_tensor(f"c_t1{s}", [128, 2 * 512], F32))
            ct2 = es_c.enter_context(
                nc.sbuf_tensor(f"c_t2{s}", [128, 512], F32))
            mxt = es_c.enter_context(
                nc.sbuf_tensor(f"c_mx{s}", [128, 2 * 512], BF16))
            mxn = es_c.enter_context(
                nc.sbuf_tensor(f"c_mn{s}", [128, 2 * 512], BF16))
            hqg = es_c.enter_context(
                nc.sbuf_tensor(f"c_hq{s}", [128, 32 * 512], BF16))
            S2sb = es_c.enter_context(
                nc.sbuf_tensor(f"c_ss{s}", [128, 512], F32))
            g2buf = es_c.enter_context(
                nc.sbuf_tensor(f"c_g2{s}", [128, 8], F32))
            obuf = es_c.enter_context(
                nc.sbuf_tensor(f"c_obf{s}", [128, D], F32, side="right"))
            ve_h = [0] * 32
            sc_sil = [0] * 32
            ve_hq = [0] * 32
            tn_cv = [0] * 32
            ve_dgb = [0] * 32
            slot_free = [0, 0, 0]
            ve_mx = 0
            ve_ev = 0
            ve_g1 = 0
            gp_ob = 0
            dl_w1 = [0] * 16
            st_ch = {}
            w2_done = {}
            g2s = {}
            w2b_cons = [0, 0]

            def emit_w2_sub(t, half, glob):
                vp = 0 if half == 0 else 2
                wb2 = glob % 2
                g2c = 2 * glob
                r.wait(r.sync, tn, w2b_cons[wb2])
                dlw = dw2[wb2].inc(r.sync.dma_start(
                    out=w2b[:, wb2 * 2048:(wb2 + 1) * 2048].rearrange(
                        "p (l m) -> p l m", l=2),
                    in_=t_w2t[g2c:g2c + 2].rearrange("l p m -> p l m")), 16)
                if glob == 0:
                    r.wait(r.tensor, ve, ve_ev)
                r.wait(r.tensor, ve, ve_hq[2 * glob + 1])
                r.wait(r.tensor, dw2[wb2], dlw)
                for l in range(2):
                    hl2 = g2c + l
                    mv = w2b[:, (wb2 * 2 + l) * 1024:
                             (wb2 * 2 + l + 1) * 1024]
                    first = (glob == 0 and l == 0)
                    last = (glob == 15 and l == 1)
                    for v in (vp, vp + 1):
                        vv = v - vp
                        tn.inc(r.tensor.matmul(
                            ops[:, (2 * vv) * 512:(2 * vv + 1) * 512],
                            hqg[:, hl2 * 512 + v * 128:
                                hl2 * 512 + (v + 1) * 128],
                            mv[:, 0:512], start=first, stop=last,
                            skip_group_check=True))
                        tn.inc(r.tensor.matmul(
                            ops[:, (2 * vv + 1) * 512:(2 * vv + 2) * 512],
                            hqg[:, hl2 * 512 + v * 128:
                                hl2 * 512 + (v + 1) * 128],
                            mv[:, 512:1024], start=first, stop=last,
                            skip_group_check=True))
                w2b_cons[wb2] = tn.v

            def emit_final_pair(t, vp):
                nonlocal ve_ev, gp_ob
                tn_w2 = tn.v
                for vv in range(2):
                    v = vp + vv
                    u = 4 * t + v
                    r.wait(r.vector, tn, tn_w2)
                    r.wait(r.vector, dst, gp_ob)
                    ve.inc(r.vector.scalar_tensor_tensor(
                        obuf[:, 0:512], ops[:, (2 * vv) * 512:
                                            (2 * vv + 1) * 512],
                        g2s[t][:, v:v + 1],
                        x2[:, u * D:u * D + 512], ALU.mult, ALU.add))
                    ve_evl = ve.inc(r.vector.scalar_tensor_tensor(
                        obuf[:, 512:1024], ops[:, (2 * vv + 1) * 512:
                                               (2 * vv + 2) * 512],
                        g2s[t][:, v:v + 1],
                        x2[:, u * D + 512:(u + 1) * D], ALU.mult, ALU.add))
                    ve.inc(r.vector.drain())
                    r.wait(r.gpsimd, ve, ve_evl)
                    gp_ob = dst.inc(r.gpsimd.dma_start(
                        out=t_out[s, u * 128:(u + 1) * 128, :],
                        in_=obuf[:, :]), 16)
                    ve_ev = ve_evl

            def emit_conv_silu(t, hc):
                par = hc % 2
                dbase = (hc % 3) * 384
                r.wait(r.tensor, ve, ve_h[hc])
                r.wait(r.tensor, ve, ve_dgb[hc])
                for j in range(3):
                    tn.inc(r.tensor.matmul(
                        hps[:, par * 1024:par * 1024 + 512],
                        dgb[:, dbase + j * 128:dbase + (j + 1) * 128],
                        Hbq[:, hc * 514 + j:hc * 514 + j + 512],
                        start=(j == 0), stop=(j == 2)))
                tn_cv[hc] = tn.v
                r.wait(r.scalar, tn, tn_cv[hc])
                r.wait(r.scalar, ve, ve_hq[hc])
                sc_sil[hc] = sc.inc(r.scalar.activation(
                    Hb2[:, hc * 512:(hc + 1) * 512],
                    hps[:, par * 1024:par * 1024 + 512],
                    AF.Silu, bias=cb_t[:, hc:hc + 1]))

            def emit_mx(gq):
                r.wait(r.vector, sc, sc_sil[4 * gq + 3])
                for j in range(4):
                    c = 4 * gq + j
                    ja = (j % 2) * 512
                    hsl = Hb2[:, c * 512:(c + 1) * 512]
                    if j == 2:
                        ve.inc(r.vector.drain())
                    if gq == 0 and j < 2:
                        ve.inc(r.vector.tensor_copy(mxt[:, ja:ja + 512], hsl))
                        ve.inc(r.vector.tensor_copy(mxn[:, ja:ja + 512], hsl))
                    else:
                        ve.inc(r.vector.tensor_tensor(
                            mxt[:, ja:ja + 512], mxt[:, ja:ja + 512],
                            hsl, ALU.max))
                        ve.inc(r.vector.tensor_tensor(
                            mxn[:, ja:ja + 512], mxn[:, ja:ja + 512],
                            hsl, ALU.min))
                ve.inc(r.vector.drain())

            def emit_hq_batch(t, c0):
                r.wait(r.vector, sc, sc_sil[c0 + 3])
                for pair in range(2):
                    for k in range(2):
                        c = c0 + 2 * pair + k
                        ve.inc(r.vector.tensor_mul(
                            ct1[:, k * 512:(k + 1) * 512],
                            Hb2[:, c * 512:(c + 1) * 512], S2sb[:, :]))
                    ve.inc(r.vector.drain())
                    for k in range(2):
                        c = c0 + 2 * pair + k
                        ve_hq[c] = ve.inc(r.vector.tensor_scalar(
                            hqg[:, c * 512:(c + 1) * 512],
                            ct1[:, k * 512:(k + 1) * 512],
                            MAGIC, -MAGIC, ALU.add, ALU.add))
                    ve.inc(r.vector.drain())

            def emit_g1(t):
                nonlocal ve_g1
                if t > 0:
                    r.wait(r.tensor, ve, st_ch[t - 1])
                for b in range(4):
                    tn.inc(r.tensor.transpose(
                        hps[0:1, 1024 + b * 128:1024 + (b + 1) * 128],
                        g1col[:, 4 * t + b:4 * t + b + 1], ident_f32[:, :]))
                if t + 1 < NCT:
                    tn.inc(r.tensor.transpose(
                        hps[0:1, 1536:1664],
                        g1col[:, 4 * t + 4:4 * t + 5], ident_f32[:, :]))
                tn_g = tn.v
                r.wait(r.vector, tn, tn_g)
                if t == 0:
                    ve.inc(r.vector.memset(G1sb[0:1, 0:1], 0.0))
                else:
                    ve.inc(r.vector.tensor_copy(G1sb[0:1, 0:1],
                                                G1sb[0:1, 512:513]))
                ve.inc(r.vector.drain())
                ve.inc(r.vector.tensor_copy(G1sb[0:1, 1:513],
                                            hps[0:1, 1024:1536]))
                if t + 1 < NCT:
                    ve.inc(r.vector.tensor_copy(G1sb[0:1, 513:514],
                                                hps[0:1, 1536:1537]))
                else:
                    ve.inc(r.vector.memset(G1sb[0:1, 513:514], 0.0))
                ve_row = ve.inc(r.vector.drain())
                r.wait(r.tensor, ve, ve_row)
                tn.inc(r.tensor.matmul(hps[:, 1024:1536], ones1f[0:1, :],
                                       G1sb[0:1, 0:512], start=True,
                                       stop=True))
                tn.inc(r.tensor.matmul(hps[:, 1536:1538], ones1f[0:1, :],
                                       G1sb[0:1, 512:514], start=True,
                                       stop=True))
                r.wait(r.vector, tn, tn.v)
                ve.inc(r.vector.tensor_copy(G1sb[:, :], hps[:, 1024:1538]))
                ve_g1 = ve.inc(r.vector.drain())

            def emit_g1_w1_conv(t, w2t=None):
                nonlocal ve_mx

                def w1_load(k):
                    sl = k % 3
                    r.wait(r.sync, tn, slot_free[sl])
                    dl_w1[k] = dw1[sl].inc(r.sync.dma_start(
                        out=w1b[:, sl * 2048:(sl + 1) * 2048].rearrange(
                            "p (l f h) -> p l f h", l=2, f=8),
                        in_=t_w1t[2 * k:2 * k + 2].rearrange(
                            "l f p h -> p l f h")), 16)

                def dg_build(hc):
                    r.wait(r.vector, tn, tn_cv[(hc - 3) % 32])
                    for j in range(3):
                        ve_dgb[hc] = ve.inc(r.vector.tensor_scalar(
                            dgb[:, (hc % 3) * 384 + j * 128:
                                (hc % 3) * 384 + (j + 1) * 128],
                            ident_f32[:, :],
                            cw_t[:, j * 32 + hc:j * 32 + hc + 1], 0.0,
                            ALU.mult, ALU.add))

                def w1_chunk(hc, sl):
                    par = hc % 2
                    if t > 0 or hc >= 2:
                        r.wait(r.tensor, sc, sc_sil[(hc - 2) % 32])
                    if t > 0 and hc == 1:
                        r.wait(r.tensor, ve, ve_g1)
                    lbase = sl * 2048 + (hc % 2) * 1024
                    for fc in range(8):
                        wsl = w1b[:, lbase + fc * 128:lbase + (fc + 1) * 128]
                        tn.inc(r.tensor.matmul(
                            hps[:, par * 1024:par * 1024 + 257], wsl,
                            xq2[:, fc * SP2 + t * 512:
                                fc * SP2 + t * 512 + 257],
                            start=(fc == 0), stop=(fc == 7)))
                        tn.inc(r.tensor.matmul(
                            hps[:, par * 1024 + 512:par * 1024 + 769], wsl,
                            xq2[:, fc * SP2 + t * 512 + 257:
                                fc * SP2 + t * 512 + 514],
                            start=(fc == 0), stop=(fc == 7)))
                    tn_h = tn.v
                    r.wait(r.vector, tn, tn_h)
                    ve_h[hc] = ve.inc(r.vector.tensor_mul(
                        Hbq[:, hc * 514:hc * 514 + 514].rearrange(
                            "p (b c) -> p b c", b=2),
                        bass.AP(hps[:, :].tensor, par * 1024,
                                [[2048, 128], [512, 2], [1, 257]]),
                        G1sb.rearrange("p (b c) -> p b c", b=2)))

                dg_build(0)
                w1_load(0)
                w1_load(1)
                for g in range(16):
                    sl = g % 3
                    if g + 2 <= 15:
                        w1_load(g + 2)
                    dg_build(2 * g + 1)
                    if g == 0:
                        dg_build(2)
                    r.wait(r.tensor, dw1[sl], dl_w1[g])
                    if g == 0:
                        r.wait(r.tensor, ve, ve_g1)
                    w1_chunk(2 * g, sl)
                    if g > 0:
                        emit_conv_silu(t, 2 * g - 1)
                        if g < 15:
                            dg_build(2 * g + 2)
                    if w2t is not None:
                        emit_w2_sub(w2t, g // 8, 2 * (g % 8))
                        if g <= 6:
                            emit_hq_batch(w2t, 4 * (g + 1))
                    w1_chunk(2 * g + 1, sl)
                    slot_free[sl] = tn.v
                    emit_conv_silu(t, 2 * g)
                    if w2t is not None:
                        emit_w2_sub(w2t, g // 8, 2 * (g % 8) + 1)
                        if g == 7:
                            emit_final_pair(w2t, 0)
                        if g == 15:
                            emit_final_pair(w2t, 2)
                            w2_done[w2t] = tn.v
                    if g % 2 == 0 and g >= 2:
                        emit_mx(g // 2 - 1)
                emit_conv_silu(t, 31)
                emit_mx(6)
                emit_mx(7)
                ve.inc(r.vector.tensor_tensor(mxt[:, 0:512], mxt[:, 0:512],
                                              mxt[:, 512:1024], ALU.max))
                ve.inc(r.vector.tensor_tensor(mxn[:, 0:512], mxn[:, 0:512],
                                              mxn[:, 512:1024], ALU.min))
                ve.inc(r.vector.drain())
                ve.inc(r.vector.tensor_single_scalar(mxn[:, 0:512],
                                                     mxn[:, 0:512],
                                                     -1.0, ALU.mult))
                ve.inc(r.vector.drain())
                ve.inc(r.vector.tensor_tensor(
                    mxt[:, 0:512], mxt[:, 0:512], mxn[:, 0:512], ALU.max))
                ve.inc(r.vector.drain())
                ve_mx = ve.inc(r.vector.tensor_copy(ct2[:, 0:512],
                                                    mxt[:, 0:512]))
                ve.inc(r.vector.drain())

            def emit_chain(t):
                r.wait(r.tensor, ve, ve_mx)
                for b in range(4):
                    tn.inc(r.tensor.transpose(
                        hps[:, 1024 + b * 128:1024 + (b + 1) * 128],
                        ct2[:, b * 128:(b + 1) * 128], ident_f32[:, :]))
                tn_mx = tn.v
                r.wait(r.vector, tn, tn_mx)
                for b in range(4):
                    ve.inc(r.vector.reduce_max(
                        cst[:, 32 + b:33 + b],
                        hps[:, 1024 + b * 128:1024 + (b + 1) * 128],
                        axis=AX.X))
                ve.inc(r.vector.drain())
                ve.inc(r.vector.tensor_scalar_max(cst[:, 36:40], cst[:, 32:36],
                                                  1e-5))
                ve.inc(r.vector.drain())
                ve.inc(r.vector.reciprocal(cst[:, 40:44], cst[:, 36:40]))
                ve.inc(r.vector.drain())
                ve.inc(r.vector.tensor_single_scalar(cst[:, 44:48],
                                                     cst[:, 40:44],
                                                     127.0, ALU.mult))
                ve_g2 = ve.inc(r.vector.tensor_scalar(
                    g2buf[:, (t % 2) * 4:(t % 2) * 4 + 4], cst[:, 36:40],
                    mw2_bc[:, 0:1], 1.0 / 127.0, ALU.mult, ALU.mult))
                g2s[t] = g2buf[:, (t % 2) * 4:(t % 2) * 4 + 4]
                r.wait(r.tensor, ve, ve_g2)
                for b in range(4):
                    tn.inc(r.tensor.transpose(
                        hps[0:1, 1024 + b * 128:1024 + (b + 1) * 128],
                        cst[:, 44 + b:45 + b], ident_f32[:, :]))
                r.wait(r.vector, tn, tn.v)
                ve_s2 = ve.inc(r.vector.tensor_copy(S2sb[0:1, 0:512],
                                                    hps[0:1, 1024:1536]))
                r.wait(r.tensor, ve, ve_s2)
                tn_s2 = tn.inc(r.tensor.matmul(hps[:, 1024:1536],
                                               ones1f[0:1, :],
                                               S2sb[0:1, 0:512],
                                               start=True, stop=True))
                r.wait(r.vector, tn, tn_s2)
                ve.inc(r.vector.tensor_copy(S2sb[:, :], hps[:, 1024:1536]))
                ve.inc(r.vector.drain())
                st_ch[t] = ve.v

            emit_g1(0)
            for t in range(NCT):
                emit_g1_w1_conv(t, w2t=(t - 1 if t > 0 else None))
                emit_chain(t)
                if t + 1 < NCT:
                    emit_g1(t + 1)
                if t > 0:
                    r.wait(r.vector, tn, w2_done[t - 1])
                emit_hq_batch(t, 0)
            tl = NCT - 1
            for g in range(16):
                emit_w2_sub(tl, g // 8, 2 * (g % 8))
                emit_w2_sub(tl, g // 8, 2 * (g % 8) + 1)
                if g == 7:
                    emit_final_pair(tl, 0)
                if g == 15:
                    emit_final_pair(tl, 2)
                    w2_done[tl] = tn.v
                if g <= 6:
                    emit_hq_batch(tl, 4 * (g + 1))
            r.wait(r.gpsimd, dst, dst.v)
            r.emit()
            barrier()
        es_c.close()
        es_s.close()
        r.emit()

    r.wait(r.gpsimd, dst, dst.v)
    gp.inc(r.gpsimd.engine_nop())
    r.emit()
    es.close()
    nc.finalize()
    return nc


_CACHE = {}


def _get_nc(nseq, S):
    key = (nseq, S)
    if key not in _CACHE:
        _CACHE[key] = build_program(nseq, S)
    return _CACHE[key]


def kernel(L0_attn_x, L0_attn_Q_w, L0_attn_Q_b, L0_attn_K_w, L0_attn_K_b,
           L0_attn_V_w, L0_attn_V_b, L0_attn_O_w, L0_attn_O_b,
           L0_ffn_w1, L0_ffn_conv_w, L0_ffn_conv_b, L0_ffn_w2):
    import ml_dtypes
    x = np.ascontiguousarray(np.asarray(L0_attn_x, dtype=np.float32))
    B, S, _ = x.shape
    ncores = 8
    nseq = B // ncores
    nc = _get_nc(nseq, S)

    tri = np.triu(np.ones((128, 128))).astype(ml_dtypes.bfloat16)
    o8 = np.zeros((8, 128, 8), dtype=ml_dtypes.bfloat16)
    for hp in range(8):
        o8[hp, :, hp] = 1.0
    asf = lambda a: np.ascontiguousarray(np.asarray(a, dtype=np.float32))
    common = {
        "qw": asf(L0_attn_Q_w), "kw": asf(L0_attn_K_w), "vw": asf(L0_attn_V_w),
        "ow": asf(L0_attn_O_w), "qbias": asf(L0_attn_Q_b),
        "kbias": asf(L0_attn_K_b), "vbias": asf(L0_attn_V_b),
        "obias": asf(L0_attn_O_b), "w1": asf(L0_ffn_w1),
        "cw": asf(L0_ffn_conv_w), "cb": asf(L0_ffn_conv_b),
        "w2": asf(L0_ffn_w2),
        "ident_bf": np.eye(128, dtype=ml_dtypes.bfloat16),
        "ident_f32": np.eye(128, dtype=np.float32),
        "trimask": tri,
        "ones8s": o8,
    }
    in_maps = []
    for i in range(ncores):
        m = dict(common)
        m["x"] = np.ascontiguousarray(x[i * nseq:(i + 1) * nseq])
        in_maps.append(m)
    res = run_bass_kernel_spmd(nc, in_maps, core_ids=list(range(ncores)))
    out = np.concatenate([np.asarray(res.results[i]["out"])
                          for i in range(ncores)], axis=0)
    return out.astype(np.float32)


# version 25
# speedup vs baseline: 1.0618x; 1.0253x over previous
import contextlib
import numpy as np

import concourse.bass as bass
import concourse.mybir as mybir
from concourse.bass_utils import run_bass_kernel_spmd

F32 = mybir.dt.float32
BF16 = mybir.dt.bfloat16
AF = mybir.ActivationFunctionType
ALU = mybir.AluOpType
AX = mybir.AxisListType

MAGIC = float(np.float32(3 * 2.0 ** 22))
D = 1024
NH = 16
DH = 64
HID = 4096
EPS = 1e-5

ENGS = ("sync", "tensor", "scalar", "vector", "gpsimd")
DBG = {}


class Ctr:
    def __init__(self, nc, es, name):
        self.sem = es.enter_context(nc.semaphore(name))
        self.v = 0

    def inc(self, h, k=1):
        h.then_inc(self.sem, k)
        self.v += k
        return self.v


class Handle:
    def __init__(self, entry):
        self.entry = entry

    def then_inc(self, sem, k):
        self.entry["incs"].append((sem, k))
        return self


class EngRec:
    def __init__(self, ops):
        self._ops = ops

    def __getattr__(self, name):
        ops = self._ops

        def call(*args, **kwargs):
            entry = {"name": name, "args": args, "kwargs": kwargs, "incs": []}
            ops.append(entry)
            return Handle(entry)

        return call


class Rec:
    def __init__(self, nc):
        self.nc = nc
        self._reset()

    def _reset(self):
        self.ops = {e: [] for e in ENGS}
        for e in ENGS:
            setattr(self, e, EngRec(self.ops[e]))

    def wait(self, engrec, ctr, v=None):
        if v is None:
            v = ctr.v
        assert v <= ctr.v, "wait on future value -> deadlock"
        if v > 0:
            engrec.wait_ge(ctr.sem, v)

    def emit(self):
        def replay(ops, eng):
            for en in ops:
                h = getattr(eng, en["name"])(*en["args"], **en["kwargs"])
                for sem, k in en["incs"]:
                    h.then_inc(sem, k)

        with self.nc.Block() as b:
            for e in ENGS:
                if self.ops[e]:
                    getattr(b, e)(
                        (lambda ops: (lambda eng: replay(ops, eng)))(self.ops[e]))
        self._reset()


def build_program(nseq, S):
    NP = S // 512
    NT = S // 128
    NCT = S // 512
    SP2 = S + 2

    nc = bass.Bass("TRN2", target_bir_lowering=False, debug=False)

    t_x = nc.dram_tensor("x", [nseq, S, D], F32, kind="ExternalInput").ap()
    t_qw = nc.dram_tensor("qw", [NH, D, DH], F32, kind="ExternalInput").ap()
    t_kw = nc.dram_tensor("kw", [NH, D, DH], F32, kind="ExternalInput").ap()
    t_vw = nc.dram_tensor("vw", [NH, D, DH], F32, kind="ExternalInput").ap()
    t_ow = nc.dram_tensor("ow", [NH, DH, D], F32, kind="ExternalInput").ap()
    t_qb = nc.dram_tensor("qbias", [NH, DH], F32, kind="ExternalInput").ap()
    t_kb = nc.dram_tensor("kbias", [NH, DH], F32, kind="ExternalInput").ap()
    t_vb = nc.dram_tensor("vbias", [NH, DH], F32, kind="ExternalInput").ap()
    t_ob = nc.dram_tensor("obias", [D], F32, kind="ExternalInput").ap()
    t_w1 = nc.dram_tensor("w1", [HID, D], F32, kind="ExternalInput").ap()
    t_cw = nc.dram_tensor("cw", [HID, 1, 3], F32, kind="ExternalInput").ap()
    t_cb = nc.dram_tensor("cb", [HID], F32, kind="ExternalInput").ap()
    t_w2 = nc.dram_tensor("w2", [D, HID], F32, kind="ExternalInput").ap()
    t_idb = nc.dram_tensor("ident_bf", [128, 128], BF16, kind="ExternalInput").ap()
    t_idf = nc.dram_tensor("ident_f32", [128, 128], F32, kind="ExternalInput").ap()
    t_tri = nc.dram_tensor("trimask", [128, 128], BF16, kind="ExternalInput").ap()
    t_o8 = nc.dram_tensor("ones8s", [8, 128, 8], BF16, kind="ExternalInput").ap()
    t_out = nc.dram_tensor("out", [nseq, S, D], F32, kind="ExternalOutput").ap()
    t_w1t = nc.dram_tensor("w1t_s", [32, 8, 128, 128], BF16, kind="Internal").ap()
    t_wq4 = nc.dram_tensor("wq4_s", [4, 128, 8192], BF16, kind="Internal").ap()
    t_w2t = nc.dram_tensor("w2t_s", [32, 128, D], BF16, kind="Internal").ap()

    es = contextlib.ExitStack()
    sb = lambda n, shape, dt: es.enter_context(nc.sbuf_tensor(n, shape, dt))

    ident_bf = sb("c_idb", [128, 128], BF16)
    ident_f32 = sb("c_idf", [128, 128], F32)
    trimask = sb("c_tri", [128, 128], BF16)
    ones8s = sb("c_o8", [128, 64], BF16)
    ones1b = sb("c_1b", [1, 128], BF16)
    ones1f = sb("c_1f", [1, 128], F32)
    onescol = sb("c_oc", [128, 1], F32)
    qb_t = sb("c_qb", [128, 8], F32)
    kb_t = sb("c_kb", [128, 8], F32)
    vb_row = sb("c_vb", [1, D], BF16)
    ob_row = sb("c_ob", [1, D], BF16)
    cw_t = sb("c_cw", [128, 96], F32)
    cb_t = sb("c_cb", [128, 32], F32)
    sw1_bc = sb("c_s1", [128, 1], F32)
    mw1_bc = sb("c_m1", [128, 1], F32)
    sw2_bc = sb("c_s2", [128, 1], F32)
    mw2_bc = sb("c_m2", [128, 1], F32)
    row1 = sb("c_r1", [1, 16], F32)
    magic_bc = sb("c_mg", [128, 1], F32)
    nmagic_bc = sb("c_nmg", [128, 1], F32)

    ctrs = {n: Ctr(nc, es, n) for n in
            ("bar", "dl", "dw", "dst", "tn", "sc", "ve", "gp",
             "dlx0", "dlx1", "dlx2", "dlx3", "dlb0", "dlb1",
             "dw1a", "dw1b", "dw1c", "dw2a", "dw2b")}
    bar, dl, dw, dst = ctrs["bar"], ctrs["dl"], ctrs["dw"], ctrs["dst"]
    tn, sc, ve, gp = ctrs["tn"], ctrs["sc"], ctrs["ve"], ctrs["gp"]
    dlx = (ctrs["dlx0"], ctrs["dlx1"], ctrs["dlx2"], ctrs["dlx3"])
    dlb = (ctrs["dlb0"], ctrs["dlb1"])
    dw1 = (ctrs["dw1a"], ctrs["dw1b"], ctrs["dw1c"])
    dw2 = (ctrs["dw2a"], ctrs["dw2b"])

    r = Rec(nc)

    def barrier():
        base = bar.v
        for e in ENGS:
            er = getattr(r, e)
            bar.inc(er.nop() if e in ("sync", "gpsimd") else er.drain())
        for e in ENGS:
            getattr(r, e).wait_ge(bar.sem, base + 5)

    dl.inc(r.sync.dma_start(out=ident_bf[:, :], in_=t_idb[:, :]), 16)
    dl.inc(r.sync.dma_start(out=ident_f32[:, :], in_=t_idf[:, :]), 16)
    dl.inc(r.sync.dma_start(out=trimask[:, :], in_=t_tri[:, :]), 16)
    dl.inc(r.sync.dma_start(
        out=ones8s.rearrange("p (hp j) -> p hp j", hp=8),
        in_=t_o8.rearrange("hp p j -> p hp j")), 16)
    dl.inc(r.sync.dma_start(
        out=qb_t[:, :], in_=t_qb.rearrange("(hp hh) d -> (hh d) hp", hh=2),
        allow_slow_non_contiguous=True), 16)
    dl.inc(r.sync.dma_start(
        out=kb_t[:, :], in_=t_kb.rearrange("(hp hh) d -> (hh d) hp", hh=2),
        allow_slow_non_contiguous=True), 16)
    dl.inc(r.sync.dma_start(
        out=cw_t.rearrange("p (j hc) -> p j hc", j=3),
        in_=t_cw.rearrange("(hc p) one j -> p j (one hc)", p=128),
        allow_slow_non_contiguous=True), 16)
    dl.inc(r.sync.dma_start(
        out=cb_t[:, :], in_=t_cb.rearrange("(hc p) -> p hc", p=128),
        allow_slow_non_contiguous=True), 16)

    with nc.sbuf_tensor("p_vst", [1, 2 * D], F32) as vbst:
        dl.inc(r.sync.dma_start(out=vbst[:, 0:D],
                                in_=t_vb.rearrange("h d -> (h d)")[None, :]), 16)
        dl.inc(r.sync.dma_start(out=vbst[:, D:2 * D], in_=t_ob[None, :]), 16)
        r.wait(r.vector, dl)
        ve.inc(r.vector.tensor_copy(vb_row[:, :], vbst[:, 0:D]))
        ve.inc(r.vector.tensor_copy(ob_row[:, :], vbst[:, D:2 * D]))
        ve.inc(r.vector.memset(ones1b[:, :], 1.0))
        ve.inc(r.vector.memset(ones1f[:, :], 1.0))
        ve.inc(r.vector.memset(onescol[:, :], 1.0))
        ve.inc(r.vector.memset(magic_bc[:, :], MAGIC))
        ve.inc(r.vector.memset(nmagic_bc[:, :], -MAGIC))
        ve.inc(r.vector.tensor_tensor(trimask[:, :], trimask[:, :],
                                      ident_bf[:, :], ALU.subtract))
        ve.inc(r.vector.drain())
        ve.inc(r.vector.tensor_single_scalar(trimask[:, :], trimask[:, :],
                                             -10000.0, ALU.mult))
        r.emit()
        barrier()
        r.emit()

    with nc.sbuf_tensor("p_wst", [128, 4096], F32) as wstg, \
         nc.sbuf_tensor("p_wbf", [128, 8192], BF16) as wbf:
        ve_cast = [0, 0]
        dw_w = 0
        for i4, t_w in enumerate((t_qw, t_kw, t_vw, t_ow)):
            if i4 < 3:
                wsrc = t_w.rearrange("h (mc p) d -> mc p h d", p=128)
            else:
                wsrc = t_ow.rearrange("(hp hh) d m -> hp (hh d) m", hh=2)
            for k4 in range(4):
                par = k4 % 2
                if i4 + k4 > 0:
                    r.wait(r.sync, ve, ve_cast[par])
                if i4 < 3:
                    for mm in range(2):
                        dlx[par].inc(r.sync.dma_start(
                            out=wstg[:, par * 2048 + mm * 1024:
                                par * 2048 + (mm + 1) * 1024].rearrange(
                                "p (h d) -> p h d", h=NH),
                            in_=wsrc[2 * k4 + mm]), 16)
                else:
                    dlx[par].inc(r.sync.dma_start(
                        out=wstg[:, par * 2048:(par + 1) * 2048].rearrange(
                            "p (hp m) -> p hp m", hp=2),
                        in_=wsrc[2 * k4:2 * k4 + 2].rearrange(
                            "hp pd m -> pd hp m")), 16)
                r.wait(r.vector, dlx[par])
                if i4 + k4 > 0:
                    r.wait(r.vector, dw, dw_w)
                ve_cast[par] = ve.inc(r.vector.tensor_copy(
                    wbf[:, k4 * 2048:(k4 + 1) * 2048],
                    wstg[:, par * 2048:(par + 1) * 2048]))
                ve.inc(r.vector.drain())
                r.wait(r.gpsimd, ve, ve_cast[par])
                dw_w = dw.inc(r.gpsimd.dma_start(
                    out=t_wq4[i4, :, k4 * 2048:(k4 + 1) * 2048],
                    in_=wbf[:, k4 * 2048:(k4 + 1) * 2048]), 16)
        r.wait(r.gpsimd, dw, dw.v)
        r.emit()
        barrier()
        r.emit()

    def wprep_quant_all():
        with nc.sbuf_tensor("wq_st", [128, 2 * 4096], F32) as stg2, \
             nc.sbuf_tensor("wq_f", [128, 4096], F32) as qf, \
             nc.sbuf_tensor("wq_b", [128, 4096], BF16) as qb8, \
             nc.sbuf_tensor("wq_sp", [128, 4096], BF16) as strip, \
             nc.psum_tensor("wq_tp", [128, 1024], BF16) as tpp:
            ve_q = 0
            dw_s = 0
            ve_g = 0
            sc_q = [0, 0]
            for j16 in range(16):
                which = "w1" if j16 < 8 else "w2"
                t_w = t_w1 if j16 < 8 else t_w2
                sw_bc = sw1_bc if j16 < 8 else sw2_bc
                blk = j16 % 8
                par = j16 % 2
                stg = stg2[:, par * 4096:(par + 1) * 4096]
                if j16 >= 2:
                    r.wait(r.sync, sc, sc_q[par])
                if which == "w1":
                    src_w = t_w.rearrange("(hc p) (fc f) -> fc p hc f",
                                          p=128, f=128)[blk]
                    dlx[par].inc(r.sync.dma_start(
                        out=stg.rearrange("p (hc f) -> p hc f", hc=32),
                        in_=src_w), 16)
                else:
                    src_w = t_w.rearrange("(oc p) (hg hq) -> hg p oc hq",
                                          p=128, hq=512)[blk]
                    dlx[par].inc(r.sync.dma_start(
                        out=stg.rearrange("p (oc hq) -> p oc hq", oc=8),
                        in_=src_w), 16)
                r.wait(r.scalar, dlx[par])
                if j16 > 0:
                    r.wait(r.scalar, ve, ve_q)
                sc_q[par] = sc.inc(r.scalar.mul(qf[:, :], stg[:, :],
                                                sw_bc[:, 0:1]))
                r.wait(r.vector, sc)
                if j16 > 0:
                    r.wait(r.vector, dw, dw_s)
                ve.inc(r.vector.tensor_scalar(qf[:, :], qf[:, :], -1.0, 1.0,
                                              ALU.max, ALU.min))
                ve.inc(r.vector.drain())
                ve_q = ve.inc(r.vector.tensor_scalar(
                    qb8[:, :], qf[:, :], MAGIC, -MAGIC, ALU.add, ALU.add))
                r.wait(r.tensor, ve, ve_q)
                for g in range(4):
                    if not (j16 == 0 and g == 0):
                        r.wait(r.tensor, ve, ve_g)
                    for k in range(8):
                        if which == "w1":
                            src_t = qb8[:, (g * 8 + k) * 128:
                                        (g * 8 + k + 1) * 128]
                        else:
                            src_t = qb8[:, k * 512 + g * 128:
                                        k * 512 + (g + 1) * 128]
                        tn.inc(r.tensor.transpose(
                            tpp[:, k * 128:(k + 1) * 128], src_t,
                            ident_bf[:, :]))
                    r.wait(r.vector, tn)
                    ve_g = ve.inc(r.vector.tensor_copy(
                        strip[:, g * 1024:(g + 1) * 1024], tpp[:, :]))
                r.wait(r.gpsimd, ve, ve_g)
                if which == "w1":
                    dw_s = dw.inc(r.gpsimd.dma_start(
                        out=t_w1t[:, blk].rearrange("hc p h -> p hc h"),
                        in_=strip.rearrange("p (hc h) -> p hc h", hc=32)), 16)
                else:
                    dw_s = dw.inc(r.gpsimd.dma_start(
                        out=t_w2t[blk * 4:(blk + 1) * 4].rearrange(
                            "hl p o -> p hl o"),
                        in_=strip.rearrange("p (hl o) -> p hl o", hl=4)), 16)
            r.wait(r.gpsimd, dw, dw.v)
            r.emit()
            barrier()

    r.emit()

    for s in range(nseq):
        es_s = contextlib.ExitStack()
        sbs = lambda n, shape, dt: es_s.enter_context(
            nc.sbuf_tensor(f"{n}_{s}", shape, dt))

        es_qkv = contextlib.ExitStack()
        Q_sb = es_qkv.enter_context(
            nc.sbuf_tensor(f"Q_{s}", [128, 8 * S], BF16))
        K_sb = es_qkv.enter_context(
            nc.sbuf_tensor(f"K_{s}", [128, 8 * S], BF16))
        V_sb = es_qkv.enter_context(
            nc.sbuf_tensor(f"V_{s}", [128, NT * NH * 65], BF16))

        es_a = contextlib.ExitStack()
        QWb = es_a.enter_context(nc.sbuf_tensor(f"a_qw{s}", [128, 8192], BF16))
        KWb = es_a.enter_context(nc.sbuf_tensor(f"a_kw{s}", [128, 8192], BF16))
        VWb = es_a.enter_context(nc.sbuf_tensor(f"a_vw{s}", [128, 8192], BF16))
        xhat = es_a.enter_context(nc.sbuf_tensor(f"a_xh{s}", [128, 8 * S], BF16))
        xt2 = es_a.enter_context(nc.sbuf_tensor(f"a_xt{s}", [128, 4 * D], F32))
        xh2 = es_a.enter_context(nc.sbuf_tensor(f"a_xm{s}", [128, 2 * D], BF16))
        st24 = es_a.enter_context(nc.sbuf_tensor(f"a_st{s}", [128, 32], F32))
        bnsa = es_a.enter_context(nc.sbuf_tensor(f"a_bn{s}", [128, 48], F32))
        mva = es_a.enter_context(nc.sbuf_tensor(f"a_mv{s}", [128, 8], F32))
        tpp = es_a.enter_context(nc.psum_tensor(f"a_tp{s}", [128, 2048], BF16))
        qkps = es_a.enter_context(nc.psum_tensor(f"a_qk{s}", [128, 1024], F32))
        vps = es_a.enter_context(nc.psum_tensor(f"a_vp{s}", [128, 2048], F32))
        if True:
            if True:
                for i4, Wb in ((0, QWb), (1, KWb), (2, VWb)):
                    dlx[i4 % 2].inc(r.sync.dma_start(
                        out=Wb[:, :], in_=t_wq4[i4]), 16)
                for i4 in range(3):
                    r.wait(r.vector, dlx[i4 % 2])
                ve.inc(r.vector.memset(
                    V_sb.rearrange("p (kt h e) -> p kt h e", kt=NT, h=NH)
                    [:, :, :, 64:65], 1.0))
                r.emit()
                barrier()

            ve_xh = [0] * NT
            ve_vv = [0] * NT
            ve_ap = [0] * NT
            tn_xh = [0] * NT
            sc_qk2 = [0, 0]
            var_sA = mva.rearrange("p (j two) -> p two j", two=2)[:, 1:2, :]
            for p in range(NP):
                for j in range(4):
                    u = 4 * p + j
                    if u >= 4:
                        r.wait(r.sync, ve, ve_ap[u - 4])
                    dlx[j].inc(r.sync.dma_start(
                        out=xt2[:, j * D:(j + 1) * D],
                        in_=t_x[s, u * 128:(u + 1) * 128, :]), 16)
                for j in range(4):
                    r.wait(r.vector, dlx[j])
                    ve.inc(r.vector.bn_stats(bnsa[:, j * 12:j * 12 + 6],
                                             xt2[:, j * D:j * D + 512]))
                    ve.inc(r.vector.bn_stats(bnsa[:, j * 12 + 6:j * 12 + 12],
                                             xt2[:, j * D + 512:(j + 1) * D]))
                ve.inc(r.vector.drain())
                for j in range(4):
                    ve.inc(r.vector.bn_aggr(mva[:, j * 2:(j + 1) * 2],
                                            bnsa[:, j * 12:j * 12 + 12]))
                ve.inc(r.vector.drain())
                ve.inc(r.vector.tensor_scalar_add(
                    st24.rearrange("p (a b) -> p a b", a=1)[:, :, 0:4],
                    var_sA, EPS))
                r.wait(r.scalar, ve)
                sc.inc(r.scalar.activation(st24[:, 4:8], st24[:, 0:4],
                                           AF.Sqrt))
                r.wait(r.vector, sc)
                ve.inc(r.vector.reciprocal(st24[:, 8:12], st24[:, 4:8]))
                ve.inc(r.vector.drain())
                ve.inc(r.vector.tensor_mul(st24[:, 12:16], st24[:, 8:12],
                                           st24[:, 8:12]))
                ve.inc(r.vector.drain())
                ve.inc(r.vector.tensor_mul(st24[:, 12:16], st24[:, 12:16],
                                           st24[:, 0:4]))
                ve.inc(r.vector.drain())
                ve.inc(r.vector.tensor_scalar(st24[:, 12:16], st24[:, 12:16],
                                              -0.5, 1.5, ALU.mult, ALU.add))
                ve.inc(r.vector.drain())
                ve.inc(r.vector.tensor_mul(st24[:, 16:20], st24[:, 8:12],
                                           st24[:, 12:16]))
                ve.inc(r.vector.drain())
                for j in range(4):
                    u = 4 * p + j
                    ub = u % 2
                    xt = xt2[:, j * D:(j + 1) * D]
                    xh = xh2[:, ub * D:(ub + 1) * D]
                    if u >= 2:
                        r.wait(r.vector, tn, tn_xh[u - 2])
                    ve_ap[u] = ve.inc(r.vector.tensor_scalar(
                        xh[:, :], xt[:, :], mva[:, 2 * j:2 * j + 1],
                        st24[:, 16 + j:17 + j], ALU.subtract, ALU.mult))
                    r.wait(r.tensor, ve, ve_ap[u])
                    if u >= 2:
                        r.wait(r.tensor, ve, ve_xh[u - 2])
                    tb = (u % 2) * 1024
                    for c in range(8):
                        tn.inc(r.tensor.transpose(
                            tpp[:, tb + c * 128:tb + (c + 1) * 128],
                            xh[:, c * 128:(c + 1) * 128], ident_bf[:, :]))
                    tn_xh[u] = tn.v
                    r.wait(r.vector, tn, tn_xh[u])
                    ve_xh[u] = ve.inc(r.vector.tensor_copy(
                        xhat.rearrange("p (c t) -> p c t", c=8)
                        [:, :, u * 128:(u + 1) * 128],
                        tpp[:, tb:tb + 1024].rearrange("p (c t) -> p c t",
                                                       c=8)))
                    r.wait(r.tensor, ve, ve_xh[u])
                    if u >= 2:
                        r.wait(r.tensor, ve, ve_vv[u - 2])
                    vpb = ub * 1024
                    for half in range(2):
                        for mc in range(8):
                            tn.inc(r.tensor.matmul(
                                vps[:, vpb + half * 512:vpb + (half + 1) * 512],
                                xhat[:, mc * S + u * 128:mc * S + (u + 1) * 128],
                                VWb[:, mc * 1024 + half * 512:
                                    mc * 1024 + (half + 1) * 512],
                                start=(mc == 0), stop=False))
                        tn.inc(r.tensor.matmul(
                            vps[:, vpb + half * 512:vpb + (half + 1) * 512],
                            ones1b[0:1, :],
                            vb_row[0:1, half * 512:(half + 1) * 512],
                            start=False, stop=True))
                    r.wait(r.vector, tn)
                    ve_vv[u] = ve.inc(r.vector.tensor_copy(
                        V_sb.rearrange("p (kt h e) -> p kt h e", kt=NT, h=NH)
                        [:, u, :, 0:64],
                        vps[:, vpb:vpb + 1024].rearrange("p (h d) -> p h d",
                                                         h=NH)))
                for W, bias, dstb in ((QWb, qb_t, Q_sb), (KWb, kb_t, K_sb)):
                    for oc in range(8):
                        off = (oc % 2) * 512
                        if sc_qk2[oc % 2]:
                            r.wait(r.tensor, sc, sc_qk2[oc % 2])
                        for mc in range(8):
                            tn.inc(r.tensor.matmul(
                                qkps[:, off:off + 512],
                                W[:, mc * 1024 + oc * 128:
                                  mc * 1024 + (oc + 1) * 128],
                                xhat[:, mc * S + p * 512:mc * S + (p + 1) * 512],
                                start=(mc == 0), stop=(mc == 7)))
                        r.wait(r.scalar, tn)
                        sc_qk2[oc % 2] = sc.inc(r.scalar.activation(
                            dstb[:, oc * S + p * 512:oc * S + (p + 1) * 512],
                            qkps[:, off:off + 512], AF.Identity,
                            bias=bias[:, oc:oc + 1]))
            r.emit()
            barrier()

        es_a.close()

        DBG.update({f"xhat{s}": xhat, f"Q{s}": Q_sb, f"K{s}": K_sb,
                    f"V{s}": V_sb, f"st24{s}": st24})
        x2 = es_s.enter_context(
            nc.sbuf_tensor(f"x2_{s}", [128, NT * D], BF16, side="right"))
        es_pw = contextlib.ExitStack()
        if s == 0:
            pstg = es_pw.enter_context(
                nc.sbuf_tensor("pw_stg", [128, 2 * 2048], F32, side="right"))
            pasum = es_pw.enter_context(
                nc.sbuf_tensor("pw_as", [128, 36], F32, side="right"))
            pw_flat = [t_w.rearrange("a b -> (a b)").rearrange(
                "(ci p a) -> ci p a", ci=16, p=128)
                for t_w in (t_w1, t_w2)]
            pw_abs = [0] * 32

            def stats_load(n):
                if n >= 32:
                    return
                par = n % 2
                if n >= 2:
                    r.wait(r.sync, ve, pw_abs[n - 2])
                dlx[2 + par].inc(r.sync.dma_start(
                    out=pstg[:, par * 2048:(par + 1) * 2048],
                    in_=pw_flat[n // 16][n % 16]), 16)

            def stats_step(n):
                par = n % 2
                r.wait(r.vector, dlx[2 + par])
                pw_abs[n] = ve.inc(r.vector.reduce_sum(
                    pasum[:, n:n + 1], pstg[:, par * 2048:(par + 1) * 2048],
                    axis=AX.X, apply_absolute_value=True))
                stats_load(n + 2)
        es_b = contextlib.ExitStack()
        OWb = es_b.enter_context(nc.sbuf_tensor(f"b_ow{s}", [128, 8192], BF16))
        Cbias = es_b.enter_context(nc.sbuf_tensor(f"b_cb{s}", [128, 8], F32))
        qsq = es_b.enter_context(nc.sbuf_tensor(f"b_sq{s}", [128, 512], BF16))
        smf = es_b.enter_context(nc.sbuf_tensor(f"b_sm{s}", [128, 16], F32))
        ptil = es_b.enter_context(nc.sbuf_tensor(f"b_pt{s}", [128, 2048], BF16))
        atok = es_b.enter_context(nc.sbuf_tensor(f"b_at{s}", [128, 4 * 1024], F32))
        attT = es_b.enter_context(nc.sbuf_tensor(f"b_tt{s}", [128, 1024], BF16))
        zrec = es_b.enter_context(nc.sbuf_tensor(f"b_zr{s}", [128, 8], F32))
        xre = es_b.enter_context(nc.sbuf_tensor(f"b_xr{s}", [128, 2 * D], F32))
        if True:
            with nc.psum_tensor(f"b_ut{s}", [128, 512], F32) as util:
                dlx[0].inc(r.sync.dma_start(out=OWb[:, :], in_=t_wq4[3]), 16)
                r.wait(r.vector, dlx[0])
                ve.inc(r.vector.engine_nop())

                ve_red = 0
                for ti, Xb in ((0, Q_sb), (1, K_sb)):
                    for p in range(NP):
                        for hp in range(8):
                            if not (ti == 0 and p == 0 and hp == 0):
                                r.wait(r.vector, tn, tn.v)
                            ve.inc(r.vector.tensor_mul(
                                qsq[:, :],
                                Xb[:, hp * S + p * 512:hp * S + (p + 1) * 512],
                                Xb[:, hp * S + p * 512:hp * S + (p + 1) * 512]))
                            r.wait(r.tensor, ve)
                            if hp == 0 and not (ti == 0 and p == 0):
                                r.wait(r.tensor, ve, ve_red)
                            tn.inc(r.tensor.matmul(
                                util[0:8, :], ones8s[:, hp * 8:(hp + 1) * 8],
                                qsq[:, :], start=(hp == 0), stop=(hp == 7)))
                        r.wait(r.vector, tn)
                        ve_red = ve.inc(r.vector.reduce_max(
                            smf[0:8, 8 + p:9 + p], util[0:8, :], axis=AX.X))
                    ve.inc(r.vector.drain())
                    ve_red = ve.inc(r.vector.reduce_max(
                        smf[0:8, ti:ti + 1], smf[0:8, 8:8 + NP], axis=AX.X))
                    ve.inc(r.vector.drain())
                ve.inc(r.vector.tensor_mul(smf[0:8, 2:3], smf[0:8, 0:1],
                                           smf[0:8, 1:2]))
                r.wait(r.scalar, ve)
                sc.inc(r.scalar.activation(smf[0:8, 3:4], smf[0:8, 2:3],
                                           AF.Sqrt))
                r.wait(r.vector, sc)
                ve.inc(r.vector.tensor_single_scalar(smf[0:8, 4:5],
                                                     smf[0:8, 3:4],
                                                     -1.02 * 0.125, ALU.mult))
                r.wait(r.tensor, ve)
                tn.inc(r.tensor.transpose(util[0:1, 0:8], smf[0:8, 4:5],
                                          ident_f32[0:8, 0:8]))
                r.wait(r.vector, tn)
                ve.inc(r.vector.tensor_copy(row1[:, 0:8], util[0:1, 0:8]))
                r.wait(r.tensor, ve)
                tn.inc(r.tensor.matmul(util[:, 0:8], ones1f[0:1, :],
                                       row1[0:1, 0:8], start=True, stop=True))
                r.wait(r.vector, tn)
                ve.inc(r.vector.tensor_copy(Cbias[:, :], util[:, 0:8]))
                r.emit()
                barrier()

            DBG.update({f"Cb{s}": Cbias, f"smf{s}": smf, f"pt{s}": ptil,
                        f"zr{s}": zrec, f"at{s}": atok})
            Lps = es_b.enter_context(
                nc.psum_tensor(f"b_lp{s}", [128, 2048], F32))
            AVps2 = es_b.enter_context(
                nc.psum_tensor(f"b_av{s}", [128, 2048], F32))
            sc_exp = [0, 0]
            ve_pn = 0
            ve_pn2 = [0, 0]
            ve_aT2 = 0
            ve_x2 = 0
            tn_tT = 0
            dl_x = [0] * NT
            for u in (0, 1):
                dl_x[u] = dlb[u % 2].inc(r.sync.dma_start(
                    out=xre[:, (u % 2) * D:(u % 2 + 1) * D],
                    in_=t_x[s, u * 128:(u + 1) * 128, :]), 16)
            if s == 0:
                stats_load(0)
                stats_load(1)
            for p in range(NP):
                kcmax = 4 * (p + 1)
                for hp in range(8):
                    tn_log = [0, 0]

                    def emit_logits(kc, hp=hp, p=p):
                        lb = kc % 2
                        Lp = Lps[:, lb * 1024:(lb + 1) * 1024]
                        r.wait(r.tensor, sc, sc_exp[lb])
                        if p > 0 and hp == 0 and kc < 2:
                            r.wait(r.tensor, ve, ve_x2)
                        tn.inc(r.tensor.matmul(
                            Lp[:, 0:512],
                            K_sb[0:64,
                                 hp * S + kc * 128:hp * S + (kc + 1) * 128],
                            Q_sb[0:64,
                                 hp * S + p * 512:hp * S + (p + 1) * 512],
                            start=True, stop=True, tile_position=(0, 0)))
                        tn.inc(r.tensor.matmul(
                            Lp[:, 512:1024],
                            K_sb[64:128,
                                 hp * S + kc * 128:hp * S + (kc + 1) * 128],
                            Q_sb[64:128,
                                 hp * S + p * 512:hp * S + (p + 1) * 512],
                            start=True, stop=True, tile_position=(64, 0)))
                        if kc >= 4 * p:
                            ob = (kc - 4 * p) * 128
                            for hh in range(2):
                                tn.inc(r.tensor.matmul(
                                    Lp[:, hh * 512 + ob:hh * 512 + ob + 128],
                                    trimask[:, :], ident_bf[:, :],
                                    start=False, stop=True,
                                    skip_group_check=True))
                        tn_log[lb] = tn.v

                    emit_logits(0)
                    for kc in range(kcmax):
                        lb = kc % 2
                        if kc + 1 < kcmax:
                            emit_logits(kc + 1)
                        Lp = Lps[:, lb * 1024:(lb + 1) * 1024]
                        pb = ptil[:, lb * 1024:(lb + 1) * 1024]
                        o = (kc - 4 * p) * 128
                        r.wait(r.scalar, tn, tn_log[lb])
                        if kc < 4 * p:
                            sc_exp[lb] = sc.inc(r.scalar.activation(
                                pb, Lp, AF.Exp, scale=0.125,
                                bias=Cbias[:, hp:hp + 1]))
                        else:
                            if o > 0:
                                r.wait(r.vector, sc, sc_exp[lb])
                                r.wait(r.vector, tn, tn_log[lb])
                                ve.inc(r.vector.memset(
                                    pb.rearrange("p (h q) -> p h q", h=2)
                                    [:, :, 0:o], 0.0))
                                r.wait(r.scalar, ve)
                            sc_exp[lb] = sc.inc(r.scalar.activation(
                                pb.rearrange("p (h q) -> p h q", h=2)
                                [:, :, o:512],
                                Lp.rearrange("p (h q) -> p h q", h=2)
                                [:, :, o:512],
                                AF.Exp, scale=0.125, bias=Cbias[:, hp:hp + 1]))
                        r.wait(r.tensor, sc, sc_exp[lb])
                        if kc == 0:
                            r.wait(r.tensor, ve, ve_pn2[hp % 2])
                            if hp < 2 and p > 0:
                                r.wait(r.tensor, ve, ve_aT2)
                        ab = (hp % 2) * 1024
                        for hh in range(2):
                            for qs in range(4):
                                tn.inc(r.tensor.matmul(
                                    AVps2[:, ab + hh * 512 + qs * 65:
                                          ab + hh * 512 + qs * 65 + 65],
                                    pb[:, hh * 512 + qs * 128:
                                       hh * 512 + (qs + 1) * 128],
                                    V_sb[:, (kc * NH + 2 * hp + hh) * 65:
                                         (kc * NH + 2 * hp + hh) * 65 + 65],
                                    start=(kc == 0 and qs == 0),
                                    stop=(kc == kcmax - 1),
                                    skip_group_check=True))
                    tn_av = tn.v
                    r.wait(r.vector, tn, tn_av)
                    if p > 0 and hp == 0:
                        r.wait(r.vector, tn, tn_tT)
                    ve.inc(r.vector.reciprocal(
                        zrec.rearrange("p (hh q o) -> p hh q o", hh=2, o=1),
                        bass.AP(AVps2[:, :].tensor, ab + 64,
                                [[2048, 128], [512, 2], [65, 4], [1, 1]])))
                    ve.inc(r.vector.drain())
                    for hh in range(2):
                        for qs in range(4):
                            ve_pn = ve.inc(r.vector.tensor_single_scalar(
                                atok[:, qs * 1024 + (2 * hp + hh) * 64:
                                     qs * 1024 + (2 * hp + hh + 1) * 64],
                                AVps2[:, ab + hh * 512 + qs * 65:
                                      ab + hh * 512 + qs * 65 + 64],
                                zrec[:, hh * 4 + qs:hh * 4 + qs + 1], ALU.mult))
                    ve_pn2[hp % 2] = ve_pn
                    if s == 0:
                        stats_step(p * 8 + hp)
                for v in range(4):
                    u = 4 * p + v
                    r.wait(r.tensor, ve, ve_pn)
                    r.wait(r.tensor, sc, sc.v)
                    if v == 0:
                        r.wait(r.tensor, ve, ve_x2)
                    for c in range(8):
                        tn.inc(r.tensor.transpose(
                            AVps2[:, c * 128:(c + 1) * 128],
                            atok[:, v * 1024 + c * 128:v * 1024 + (c + 1) * 128],
                            ident_f32[:, :]))
                    tn_tT = tn.v
                    r.wait(r.vector, tn, tn_tT)
                    ve_aT = ve.inc(r.vector.tensor_copy(attT[:, :],
                                                        AVps2[:, 0:1024]))
                    ve_aT2 = ve_aT
                    r.wait(r.tensor, ve, ve_aT)
                    for half in range(2):
                        hs = half * 512
                        for c in range(8):
                            tn.inc(r.tensor.matmul(
                                Lps[:, hs:hs + 512],
                                attT[:, c * 128:(c + 1) * 128],
                                OWb[:, c * 1024 + hs:c * 1024 + hs + 512],
                                start=(c == 0), stop=False))
                        tn.inc(r.tensor.matmul(
                            Lps[:, hs:hs + 512], ones1b[0:1, :],
                            ob_row[0:1, hs:hs + 512], start=False, stop=True))
                    tn_o = tn.v
                    r.wait(r.vector, tn, tn_o)
                    r.wait(r.vector, dlb[u % 2], dl_x[u])
                    ve_x2 = ve.inc(r.vector.tensor_add(
                        x2[:, u * D:(u + 1) * D],
                        xre[:, (u % 2) * D:(u % 2 + 1) * D], Lps[:, 0:1024]))
                    nu = u + 2
                    if nu < NT:
                        r.wait(r.sync, ve, ve_x2)
                        dl_x[nu] = dlb[nu % 2].inc(r.sync.dma_start(
                            out=xre[:, (nu % 2) * D:(nu % 2 + 1) * D],
                            in_=t_x[s, nu * 128:(nu + 1) * 128, :]), 16)
            if s == 0:
                for n in range(NP * 8, 32):
                    stats_step(n)
            r.emit()
            barrier()

        es_b.close()
        es_qkv.close()

        if s == 0:
            with nc.sbuf_tensor("pw_s1", [1, 8], F32) as s1s, \
                 nc.psum_tensor("pw_ps", [128, 512], F32) as ups:
                for wi, (sw_bc, mw_bc) in enumerate(
                        ((sw1_bc, mw1_bc), (sw2_bc, mw2_bc))):
                    nel = HID * D
                    ve.inc(r.vector.reduce_sum(
                        pasum[:, 32 + wi:33 + wi],
                        pasum[:, wi * 16:wi * 16 + 16], axis=AX.X))
                    ve.inc(r.vector.drain())
                    r.wait(r.tensor, ve, ve.v)
                    tn.inc(r.tensor.matmul(
                        ups[0:1, wi:wi + 1], pasum[:, 32 + wi:33 + wi],
                        onescol[:, :], start=True, stop=True))
                    r.wait(r.vector, tn, tn.v)
                    ve.inc(r.vector.tensor_scalar(
                        s1s[:, 4 * wi:4 * wi + 1], ups[0:1, wi:wi + 1],
                        1.0 / nel, 1e-5, ALU.mult, ALU.max))
                    ve.inc(r.vector.drain())
                    ve.inc(r.vector.reciprocal(s1s[:, 4 * wi + 1:4 * wi + 2],
                                               s1s[:, 4 * wi:4 * wi + 1]))
                    ve.inc(r.vector.drain())
                    r.wait(r.tensor, ve, ve.v)
                    tn.inc(r.tensor.matmul(
                        ups[:, 0:1], ones1f[0:1, :],
                        s1s[0:1, 4 * wi:4 * wi + 1], start=True, stop=True))
                    tn.inc(r.tensor.matmul(
                        ups[:, 1:2], ones1f[0:1, :],
                        s1s[0:1, 4 * wi + 1:4 * wi + 2],
                        start=True, stop=True))
                    r.wait(r.vector, tn, tn.v)
                    ve.inc(r.vector.tensor_copy(mw_bc[:, :], ups[:, 0:1]))
                    ve.inc(r.vector.tensor_copy(sw_bc[:, :], ups[:, 1:2]))
                    ve.inc(r.vector.drain())
                r.emit()
                barrier()
            wprep_quant_all()
            r.emit()
        es_pw.close()

        es_c = contextlib.ExitStack()
        xq2 = es_c.enter_context(nc.sbuf_tensor(f"c_xq{s}", [128, 8 * SP2], BF16))
        g1col = es_c.enter_context(nc.sbuf_tensor(f"c_gc{s}", [128, NT], F32))
        cst = es_c.enter_context(nc.sbuf_tensor(f"c_st{s}", [128, 64], F32))
        bnsc = es_c.enter_context(nc.sbuf_tensor(f"c_bn{s}", [128, 48], F32))
        mvc = es_c.enter_context(nc.sbuf_tensor(f"c_mv{s}", [128, 8], F32))
        if True:
            es_cp = contextlib.ExitStack()
            tppC = es_cp.enter_context(
                nc.psum_tensor(f"c_tp{s}", [128, 1024], BF16))
            cxt = es_cp.enter_context(
                nc.sbuf_tensor(f"c_xv{s}", [128, 4 * D], F32, side="right"))
            cxm = es_cp.enter_context(
                nc.sbuf_tensor(f"c_xm{s}", [128, 4 * D], BF16, side="right"))
            cpb = es_cp.enter_context(
                nc.sbuf_tensor(f"c_pb{s}", [128, 192], F32, side="right"))
            mv16 = es_cp.enter_context(
                nc.sbuf_tensor(f"c_mw{s}", [128, 32], F32, side="right"))
            cs2 = es_cp.enter_context(
                nc.sbuf_tensor(f"c_cq{s}", [128, 128], F32, side="right"))
            ve.inc(r.vector.memset(
                xq2.rearrange("p (c t) -> p c t", c=8)[:, :, 0:1], 0.0))
            ve.inc(r.vector.memset(
                xq2.rearrange("p (c t) -> p c t", c=8)[:, :, S + 1:S + 2], 0.0))
            tn_pq = [0] * NT
            ve_pq = [0] * NT
            sc_r2 = [0] * NT
            sc_r1 = [0] * NT
            for u in range(NT):
                ve.inc(r.vector.bn_stats(cpb[:, u * 12:u * 12 + 6],
                                         x2[:, u * D:u * D + 512]))
                ve.inc(r.vector.bn_stats(cpb[:, u * 12 + 6:u * 12 + 12],
                                         x2[:, u * D + 512:(u + 1) * D]))
            ve.inc(r.vector.drain())
            for u in range(NT):
                ve.inc(r.vector.bn_aggr(mv16[:, u * 2:(u + 1) * 2],
                                        cpb[:, u * 12:u * 12 + 12]))
            ve.inc(r.vector.drain())
            var16 = mv16.rearrange("p (j two) -> p two j", two=2)[:, 1:2, :]
            ve.inc(r.vector.tensor_scalar_add(
                cs2.rearrange("p (a b) -> p a b", a=1)[:, :, 0:16], var16,
                EPS))
            r.wait(r.scalar, ve)
            sc.inc(r.scalar.activation(cs2[:, 16:32], cs2[:, 0:16], AF.Sqrt))
            r.wait(r.vector, sc)
            ve.inc(r.vector.reciprocal(cs2[:, 32:48], cs2[:, 16:32]))
            ve.inc(r.vector.drain())
            ve.inc(r.vector.tensor_mul(cs2[:, 48:64], cs2[:, 32:48],
                                       cs2[:, 32:48]))
            ve.inc(r.vector.drain())
            ve.inc(r.vector.tensor_mul(cs2[:, 48:64], cs2[:, 48:64],
                                       cs2[:, 0:16]))
            ve.inc(r.vector.drain())
            ve.inc(r.vector.tensor_scalar(cs2[:, 48:64], cs2[:, 48:64],
                                          -0.5, 1.5, ALU.mult, ALU.add))
            ve.inc(r.vector.drain())
            ve.inc(r.vector.tensor_mul(cs2[:, 64:80], cs2[:, 32:48],
                                       cs2[:, 48:64]))
            ve.inc(r.vector.drain())
            for tq in range(NT // 4):
                q0 = 4 * tq
                if tq >= 1:
                    r.wait(r.vector, tn, tn_pq[q0 - 1])
                    r.wait(r.vector, sc, sc_r1[q0 - 1])
                for j in range(4):
                    u = q0 + j
                    ve.inc(r.vector.tensor_scalar(
                        cxt[:, j * D:(j + 1) * D], x2[:, u * D:(u + 1) * D],
                        mv16[:, 2 * u:2 * u + 1], cs2[:, 64 + u:65 + u],
                        ALU.subtract, ALU.mult))
                ve.inc(r.vector.drain())
                for j in range(4):
                    u = q0 + j
                    ve.inc(r.vector.reduce_max(cs2[:, 80 + u:81 + u],
                                               cxt[:, j * D:(j + 1) * D],
                                               axis=AX.X,
                                               apply_absolute_value=True))
                ve.inc(r.vector.drain())
                ve.inc(r.vector.tensor_scalar_max(
                    cs2[:, 96 + q0:100 + q0], cs2[:, 80 + q0:84 + q0], 1e-5))
                ve.inc(r.vector.drain())
                ve.inc(r.vector.reciprocal(cs2[:, 112 + q0:116 + q0],
                                           cs2[:, 96 + q0:100 + q0]))
                ve.inc(r.vector.drain())
                ve.inc(r.vector.tensor_single_scalar(
                    cs2[:, 112 + q0:116 + q0], cs2[:, 112 + q0:116 + q0],
                    127.0, ALU.mult))
                ve.inc(r.vector.tensor_scalar(
                    g1col[:, q0:q0 + 4], cs2[:, 96 + q0:100 + q0],
                    mw1_bc[:, 0:1], 1.0 / 127.0, ALU.mult, ALU.mult))
                ve_q = ve.inc(r.vector.drain())
                r.wait(r.scalar, ve, ve_q)
                if tq >= 1:
                    r.wait(r.scalar, tn, tn_pq[q0 - 1])
                for pj in range(2):
                    for j in (2 * pj, 2 * pj + 1):
                        u = q0 + j
                        sc_r1[u] = sc.inc(r.scalar.activation(
                            cxt[:, j * D:(j + 1) * D],
                            cxt[:, j * D:(j + 1) * D], AF.Identity,
                            scale=cs2[:, 112 + u:113 + u],
                            bias=magic_bc[:, 0:1]))
                    sc.inc(r.scalar.drain())
                    for j in (2 * pj, 2 * pj + 1):
                        u = q0 + j
                        sc_r2[u] = sc.inc(r.scalar.activation(
                            cxm[:, j * D:(j + 1) * D],
                            cxt[:, j * D:(j + 1) * D], AF.Identity,
                            bias=nmagic_bc[:, 0:1]))
                for j in range(4):
                    u = q0 + j
                    r.wait(r.tensor, sc, sc_r2[u])
                    if not (tq == 0 and j == 0):
                        r.wait(r.tensor, ve, ve_pq[u - 1])
                    for c in range(8):
                        tn.inc(r.tensor.transpose(
                            tppC[:, c * 128:(c + 1) * 128],
                            cxm[:, j * D + c * 128:j * D + (c + 1) * 128],
                            ident_bf[:, :]))
                    tn_pq[u] = tn.v
                    r.wait(r.vector, tn, tn_pq[u])
                    ve_pq[u] = ve.inc(r.vector.tensor_copy(
                        xq2.rearrange("p (c t) -> p c t", c=8)
                        [:, :, 1 + u * 128:1 + (u + 1) * 128],
                        tppC.rearrange("p (c t) -> p c t", c=8)))
            r.emit()
            barrier()
            es_cp.close()

            hps = es_c.enter_context(nc.psum_tensor(f"c_hp{s}",
                                                    [128, 2048], F32))
            ops = es_c.enter_context(nc.psum_tensor(f"c_op{s}",
                                                    [128, 2048], F32))
            Hbq = es_c.enter_context(
                nc.sbuf_tensor(f"c_H{s}", [128, 32 * 514], BF16))
            Hb2 = es_c.enter_context(
                nc.sbuf_tensor(f"c_H2{s}", [128, 32 * 512], BF16))
            G1sb = es_c.enter_context(
                nc.sbuf_tensor(f"c_g1b{s}", [128, 514], F32))
            w1b = es_c.enter_context(
                nc.sbuf_tensor(f"c_w1b{s}", [128, 3 * 2048], BF16))
            w2b = es_c.enter_context(
                nc.sbuf_tensor(f"c_w2b{s}", [128, 2 * 2048], BF16))
            dgb = es_c.enter_context(
                nc.sbuf_tensor(f"c_dg{s}", [128, 3 * 384], BF16))
            ct1 = es_c.enter_context(
                nc.sbuf_tensor(f"c_t1{s}", [128, 2 * 512], F32))
            ct2 = es_c.enter_context(
                nc.sbuf_tensor(f"c_t2{s}", [128, 512], F32))
            mxt = es_c.enter_context(
                nc.sbuf_tensor(f"c_mx{s}", [128, 2 * 512], BF16))
            mxn = es_c.enter_context(
                nc.sbuf_tensor(f"c_mn{s}", [128, 2 * 512], BF16))
            hqg = es_c.enter_context(
                nc.sbuf_tensor(f"c_hq{s}", [128, 32 * 512], BF16))
            S2sb = es_c.enter_context(
                nc.sbuf_tensor(f"c_ss{s}", [128, 512], F32))
            g2buf = es_c.enter_context(
                nc.sbuf_tensor(f"c_g2{s}", [128, 8], F32))
            obuf = es_c.enter_context(
                nc.sbuf_tensor(f"c_obf{s}", [128, D], F32, side="right"))
            ve_h = [0] * 32
            sc_sil = [0] * 32
            ve_hq = [0] * 32
            tn_cv = [0] * 32
            ve_dgb = [0] * 32
            slot_free = [0, 0, 0]
            ve_mx = 0
            ve_ev = 0
            ve_g1 = 0
            gp_ob = 0
            dl_w1 = [0] * 16
            st_ch = {}
            w2_done = {}
            g2s = {}
            w2b_cons = [0, 0]

            def emit_w2_sub(t, half, glob):
                vp = 0 if half == 0 else 2
                wb2 = glob % 2
                g2c = 2 * glob
                r.wait(r.sync, tn, w2b_cons[wb2])
                dlw = dw2[wb2].inc(r.sync.dma_start(
                    out=w2b[:, wb2 * 2048:(wb2 + 1) * 2048].rearrange(
                        "p (l m) -> p l m", l=2),
                    in_=t_w2t[g2c:g2c + 2].rearrange("l p m -> p l m")), 16)
                if glob == 0:
                    r.wait(r.tensor, ve, ve_ev)
                r.wait(r.tensor, ve, ve_hq[2 * glob + 1])
                r.wait(r.tensor, dw2[wb2], dlw)
                for l in range(2):
                    hl2 = g2c + l
                    mv = w2b[:, (wb2 * 2 + l) * 1024:
                             (wb2 * 2 + l + 1) * 1024]
                    first = (glob == 0 and l == 0)
                    last = (glob == 15 and l == 1)
                    for v in (vp, vp + 1):
                        vv = v - vp
                        tn.inc(r.tensor.matmul(
                            ops[:, (2 * vv) * 512:(2 * vv + 1) * 512],
                            hqg[:, hl2 * 512 + v * 128:
                                hl2 * 512 + (v + 1) * 128],
                            mv[:, 0:512], start=first, stop=last,
                            skip_group_check=True))
                        tn.inc(r.tensor.matmul(
                            ops[:, (2 * vv + 1) * 512:(2 * vv + 2) * 512],
                            hqg[:, hl2 * 512 + v * 128:
                                hl2 * 512 + (v + 1) * 128],
                            mv[:, 512:1024], start=first, stop=last,
                            skip_group_check=True))
                w2b_cons[wb2] = tn.v

            def emit_final_pair(t, vp):
                nonlocal ve_ev, gp_ob
                tn_w2 = tn.v
                for vv in range(2):
                    v = vp + vv
                    u = 4 * t + v
                    r.wait(r.vector, tn, tn_w2)
                    r.wait(r.vector, dst, gp_ob)
                    ve.inc(r.vector.scalar_tensor_tensor(
                        obuf[:, 0:512], ops[:, (2 * vv) * 512:
                                            (2 * vv + 1) * 512],
                        g2s[t][:, v:v + 1],
                        x2[:, u * D:u * D + 512], ALU.mult, ALU.add))
                    ve_evl = ve.inc(r.vector.scalar_tensor_tensor(
                        obuf[:, 512:1024], ops[:, (2 * vv + 1) * 512:
                                               (2 * vv + 2) * 512],
                        g2s[t][:, v:v + 1],
                        x2[:, u * D + 512:(u + 1) * D], ALU.mult, ALU.add))
                    ve.inc(r.vector.drain())
                    r.wait(r.gpsimd, ve, ve_evl)
                    gp_ob = dst.inc(r.gpsimd.dma_start(
                        out=t_out[s, u * 128:(u + 1) * 128, :],
                        in_=obuf[:, :]), 16)
                    ve_ev = ve_evl

            def emit_conv_silu(t, hc):
                par = hc % 2
                dbase = (hc % 3) * 384
                r.wait(r.tensor, ve, ve_h[hc])
                r.wait(r.tensor, ve, ve_dgb[hc])
                for j in range(3):
                    tn.inc(r.tensor.matmul(
                        hps[:, par * 1024:par * 1024 + 512],
                        dgb[:, dbase + j * 128:dbase + (j + 1) * 128],
                        Hbq[:, hc * 514 + j:hc * 514 + j + 512],
                        start=(j == 0), stop=(j == 2)))
                tn_cv[hc] = tn.v
                r.wait(r.scalar, tn, tn_cv[hc])
                r.wait(r.scalar, ve, ve_hq[hc])
                sc_sil[hc] = sc.inc(r.scalar.activation(
                    Hb2[:, hc * 512:(hc + 1) * 512],
                    hps[:, par * 1024:par * 1024 + 512],
                    AF.Silu, bias=cb_t[:, hc:hc + 1]))

            def emit_mx(gq):
                r.wait(r.vector, sc, sc_sil[4 * gq + 3])
                for j in range(4):
                    c = 4 * gq + j
                    ja = (j % 2) * 512
                    hsl = Hb2[:, c * 512:(c + 1) * 512]
                    if j == 2:
                        ve.inc(r.vector.drain())
                    if gq == 0 and j < 2:
                        ve.inc(r.vector.tensor_copy(mxt[:, ja:ja + 512], hsl))
                        ve.inc(r.vector.tensor_copy(mxn[:, ja:ja + 512], hsl))
                    else:
                        ve.inc(r.vector.tensor_tensor(
                            mxt[:, ja:ja + 512], mxt[:, ja:ja + 512],
                            hsl, ALU.max))
                        ve.inc(r.vector.tensor_tensor(
                            mxn[:, ja:ja + 512], mxn[:, ja:ja + 512],
                            hsl, ALU.min))
                ve.inc(r.vector.drain())

            def emit_hq_batch(t, c0):
                r.wait(r.vector, sc, sc_sil[c0 + 3])
                for pair in range(2):
                    for k in range(2):
                        c = c0 + 2 * pair + k
                        ve.inc(r.vector.tensor_mul(
                            ct1[:, k * 512:(k + 1) * 512],
                            Hb2[:, c * 512:(c + 1) * 512], S2sb[:, :]))
                    ve.inc(r.vector.drain())
                    for k in range(2):
                        c = c0 + 2 * pair + k
                        ve_hq[c] = ve.inc(r.vector.tensor_scalar(
                            hqg[:, c * 512:(c + 1) * 512],
                            ct1[:, k * 512:(k + 1) * 512],
                            MAGIC, -MAGIC, ALU.add, ALU.add))
                    ve.inc(r.vector.drain())

            def emit_g1(t):
                nonlocal ve_g1
                if t > 0:
                    r.wait(r.tensor, ve, st_ch[t - 1])
                for b in range(4):
                    tn.inc(r.tensor.transpose(
                        hps[0:1, 1024 + b * 128:1024 + (b + 1) * 128],
                        g1col[:, 4 * t + b:4 * t + b + 1], ident_f32[:, :]))
                if t + 1 < NCT:
                    tn.inc(r.tensor.transpose(
                        hps[0:1, 1536:1664],
                        g1col[:, 4 * t + 4:4 * t + 5], ident_f32[:, :]))
                tn_g = tn.v
                r.wait(r.vector, tn, tn_g)
                if t == 0:
                    ve.inc(r.vector.memset(G1sb[0:1, 0:1], 0.0))
                else:
                    ve.inc(r.vector.tensor_copy(G1sb[0:1, 0:1],
                                                G1sb[0:1, 512:513]))
                ve.inc(r.vector.drain())
                ve.inc(r.vector.tensor_copy(G1sb[0:1, 1:513],
                                            hps[0:1, 1024:1536]))
                if t + 1 < NCT:
                    ve.inc(r.vector.tensor_copy(G1sb[0:1, 513:514],
                                                hps[0:1, 1536:1537]))
                else:
                    ve.inc(r.vector.memset(G1sb[0:1, 513:514], 0.0))
                ve_row = ve.inc(r.vector.drain())
                r.wait(r.tensor, ve, ve_row)
                tn.inc(r.tensor.matmul(hps[:, 1024:1536], ones1f[0:1, :],
                                       G1sb[0:1, 0:512], start=True,
                                       stop=True))
                tn.inc(r.tensor.matmul(hps[:, 1536:1538], ones1f[0:1, :],
                                       G1sb[0:1, 512:514], start=True,
                                       stop=True))
                r.wait(r.vector, tn, tn.v)
                ve.inc(r.vector.tensor_copy(G1sb[:, :], hps[:, 1024:1538]))
                ve_g1 = ve.inc(r.vector.drain())

            def emit_g1_w1_conv(t, w2t=None):
                nonlocal ve_mx

                def w1_load(k):
                    sl = k % 3
                    r.wait(r.sync, tn, slot_free[sl])
                    dl_w1[k] = dw1[sl].inc(r.sync.dma_start(
                        out=w1b[:, sl * 2048:(sl + 1) * 2048].rearrange(
                            "p (l f h) -> p l f h", l=2, f=8),
                        in_=t_w1t[2 * k:2 * k + 2].rearrange(
                            "l f p h -> p l f h")), 16)

                def dg_build(hc):
                    r.wait(r.vector, tn, tn_cv[(hc - 3) % 32])
                    for j in range(3):
                        ve_dgb[hc] = ve.inc(r.vector.tensor_scalar(
                            dgb[:, (hc % 3) * 384 + j * 128:
                                (hc % 3) * 384 + (j + 1) * 128],
                            ident_f32[:, :],
                            cw_t[:, j * 32 + hc:j * 32 + hc + 1], 0.0,
                            ALU.mult, ALU.add))

                def w1_chunk(hc, sl):
                    par = hc % 2
                    if t > 0 or hc >= 2:
                        r.wait(r.tensor, sc, sc_sil[(hc - 2) % 32])
                    if t > 0 and hc == 1:
                        r.wait(r.tensor, ve, ve_g1)
                    lbase = sl * 2048 + (hc % 2) * 1024
                    for fc in range(8):
                        wsl = w1b[:, lbase + fc * 128:lbase + (fc + 1) * 128]
                        tn.inc(r.tensor.matmul(
                            hps[:, par * 1024:par * 1024 + 257], wsl,
                            xq2[:, fc * SP2 + t * 512:
                                fc * SP2 + t * 512 + 257],
                            start=(fc == 0), stop=(fc == 7)))
                        tn.inc(r.tensor.matmul(
                            hps[:, par * 1024 + 512:par * 1024 + 769], wsl,
                            xq2[:, fc * SP2 + t * 512 + 257:
                                fc * SP2 + t * 512 + 514],
                            start=(fc == 0), stop=(fc == 7)))
                    tn_h = tn.v
                    r.wait(r.vector, tn, tn_h)
                    ve_h[hc] = ve.inc(r.vector.tensor_mul(
                        Hbq[:, hc * 514:hc * 514 + 514].rearrange(
                            "p (b c) -> p b c", b=2),
                        bass.AP(hps[:, :].tensor, par * 1024,
                                [[2048, 128], [512, 2], [1, 257]]),
                        G1sb.rearrange("p (b c) -> p b c", b=2)))

                dg_build(0)
                w1_load(0)
                w1_load(1)
                for g in range(16):
                    sl = g % 3
                    if g + 2 <= 15:
                        w1_load(g + 2)
                    dg_build(2 * g + 1)
                    if g == 0:
                        dg_build(2)
                    r.wait(r.tensor, dw1[sl], dl_w1[g])
                    if g == 0:
                        r.wait(r.tensor, ve, ve_g1)
                    w1_chunk(2 * g, sl)
                    if g > 0:
                        emit_conv_silu(t, 2 * g - 1)
                        if g < 15:
                            dg_build(2 * g + 2)
                    if w2t is not None:
                        emit_w2_sub(w2t, g // 8, 2 * (g % 8))
                        if g <= 6:
                            emit_hq_batch(w2t, 4 * (g + 1))
                    w1_chunk(2 * g + 1, sl)
                    slot_free[sl] = tn.v
                    emit_conv_silu(t, 2 * g)
                    if w2t is not None:
                        emit_w2_sub(w2t, g // 8, 2 * (g % 8) + 1)
                        if g == 7:
                            emit_final_pair(w2t, 0)
                        if g == 15:
                            emit_final_pair(w2t, 2)
                            w2_done[w2t] = tn.v
                    if g % 2 == 0 and g >= 2:
                        emit_mx(g // 2 - 1)
                emit_conv_silu(t, 31)
                emit_mx(6)
                emit_mx(7)
                ve.inc(r.vector.tensor_tensor(mxt[:, 0:512], mxt[:, 0:512],
                                              mxt[:, 512:1024], ALU.max))
                ve.inc(r.vector.tensor_tensor(mxn[:, 0:512], mxn[:, 0:512],
                                              mxn[:, 512:1024], ALU.min))
                ve.inc(r.vector.drain())
                ve.inc(r.vector.tensor_single_scalar(mxn[:, 0:512],
                                                     mxn[:, 0:512],
                                                     -1.0, ALU.mult))
                ve.inc(r.vector.drain())
                ve.inc(r.vector.tensor_tensor(
                    mxt[:, 0:512], mxt[:, 0:512], mxn[:, 0:512], ALU.max))
                ve.inc(r.vector.drain())
                ve_mx = ve.inc(r.vector.tensor_copy(ct2[:, 0:512],
                                                    mxt[:, 0:512]))
                ve.inc(r.vector.drain())

            def emit_chain(t):
                r.wait(r.tensor, ve, ve_mx)
                for b in range(4):
                    tn.inc(r.tensor.transpose(
                        hps[:, 1024 + b * 128:1024 + (b + 1) * 128],
                        ct2[:, b * 128:(b + 1) * 128], ident_f32[:, :]))
                tn_mx = tn.v
                r.wait(r.vector, tn, tn_mx)
                for b in range(4):
                    ve.inc(r.vector.reduce_max(
                        cst[:, 32 + b:33 + b],
                        hps[:, 1024 + b * 128:1024 + (b + 1) * 128],
                        axis=AX.X))
                ve.inc(r.vector.drain())
                ve.inc(r.vector.tensor_scalar_max(cst[:, 36:40], cst[:, 32:36],
                                                  1e-5))
                ve.inc(r.vector.drain())
                ve.inc(r.vector.reciprocal(cst[:, 40:44], cst[:, 36:40]))
                ve.inc(r.vector.drain())
                ve.inc(r.vector.tensor_single_scalar(cst[:, 44:48],
                                                     cst[:, 40:44],
                                                     127.0, ALU.mult))
                ve_g2 = ve.inc(r.vector.tensor_scalar(
                    g2buf[:, (t % 2) * 4:(t % 2) * 4 + 4], cst[:, 36:40],
                    mw2_bc[:, 0:1], 1.0 / 127.0, ALU.mult, ALU.mult))
                g2s[t] = g2buf[:, (t % 2) * 4:(t % 2) * 4 + 4]
                r.wait(r.tensor, ve, ve_g2)
                for b in range(4):
                    tn.inc(r.tensor.transpose(
                        hps[0:1, 1024 + b * 128:1024 + (b + 1) * 128],
                        cst[:, 44 + b:45 + b], ident_f32[:, :]))
                r.wait(r.vector, tn, tn.v)
                ve_s2 = ve.inc(r.vector.tensor_copy(S2sb[0:1, 0:512],
                                                    hps[0:1, 1024:1536]))
                r.wait(r.tensor, ve, ve_s2)
                tn_s2 = tn.inc(r.tensor.matmul(hps[:, 1024:1536],
                                               ones1f[0:1, :],
                                               S2sb[0:1, 0:512],
                                               start=True, stop=True))
                r.wait(r.vector, tn, tn_s2)
                ve.inc(r.vector.tensor_copy(S2sb[:, :], hps[:, 1024:1536]))
                ve.inc(r.vector.drain())
                st_ch[t] = ve.v

            emit_g1(0)
            for t in range(NCT):
                emit_g1_w1_conv(t, w2t=(t - 1 if t > 0 else None))
                emit_chain(t)
                if t + 1 < NCT:
                    emit_g1(t + 1)
                if t > 0:
                    r.wait(r.vector, tn, w2_done[t - 1])
                emit_hq_batch(t, 0)
            tl = NCT - 1
            for g in range(16):
                emit_w2_sub(tl, g // 8, 2 * (g % 8))
                emit_w2_sub(tl, g // 8, 2 * (g % 8) + 1)
                if g == 7:
                    emit_final_pair(tl, 0)
                if g == 15:
                    emit_final_pair(tl, 2)
                    w2_done[tl] = tn.v
                if g <= 6:
                    emit_hq_batch(tl, 4 * (g + 1))
            r.wait(r.gpsimd, dst, dst.v)
            r.emit()
            barrier()
        es_c.close()
        es_s.close()
        r.emit()

    r.wait(r.gpsimd, dst, dst.v)
    gp.inc(r.gpsimd.engine_nop())
    r.emit()
    es.close()
    nc.finalize()
    return nc


_CACHE = {}


def _get_nc(nseq, S):
    key = (nseq, S)
    if key not in _CACHE:
        _CACHE[key] = build_program(nseq, S)
    return _CACHE[key]


def kernel(L0_attn_x, L0_attn_Q_w, L0_attn_Q_b, L0_attn_K_w, L0_attn_K_b,
           L0_attn_V_w, L0_attn_V_b, L0_attn_O_w, L0_attn_O_b,
           L0_ffn_w1, L0_ffn_conv_w, L0_ffn_conv_b, L0_ffn_w2):
    import ml_dtypes
    x = np.ascontiguousarray(np.asarray(L0_attn_x, dtype=np.float32))
    B, S, _ = x.shape
    ncores = 8
    nseq = B // ncores
    nc = _get_nc(nseq, S)

    tri = np.triu(np.ones((128, 128))).astype(ml_dtypes.bfloat16)
    o8 = np.zeros((8, 128, 8), dtype=ml_dtypes.bfloat16)
    for hp in range(8):
        o8[hp, :, hp] = 1.0
    asf = lambda a: np.ascontiguousarray(np.asarray(a, dtype=np.float32))
    common = {
        "qw": asf(L0_attn_Q_w), "kw": asf(L0_attn_K_w), "vw": asf(L0_attn_V_w),
        "ow": asf(L0_attn_O_w), "qbias": asf(L0_attn_Q_b),
        "kbias": asf(L0_attn_K_b), "vbias": asf(L0_attn_V_b),
        "obias": asf(L0_attn_O_b), "w1": asf(L0_ffn_w1),
        "cw": asf(L0_ffn_conv_w), "cb": asf(L0_ffn_conv_b),
        "w2": asf(L0_ffn_w2),
        "ident_bf": np.eye(128, dtype=ml_dtypes.bfloat16),
        "ident_f32": np.eye(128, dtype=np.float32),
        "trimask": tri,
        "ones8s": o8,
    }
    in_maps = []
    for i in range(ncores):
        m = dict(common)
        m["x"] = np.ascontiguousarray(x[i * nseq:(i + 1) * nseq])
        in_maps.append(m)
    res = run_bass_kernel_spmd(nc, in_maps, core_ids=list(range(ncores)))
    out = np.concatenate([np.asarray(res.results[i]["out"])
                          for i in range(ncores)], axis=0)
    return out.astype(np.float32)


# version 26 (final)
# speedup vs baseline: 1.0620x; 1.0002x over previous
import contextlib
import numpy as np

import concourse.bass as bass
import concourse.mybir as mybir
from concourse.bass_utils import run_bass_kernel_spmd

F32 = mybir.dt.float32
BF16 = mybir.dt.bfloat16
AF = mybir.ActivationFunctionType
ALU = mybir.AluOpType
AX = mybir.AxisListType

MAGIC = float(np.float32(3 * 2.0 ** 22))
D = 1024
NH = 16
DH = 64
HID = 4096
EPS = 1e-5

ENGS = ("sync", "tensor", "scalar", "vector", "gpsimd")
DBG = {}


class Ctr:
    def __init__(self, nc, es, name):
        self.sem = es.enter_context(nc.semaphore(name))
        self.v = 0

    def inc(self, h, k=1):
        h.then_inc(self.sem, k)
        self.v += k
        return self.v


class Handle:
    def __init__(self, entry):
        self.entry = entry

    def then_inc(self, sem, k):
        self.entry["incs"].append((sem, k))
        return self


class EngRec:
    def __init__(self, ops):
        self._ops = ops

    def __getattr__(self, name):
        ops = self._ops

        def call(*args, **kwargs):
            entry = {"name": name, "args": args, "kwargs": kwargs, "incs": []}
            ops.append(entry)
            return Handle(entry)

        return call


class Rec:
    def __init__(self, nc):
        self.nc = nc
        self._reset()

    def _reset(self):
        self.ops = {e: [] for e in ENGS}
        for e in ENGS:
            setattr(self, e, EngRec(self.ops[e]))

    def wait(self, engrec, ctr, v=None):
        if v is None:
            v = ctr.v
        assert v <= ctr.v, "wait on future value -> deadlock"
        if v > 0:
            engrec.wait_ge(ctr.sem, v)

    def emit(self):
        def replay(ops, eng):
            for en in ops:
                h = getattr(eng, en["name"])(*en["args"], **en["kwargs"])
                for sem, k in en["incs"]:
                    h.then_inc(sem, k)

        with self.nc.Block() as b:
            for e in ENGS:
                if self.ops[e]:
                    getattr(b, e)(
                        (lambda ops: (lambda eng: replay(ops, eng)))(self.ops[e]))
        self._reset()


def build_program(nseq, S):
    NP = S // 512
    NT = S // 128
    NCT = S // 512
    SP2 = S + 2

    nc = bass.Bass("TRN2", target_bir_lowering=False, debug=False)

    t_x = nc.dram_tensor("x", [nseq, S, D], F32, kind="ExternalInput").ap()
    t_qw = nc.dram_tensor("qw", [NH, D, DH], F32, kind="ExternalInput").ap()
    t_kw = nc.dram_tensor("kw", [NH, D, DH], F32, kind="ExternalInput").ap()
    t_vw = nc.dram_tensor("vw", [NH, D, DH], F32, kind="ExternalInput").ap()
    t_ow = nc.dram_tensor("ow", [NH, DH, D], F32, kind="ExternalInput").ap()
    t_qb = nc.dram_tensor("qbias", [NH, DH], F32, kind="ExternalInput").ap()
    t_kb = nc.dram_tensor("kbias", [NH, DH], F32, kind="ExternalInput").ap()
    t_vb = nc.dram_tensor("vbias", [NH, DH], F32, kind="ExternalInput").ap()
    t_ob = nc.dram_tensor("obias", [D], F32, kind="ExternalInput").ap()
    t_w1 = nc.dram_tensor("w1", [HID, D], F32, kind="ExternalInput").ap()
    t_cw = nc.dram_tensor("cw", [HID, 1, 3], F32, kind="ExternalInput").ap()
    t_cb = nc.dram_tensor("cb", [HID], F32, kind="ExternalInput").ap()
    t_w2 = nc.dram_tensor("w2", [D, HID], F32, kind="ExternalInput").ap()
    t_idb = nc.dram_tensor("ident_bf", [128, 128], BF16, kind="ExternalInput").ap()
    t_idf = nc.dram_tensor("ident_f32", [128, 128], F32, kind="ExternalInput").ap()
    t_tri = nc.dram_tensor("trimask", [128, 128], BF16, kind="ExternalInput").ap()
    t_o8 = nc.dram_tensor("ones8s", [8, 128, 8], BF16, kind="ExternalInput").ap()
    t_out = nc.dram_tensor("out", [nseq, S, D], F32, kind="ExternalOutput").ap()
    t_w1t = nc.dram_tensor("w1t_s", [32, 8, 128, 128], BF16, kind="Internal").ap()
    t_wq4 = nc.dram_tensor("wq4_s", [4, 128, 8192], BF16, kind="Internal").ap()
    t_w2t = nc.dram_tensor("w2t_s", [32, 128, D], BF16, kind="Internal").ap()

    es = contextlib.ExitStack()
    sb = lambda n, shape, dt: es.enter_context(nc.sbuf_tensor(n, shape, dt))

    ident_bf = sb("c_idb", [128, 128], BF16)
    ident_f32 = sb("c_idf", [128, 128], F32)
    trimask = sb("c_tri", [128, 128], BF16)
    ones8s = sb("c_o8", [128, 64], BF16)
    ones1b = sb("c_1b", [1, 128], BF16)
    ones1f = sb("c_1f", [1, 128], F32)
    onescol = sb("c_oc", [128, 1], F32)
    qb_t = sb("c_qb", [128, 8], F32)
    kb_t = sb("c_kb", [128, 8], F32)
    vb_row = sb("c_vb", [1, D], BF16)
    ob_row = sb("c_ob", [1, D], BF16)
    cw_t = sb("c_cw", [128, 96], F32)
    cb_t = sb("c_cb", [128, 32], F32)
    sw1_bc = sb("c_s1", [128, 1], F32)
    mw1_bc = sb("c_m1", [128, 1], F32)
    sw2_bc = sb("c_s2", [128, 1], F32)
    mw2_bc = sb("c_m2", [128, 1], F32)
    row1 = sb("c_r1", [1, 16], F32)
    magic_bc = sb("c_mg", [128, 1], F32)
    nmagic_bc = sb("c_nmg", [128, 1], F32)

    ctrs = {n: Ctr(nc, es, n) for n in
            ("bar", "dl", "dw", "dst", "tn", "sc", "ve", "gp",
             "dlx0", "dlx1", "dlx2", "dlx3", "dlb0", "dlb1",
             "dw1a", "dw1b", "dw1c", "dw2a", "dw2b")}
    bar, dl, dw, dst = ctrs["bar"], ctrs["dl"], ctrs["dw"], ctrs["dst"]
    tn, sc, ve, gp = ctrs["tn"], ctrs["sc"], ctrs["ve"], ctrs["gp"]
    dlx = (ctrs["dlx0"], ctrs["dlx1"], ctrs["dlx2"], ctrs["dlx3"])
    dlb = (ctrs["dlb0"], ctrs["dlb1"])
    dw1 = (ctrs["dw1a"], ctrs["dw1b"], ctrs["dw1c"])
    dw2 = (ctrs["dw2a"], ctrs["dw2b"])

    r = Rec(nc)

    def barrier():
        base = bar.v
        for e in ENGS:
            er = getattr(r, e)
            bar.inc(er.nop() if e in ("sync", "gpsimd") else er.drain())
        for e in ENGS:
            getattr(r, e).wait_ge(bar.sem, base + 5)

    dl.inc(r.sync.dma_start(out=ident_bf[:, :], in_=t_idb[:, :]), 16)
    dl.inc(r.sync.dma_start(out=ident_f32[:, :], in_=t_idf[:, :]), 16)
    dl.inc(r.sync.dma_start(out=trimask[:, :], in_=t_tri[:, :]), 16)
    dl.inc(r.sync.dma_start(
        out=ones8s.rearrange("p (hp j) -> p hp j", hp=8),
        in_=t_o8.rearrange("hp p j -> p hp j")), 16)
    dl.inc(r.sync.dma_start(
        out=qb_t[:, :], in_=t_qb.rearrange("(hp hh) d -> (hh d) hp", hh=2),
        allow_slow_non_contiguous=True), 16)
    dl.inc(r.sync.dma_start(
        out=kb_t[:, :], in_=t_kb.rearrange("(hp hh) d -> (hh d) hp", hh=2),
        allow_slow_non_contiguous=True), 16)
    dl.inc(r.sync.dma_start(
        out=cw_t.rearrange("p (j hc) -> p j hc", j=3),
        in_=t_cw.rearrange("(hc p) one j -> p j (one hc)", p=128),
        allow_slow_non_contiguous=True), 16)
    dl.inc(r.sync.dma_start(
        out=cb_t[:, :], in_=t_cb.rearrange("(hc p) -> p hc", p=128),
        allow_slow_non_contiguous=True), 16)

    with nc.sbuf_tensor("p_vst", [1, 2 * D], F32) as vbst:
        dl.inc(r.sync.dma_start(out=vbst[:, 0:D],
                                in_=t_vb.rearrange("h d -> (h d)")[None, :]), 16)
        dl.inc(r.sync.dma_start(out=vbst[:, D:2 * D], in_=t_ob[None, :]), 16)
        r.wait(r.vector, dl)
        ve.inc(r.vector.tensor_copy(vb_row[:, :], vbst[:, 0:D]))
        ve.inc(r.vector.tensor_copy(ob_row[:, :], vbst[:, D:2 * D]))
        ve.inc(r.vector.memset(ones1b[:, :], 1.0))
        ve.inc(r.vector.memset(ones1f[:, :], 1.0))
        ve.inc(r.vector.memset(onescol[:, :], 1.0))
        ve.inc(r.vector.memset(magic_bc[:, :], MAGIC))
        ve.inc(r.vector.memset(nmagic_bc[:, :], -MAGIC))
        ve.inc(r.vector.tensor_tensor(trimask[:, :], trimask[:, :],
                                      ident_bf[:, :], ALU.subtract))
        ve.inc(r.vector.drain())
        ve.inc(r.vector.tensor_single_scalar(trimask[:, :], trimask[:, :],
                                             -10000.0, ALU.mult))
        r.emit()
        barrier()
        r.emit()

    with nc.sbuf_tensor("p_wst", [128, 4096], F32) as wstg, \
         nc.sbuf_tensor("p_wbf", [128, 8192], BF16) as wbf:
        ve_cast = [0, 0]
        dw_w = 0
        for i4, t_w in enumerate((t_qw, t_kw, t_vw, t_ow)):
            if i4 < 3:
                wsrc = t_w.rearrange("h (mc p) d -> mc p h d", p=128)
            else:
                wsrc = t_ow.rearrange("(hp hh) d m -> hp (hh d) m", hh=2)
            for k4 in range(4):
                par = k4 % 2
                if i4 + k4 > 0:
                    r.wait(r.sync, ve, ve_cast[par])
                if i4 < 3:
                    for mm in range(2):
                        dlx[par].inc(r.sync.dma_start(
                            out=wstg[:, par * 2048 + mm * 1024:
                                par * 2048 + (mm + 1) * 1024].rearrange(
                                "p (h d) -> p h d", h=NH),
                            in_=wsrc[2 * k4 + mm]), 16)
                else:
                    dlx[par].inc(r.sync.dma_start(
                        out=wstg[:, par * 2048:(par + 1) * 2048].rearrange(
                            "p (hp m) -> p hp m", hp=2),
                        in_=wsrc[2 * k4:2 * k4 + 2].rearrange(
                            "hp pd m -> pd hp m")), 16)
                r.wait(r.vector, dlx[par])
                if i4 + k4 > 0:
                    r.wait(r.vector, dw, dw_w)
                ve_cast[par] = ve.inc(r.vector.tensor_copy(
                    wbf[:, k4 * 2048:(k4 + 1) * 2048],
                    wstg[:, par * 2048:(par + 1) * 2048]))
                ve.inc(r.vector.drain())
                r.wait(r.gpsimd, ve, ve_cast[par])
                dw_w = dw.inc(r.gpsimd.dma_start(
                    out=t_wq4[i4, :, k4 * 2048:(k4 + 1) * 2048],
                    in_=wbf[:, k4 * 2048:(k4 + 1) * 2048]), 16)
        r.wait(r.gpsimd, dw, dw.v)
        r.emit()
        barrier()
        r.emit()

    def wprep_quant_all():
        with nc.sbuf_tensor("wq_st", [128, 2 * 4096], F32) as stg2, \
             nc.sbuf_tensor("wq_f", [128, 4096], F32) as qf, \
             nc.sbuf_tensor("wq_b", [128, 4096], BF16) as qb8, \
             nc.sbuf_tensor("wq_sp", [128, 4096], BF16) as strip, \
             nc.psum_tensor("wq_tp", [128, 1024], BF16) as tpp:
            ve_q = 0
            dw_s = 0
            ve_g = 0
            sc_q = [0, 0]
            for j16 in range(16):
                which = "w1" if j16 < 8 else "w2"
                t_w = t_w1 if j16 < 8 else t_w2
                sw_bc = sw1_bc if j16 < 8 else sw2_bc
                blk = j16 % 8
                par = j16 % 2
                stg = stg2[:, par * 4096:(par + 1) * 4096]
                if j16 >= 2:
                    r.wait(r.sync, sc, sc_q[par])
                if which == "w1":
                    src_w = t_w.rearrange("(hc p) (fc f) -> fc p hc f",
                                          p=128, f=128)[blk]
                    dlx[par].inc(r.sync.dma_start(
                        out=stg.rearrange("p (hc f) -> p hc f", hc=32),
                        in_=src_w), 16)
                else:
                    src_w = t_w.rearrange("(oc p) (hg hq) -> hg p oc hq",
                                          p=128, hq=512)[blk]
                    dlx[par].inc(r.sync.dma_start(
                        out=stg.rearrange("p (oc hq) -> p oc hq", oc=8),
                        in_=src_w), 16)
                r.wait(r.scalar, dlx[par])
                if j16 > 0:
                    r.wait(r.scalar, ve, ve_q)
                sc_q[par] = sc.inc(r.scalar.mul(qf[:, :], stg[:, :],
                                                sw_bc[:, 0:1]))
                r.wait(r.vector, sc)
                if j16 > 0:
                    r.wait(r.vector, dw, dw_s)
                ve.inc(r.vector.tensor_scalar(qf[:, :], qf[:, :], -1.0, 1.0,
                                              ALU.max, ALU.min))
                ve.inc(r.vector.drain())
                ve_q = ve.inc(r.vector.tensor_scalar(
                    qb8[:, :], qf[:, :], MAGIC, -MAGIC, ALU.add, ALU.add))
                r.wait(r.tensor, ve, ve_q)
                for g in range(4):
                    if not (j16 == 0 and g == 0):
                        r.wait(r.tensor, ve, ve_g)
                    for k in range(8):
                        if which == "w1":
                            src_t = qb8[:, (g * 8 + k) * 128:
                                        (g * 8 + k + 1) * 128]
                        else:
                            src_t = qb8[:, k * 512 + g * 128:
                                        k * 512 + (g + 1) * 128]
                        tn.inc(r.tensor.transpose(
                            tpp[:, k * 128:(k + 1) * 128], src_t,
                            ident_bf[:, :]))
                    r.wait(r.vector, tn)
                    ve_g = ve.inc(r.vector.tensor_copy(
                        strip[:, g * 1024:(g + 1) * 1024], tpp[:, :]))
                r.wait(r.gpsimd, ve, ve_g)
                if which == "w1":
                    dw_s = dw.inc(r.gpsimd.dma_start(
                        out=t_w1t[:, blk].rearrange("hc p h -> p hc h"),
                        in_=strip.rearrange("p (hc h) -> p hc h", hc=32)), 16)
                else:
                    dw_s = dw.inc(r.gpsimd.dma_start(
                        out=t_w2t[blk * 4:(blk + 1) * 4].rearrange(
                            "hl p o -> p hl o"),
                        in_=strip.rearrange("p (hl o) -> p hl o", hl=4)), 16)
            r.wait(r.gpsimd, dw, dw.v)
            r.emit()
            barrier()

    r.emit()

    for s in range(nseq):
        es_s = contextlib.ExitStack()
        sbs = lambda n, shape, dt: es_s.enter_context(
            nc.sbuf_tensor(f"{n}_{s}", shape, dt))

        es_qkv = contextlib.ExitStack()
        Q_sb = es_qkv.enter_context(
            nc.sbuf_tensor(f"Q_{s}", [128, 8 * S], BF16))
        K_sb = es_qkv.enter_context(
            nc.sbuf_tensor(f"K_{s}", [128, 8 * S], BF16))
        V_sb = es_qkv.enter_context(
            nc.sbuf_tensor(f"V_{s}", [128, NT * NH * 65], BF16))

        es_a = contextlib.ExitStack()
        QWb = es_a.enter_context(nc.sbuf_tensor(f"a_qw{s}", [128, 8192], BF16))
        KWb = es_a.enter_context(nc.sbuf_tensor(f"a_kw{s}", [128, 8192], BF16))
        VWb = es_a.enter_context(nc.sbuf_tensor(f"a_vw{s}", [128, 8192], BF16))
        xhat = es_a.enter_context(nc.sbuf_tensor(f"a_xh{s}", [128, 8 * S], BF16))
        xt2 = es_a.enter_context(nc.sbuf_tensor(f"a_xt{s}", [128, 4 * D], F32))
        xh2 = es_a.enter_context(nc.sbuf_tensor(f"a_xm{s}", [128, 2 * D], BF16))
        st24 = es_a.enter_context(nc.sbuf_tensor(f"a_st{s}", [128, 32], F32))
        bnsa = es_a.enter_context(nc.sbuf_tensor(f"a_bn{s}", [128, 48], F32))
        mva = es_a.enter_context(nc.sbuf_tensor(f"a_mv{s}", [128, 8], F32))
        tpp = es_a.enter_context(nc.psum_tensor(f"a_tp{s}", [128, 2048], BF16))
        qkps = es_a.enter_context(nc.psum_tensor(f"a_qk{s}", [128, 1024], F32))
        vps = es_a.enter_context(nc.psum_tensor(f"a_vp{s}", [128, 2048], F32))
        if True:
            if True:
                for i4, Wb in ((0, QWb), (1, KWb), (2, VWb)):
                    dlx[i4 % 2].inc(r.sync.dma_start(
                        out=Wb[:, :], in_=t_wq4[i4]), 16)
                for i4 in range(3):
                    r.wait(r.vector, dlx[i4 % 2])
                ve.inc(r.vector.memset(
                    V_sb.rearrange("p (kt h e) -> p kt h e", kt=NT, h=NH)
                    [:, :, :, 64:65], 1.0))
                r.emit()
                barrier()

            ve_xh = [0] * NT
            ve_vv = [0] * NT
            ve_ap = [0] * NT
            tn_xh = [0] * NT
            sc_qk2 = [0, 0]
            var_sA = mva.rearrange("p (j two) -> p two j", two=2)[:, 1:2, :]

            def a_stats(p):
                    for j in range(4):
                        u = 4 * p + j
                        if u >= 4:
                            r.wait(r.sync, ve, ve_ap[u - 4])
                        dlx[j].inc(r.sync.dma_start(
                            out=xt2[:, j * D:(j + 1) * D],
                            in_=t_x[s, u * 128:(u + 1) * 128, :]), 16)
                    for j in range(4):
                        r.wait(r.vector, dlx[j])
                        ve.inc(r.vector.bn_stats(bnsa[:, j * 12:j * 12 + 6],
                                                 xt2[:, j * D:j * D + 512]))
                        ve.inc(r.vector.bn_stats(bnsa[:, j * 12 + 6:j * 12 + 12],
                                                 xt2[:, j * D + 512:(j + 1) * D]))
                    ve.inc(r.vector.drain())
                    for j in range(4):
                        ve.inc(r.vector.bn_aggr(mva[:, j * 2:(j + 1) * 2],
                                                bnsa[:, j * 12:j * 12 + 12]))
                    ve.inc(r.vector.drain())
                    ve.inc(r.vector.tensor_scalar_add(
                        st24.rearrange("p (a b) -> p a b", a=1)[:, :, 0:4],
                        var_sA, EPS))
                    r.wait(r.scalar, ve)
                    sc.inc(r.scalar.activation(st24[:, 4:8], st24[:, 0:4],
                                               AF.Sqrt))
                    r.wait(r.vector, sc)
                    ve.inc(r.vector.reciprocal(st24[:, 8:12], st24[:, 4:8]))
                    ve.inc(r.vector.drain())
                    ve.inc(r.vector.tensor_mul(st24[:, 12:16], st24[:, 8:12],
                                               st24[:, 8:12]))
                    ve.inc(r.vector.drain())
                    ve.inc(r.vector.tensor_mul(st24[:, 12:16], st24[:, 12:16],
                                               st24[:, 0:4]))
                    ve.inc(r.vector.drain())
                    ve.inc(r.vector.tensor_scalar(st24[:, 12:16], st24[:, 12:16],
                                                  -0.5, 1.5, ALU.mult, ALU.add))
                    ve.inc(r.vector.drain())
                    ve.inc(r.vector.tensor_mul(st24[:, 16:20], st24[:, 8:12],
                                               st24[:, 12:16]))
                    ve.inc(r.vector.drain())

            def a_tiles(p):
                    for j in range(4):
                        u = 4 * p + j
                        ub = u % 2
                        xt = xt2[:, j * D:(j + 1) * D]
                        xh = xh2[:, ub * D:(ub + 1) * D]
                        if u >= 2:
                            r.wait(r.vector, tn, tn_xh[u - 2])
                        ve_ap[u] = ve.inc(r.vector.tensor_scalar(
                            xh[:, :], xt[:, :], mva[:, 2 * j:2 * j + 1],
                            st24[:, 16 + j:17 + j], ALU.subtract, ALU.mult))
                        r.wait(r.tensor, ve, ve_ap[u])
                        if u >= 2:
                            r.wait(r.tensor, ve, ve_xh[u - 2])
                        tb = (u % 2) * 1024
                        for c in range(8):
                            tn.inc(r.tensor.transpose(
                                tpp[:, tb + c * 128:tb + (c + 1) * 128],
                                xh[:, c * 128:(c + 1) * 128], ident_bf[:, :]))
                        tn_xh[u] = tn.v
                        r.wait(r.vector, tn, tn_xh[u])
                        ve_xh[u] = ve.inc(r.vector.tensor_copy(
                            xhat.rearrange("p (c t) -> p c t", c=8)
                            [:, :, u * 128:(u + 1) * 128],
                            tpp[:, tb:tb + 1024].rearrange("p (c t) -> p c t",
                                                           c=8)))
                        r.wait(r.tensor, ve, ve_xh[u])
                        if u >= 2:
                            r.wait(r.tensor, ve, ve_vv[u - 2])
                        vpb = ub * 1024
                        for half in range(2):
                            for mc in range(8):
                                tn.inc(r.tensor.matmul(
                                    vps[:, vpb + half * 512:vpb + (half + 1) * 512],
                                    xhat[:, mc * S + u * 128:mc * S + (u + 1) * 128],
                                    VWb[:, mc * 1024 + half * 512:
                                        mc * 1024 + (half + 1) * 512],
                                    start=(mc == 0), stop=False))
                            tn.inc(r.tensor.matmul(
                                vps[:, vpb + half * 512:vpb + (half + 1) * 512],
                                ones1b[0:1, :],
                                vb_row[0:1, half * 512:(half + 1) * 512],
                                start=False, stop=True))
                        r.wait(r.vector, tn)
                        ve_vv[u] = ve.inc(r.vector.tensor_copy(
                            V_sb.rearrange("p (kt h e) -> p kt h e", kt=NT, h=NH)
                            [:, u, :, 0:64],
                            vps[:, vpb:vpb + 1024].rearrange("p (h d) -> p h d",
                                                             h=NH)))

            def a_qk(p):
                    for W, bias, dstb in ((QWb, qb_t, Q_sb), (KWb, kb_t, K_sb)):
                        for oc in range(8):
                            off = (oc % 2) * 512
                            if sc_qk2[oc % 2]:
                                r.wait(r.tensor, sc, sc_qk2[oc % 2])
                            for mc in range(8):
                                tn.inc(r.tensor.matmul(
                                    qkps[:, off:off + 512],
                                    W[:, mc * 1024 + oc * 128:
                                      mc * 1024 + (oc + 1) * 128],
                                    xhat[:, mc * S + p * 512:mc * S + (p + 1) * 512],
                                    start=(mc == 0), stop=(mc == 7)))
                            r.wait(r.scalar, tn)
                            sc_qk2[oc % 2] = sc.inc(r.scalar.activation(
                                dstb[:, oc * S + p * 512:oc * S + (p + 1) * 512],
                                qkps[:, off:off + 512], AF.Identity,
                                bias=bias[:, oc:oc + 1]))

            a_stats(0)
            for p in range(NP):
                a_tiles(p)
                if p + 1 < NP:
                    a_stats(p + 1)
                a_qk(p)
            r.emit()
            barrier()

        es_a.close()

        DBG.update({f"xhat{s}": xhat, f"Q{s}": Q_sb, f"K{s}": K_sb,
                    f"V{s}": V_sb, f"st24{s}": st24})
        x2 = es_s.enter_context(
            nc.sbuf_tensor(f"x2_{s}", [128, NT * D], BF16, side="right"))
        es_pw = contextlib.ExitStack()
        if s == 0:
            pstg = es_pw.enter_context(
                nc.sbuf_tensor("pw_stg", [128, 2 * 2048], F32, side="right"))
            pasum = es_pw.enter_context(
                nc.sbuf_tensor("pw_as", [128, 36], F32, side="right"))
            pw_flat = [t_w.rearrange("a b -> (a b)").rearrange(
                "(ci p a) -> ci p a", ci=16, p=128)
                for t_w in (t_w1, t_w2)]
            pw_abs = [0] * 32

            def stats_load(n):
                if n >= 32:
                    return
                par = n % 2
                if n >= 2:
                    r.wait(r.sync, ve, pw_abs[n - 2])
                dlx[2 + par].inc(r.sync.dma_start(
                    out=pstg[:, par * 2048:(par + 1) * 2048],
                    in_=pw_flat[n // 16][n % 16]), 16)

            def stats_step(n):
                par = n % 2
                r.wait(r.vector, dlx[2 + par])
                pw_abs[n] = ve.inc(r.vector.reduce_sum(
                    pasum[:, n:n + 1], pstg[:, par * 2048:(par + 1) * 2048],
                    axis=AX.X, apply_absolute_value=True))
                stats_load(n + 2)
        es_b = contextlib.ExitStack()
        OWb = es_b.enter_context(nc.sbuf_tensor(f"b_ow{s}", [128, 8192], BF16))
        Cbias = es_b.enter_context(nc.sbuf_tensor(f"b_cb{s}", [128, 8], F32))
        qsq = es_b.enter_context(nc.sbuf_tensor(f"b_sq{s}", [128, 512], BF16))
        smf = es_b.enter_context(nc.sbuf_tensor(f"b_sm{s}", [128, 16], F32))
        ptil = es_b.enter_context(nc.sbuf_tensor(f"b_pt{s}", [128, 2048], BF16))
        atok = es_b.enter_context(nc.sbuf_tensor(f"b_at{s}", [128, 4 * 1024], F32))
        attT = es_b.enter_context(nc.sbuf_tensor(f"b_tt{s}", [128, 1024], BF16))
        zrec = es_b.enter_context(nc.sbuf_tensor(f"b_zr{s}", [128, 8], F32))
        xre = es_b.enter_context(nc.sbuf_tensor(f"b_xr{s}", [128, 2 * D], F32))
        if True:
            with nc.psum_tensor(f"b_ut{s}", [128, 512], F32) as util:
                dlx[0].inc(r.sync.dma_start(out=OWb[:, :], in_=t_wq4[3]), 16)
                r.wait(r.vector, dlx[0])
                ve.inc(r.vector.engine_nop())

                ve_red = 0
                for ti, Xb in ((0, Q_sb), (1, K_sb)):
                    for p in range(NP):
                        for hp in range(8):
                            if not (ti == 0 and p == 0 and hp == 0):
                                r.wait(r.vector, tn, tn.v)
                            ve.inc(r.vector.tensor_mul(
                                qsq[:, :],
                                Xb[:, hp * S + p * 512:hp * S + (p + 1) * 512],
                                Xb[:, hp * S + p * 512:hp * S + (p + 1) * 512]))
                            r.wait(r.tensor, ve)
                            if hp == 0 and not (ti == 0 and p == 0):
                                r.wait(r.tensor, ve, ve_red)
                            tn.inc(r.tensor.matmul(
                                util[0:8, :], ones8s[:, hp * 8:(hp + 1) * 8],
                                qsq[:, :], start=(hp == 0), stop=(hp == 7)))
                        r.wait(r.vector, tn)
                        ve_red = ve.inc(r.vector.reduce_max(
                            smf[0:8, 8 + p:9 + p], util[0:8, :], axis=AX.X))
                    ve.inc(r.vector.drain())
                    ve_red = ve.inc(r.vector.reduce_max(
                        smf[0:8, ti:ti + 1], smf[0:8, 8:8 + NP], axis=AX.X))
                    ve.inc(r.vector.drain())
                ve.inc(r.vector.tensor_mul(smf[0:8, 2:3], smf[0:8, 0:1],
                                           smf[0:8, 1:2]))
                r.wait(r.scalar, ve)
                sc.inc(r.scalar.activation(smf[0:8, 3:4], smf[0:8, 2:3],
                                           AF.Sqrt))
                r.wait(r.vector, sc)
                ve.inc(r.vector.tensor_single_scalar(smf[0:8, 4:5],
                                                     smf[0:8, 3:4],
                                                     -1.02 * 0.125, ALU.mult))
                r.wait(r.tensor, ve)
                tn.inc(r.tensor.transpose(util[0:1, 0:8], smf[0:8, 4:5],
                                          ident_f32[0:8, 0:8]))
                r.wait(r.vector, tn)
                ve.inc(r.vector.tensor_copy(row1[:, 0:8], util[0:1, 0:8]))
                r.wait(r.tensor, ve)
                tn.inc(r.tensor.matmul(util[:, 0:8], ones1f[0:1, :],
                                       row1[0:1, 0:8], start=True, stop=True))
                r.wait(r.vector, tn)
                ve.inc(r.vector.tensor_copy(Cbias[:, :], util[:, 0:8]))
                r.emit()
                barrier()

            DBG.update({f"Cb{s}": Cbias, f"smf{s}": smf, f"pt{s}": ptil,
                        f"zr{s}": zrec, f"at{s}": atok})
            Lps = es_b.enter_context(
                nc.psum_tensor(f"b_lp{s}", [128, 2048], F32))
            AVps2 = es_b.enter_context(
                nc.psum_tensor(f"b_av{s}", [128, 2048], F32))
            sc_exp = [0, 0]
            ve_pn = 0
            ve_pn2 = [0, 0]
            ve_aT2 = 0
            ve_x2 = 0
            tn_tT = 0
            dl_x = [0] * NT
            for u in (0, 1):
                dl_x[u] = dlb[u % 2].inc(r.sync.dma_start(
                    out=xre[:, (u % 2) * D:(u % 2 + 1) * D],
                    in_=t_x[s, u * 128:(u + 1) * 128, :]), 16)
            if s == 0:
                stats_load(0)
                stats_load(1)
            for p in range(NP):
                kcmax = 4 * (p + 1)
                for hp in range(8):
                    tn_log = [0, 0]

                    def emit_logits(kc, hp=hp, p=p):
                        lb = kc % 2
                        Lp = Lps[:, lb * 1024:(lb + 1) * 1024]
                        r.wait(r.tensor, sc, sc_exp[lb])
                        if p > 0 and hp == 0 and kc < 2:
                            r.wait(r.tensor, ve, ve_x2)
                        tn.inc(r.tensor.matmul(
                            Lp[:, 0:512],
                            K_sb[0:64,
                                 hp * S + kc * 128:hp * S + (kc + 1) * 128],
                            Q_sb[0:64,
                                 hp * S + p * 512:hp * S + (p + 1) * 512],
                            start=True, stop=True, tile_position=(0, 0)))
                        tn.inc(r.tensor.matmul(
                            Lp[:, 512:1024],
                            K_sb[64:128,
                                 hp * S + kc * 128:hp * S + (kc + 1) * 128],
                            Q_sb[64:128,
                                 hp * S + p * 512:hp * S + (p + 1) * 512],
                            start=True, stop=True, tile_position=(64, 0)))
                        if kc >= 4 * p:
                            ob = (kc - 4 * p) * 128
                            for hh in range(2):
                                tn.inc(r.tensor.matmul(
                                    Lp[:, hh * 512 + ob:hh * 512 + ob + 128],
                                    trimask[:, :], ident_bf[:, :],
                                    start=False, stop=True,
                                    skip_group_check=True))
                        tn_log[lb] = tn.v

                    emit_logits(0)
                    for kc in range(kcmax):
                        lb = kc % 2
                        if kc + 1 < kcmax:
                            emit_logits(kc + 1)
                        Lp = Lps[:, lb * 1024:(lb + 1) * 1024]
                        pb = ptil[:, lb * 1024:(lb + 1) * 1024]
                        o = (kc - 4 * p) * 128
                        r.wait(r.scalar, tn, tn_log[lb])
                        if kc < 4 * p:
                            sc_exp[lb] = sc.inc(r.scalar.activation(
                                pb, Lp, AF.Exp, scale=0.125,
                                bias=Cbias[:, hp:hp + 1]))
                        else:
                            if o > 0:
                                r.wait(r.vector, sc, sc_exp[lb])
                                r.wait(r.vector, tn, tn_log[lb])
                                ve.inc(r.vector.memset(
                                    pb.rearrange("p (h q) -> p h q", h=2)
                                    [:, :, 0:o], 0.0))
                                r.wait(r.scalar, ve)
                            sc_exp[lb] = sc.inc(r.scalar.activation(
                                pb.rearrange("p (h q) -> p h q", h=2)
                                [:, :, o:512],
                                Lp.rearrange("p (h q) -> p h q", h=2)
                                [:, :, o:512],
                                AF.Exp, scale=0.125, bias=Cbias[:, hp:hp + 1]))
                        r.wait(r.tensor, sc, sc_exp[lb])
                        if kc == 0:
                            r.wait(r.tensor, ve, ve_pn2[hp % 2])
                            if hp < 2 and p > 0:
                                r.wait(r.tensor, ve, ve_aT2)
                        ab = (hp % 2) * 1024
                        for hh in range(2):
                            for qs in range(4):
                                tn.inc(r.tensor.matmul(
                                    AVps2[:, ab + hh * 512 + qs * 65:
                                          ab + hh * 512 + qs * 65 + 65],
                                    pb[:, hh * 512 + qs * 128:
                                       hh * 512 + (qs + 1) * 128],
                                    V_sb[:, (kc * NH + 2 * hp + hh) * 65:
                                         (kc * NH + 2 * hp + hh) * 65 + 65],
                                    start=(kc == 0 and qs == 0),
                                    stop=(kc == kcmax - 1),
                                    skip_group_check=True))
                    tn_av = tn.v
                    r.wait(r.vector, tn, tn_av)
                    if p > 0 and hp == 0:
                        r.wait(r.vector, tn, tn_tT)
                    ve.inc(r.vector.reciprocal(
                        zrec.rearrange("p (hh q o) -> p hh q o", hh=2, o=1),
                        bass.AP(AVps2[:, :].tensor, ab + 64,
                                [[2048, 128], [512, 2], [65, 4], [1, 1]])))
                    ve.inc(r.vector.drain())
                    for hh in range(2):
                        for qs in range(4):
                            ve_pn = ve.inc(r.vector.tensor_single_scalar(
                                atok[:, qs * 1024 + (2 * hp + hh) * 64:
                                     qs * 1024 + (2 * hp + hh + 1) * 64],
                                AVps2[:, ab + hh * 512 + qs * 65:
                                      ab + hh * 512 + qs * 65 + 64],
                                zrec[:, hh * 4 + qs:hh * 4 + qs + 1], ALU.mult))
                    ve_pn2[hp % 2] = ve_pn
                    if s == 0:
                        stats_step(p * 8 + hp)
                for v in range(4):
                    u = 4 * p + v
                    r.wait(r.tensor, ve, ve_pn)
                    r.wait(r.tensor, sc, sc.v)
                    if v == 0:
                        r.wait(r.tensor, ve, ve_x2)
                    for c in range(8):
                        tn.inc(r.tensor.transpose(
                            AVps2[:, c * 128:(c + 1) * 128],
                            atok[:, v * 1024 + c * 128:v * 1024 + (c + 1) * 128],
                            ident_f32[:, :]))
                    tn_tT = tn.v
                    r.wait(r.vector, tn, tn_tT)
                    ve_aT = ve.inc(r.vector.tensor_copy(attT[:, :],
                                                        AVps2[:, 0:1024]))
                    ve_aT2 = ve_aT
                    r.wait(r.tensor, ve, ve_aT)
                    for half in range(2):
                        hs = half * 512
                        for c in range(8):
                            tn.inc(r.tensor.matmul(
                                Lps[:, hs:hs + 512],
                                attT[:, c * 128:(c + 1) * 128],
                                OWb[:, c * 1024 + hs:c * 1024 + hs + 512],
                                start=(c == 0), stop=False))
                        tn.inc(r.tensor.matmul(
                            Lps[:, hs:hs + 512], ones1b[0:1, :],
                            ob_row[0:1, hs:hs + 512], start=False, stop=True))
                    tn_o = tn.v
                    r.wait(r.vector, tn, tn_o)
                    r.wait(r.vector, dlb[u % 2], dl_x[u])
                    ve_x2 = ve.inc(r.vector.tensor_add(
                        x2[:, u * D:(u + 1) * D],
                        xre[:, (u % 2) * D:(u % 2 + 1) * D], Lps[:, 0:1024]))
                    nu = u + 2
                    if nu < NT:
                        r.wait(r.sync, ve, ve_x2)
                        dl_x[nu] = dlb[nu % 2].inc(r.sync.dma_start(
                            out=xre[:, (nu % 2) * D:(nu % 2 + 1) * D],
                            in_=t_x[s, nu * 128:(nu + 1) * 128, :]), 16)
            if s == 0:
                for n in range(NP * 8, 32):
                    stats_step(n)
            r.emit()
            barrier()

        es_b.close()
        es_qkv.close()

        if s == 0:
            with nc.sbuf_tensor("pw_s1", [1, 8], F32) as s1s, \
                 nc.psum_tensor("pw_ps", [128, 512], F32) as ups:
                for wi, (sw_bc, mw_bc) in enumerate(
                        ((sw1_bc, mw1_bc), (sw2_bc, mw2_bc))):
                    nel = HID * D
                    ve.inc(r.vector.reduce_sum(
                        pasum[:, 32 + wi:33 + wi],
                        pasum[:, wi * 16:wi * 16 + 16], axis=AX.X))
                    ve.inc(r.vector.drain())
                    r.wait(r.tensor, ve, ve.v)
                    tn.inc(r.tensor.matmul(
                        ups[0:1, wi:wi + 1], pasum[:, 32 + wi:33 + wi],
                        onescol[:, :], start=True, stop=True))
                    r.wait(r.vector, tn, tn.v)
                    ve.inc(r.vector.tensor_scalar(
                        s1s[:, 4 * wi:4 * wi + 1], ups[0:1, wi:wi + 1],
                        1.0 / nel, 1e-5, ALU.mult, ALU.max))
                    ve.inc(r.vector.drain())
                    ve.inc(r.vector.reciprocal(s1s[:, 4 * wi + 1:4 * wi + 2],
                                               s1s[:, 4 * wi:4 * wi + 1]))
                    ve.inc(r.vector.drain())
                    r.wait(r.tensor, ve, ve.v)
                    tn.inc(r.tensor.matmul(
                        ups[:, 0:1], ones1f[0:1, :],
                        s1s[0:1, 4 * wi:4 * wi + 1], start=True, stop=True))
                    tn.inc(r.tensor.matmul(
                        ups[:, 1:2], ones1f[0:1, :],
                        s1s[0:1, 4 * wi + 1:4 * wi + 2],
                        start=True, stop=True))
                    r.wait(r.vector, tn, tn.v)
                    ve.inc(r.vector.tensor_copy(mw_bc[:, :], ups[:, 0:1]))
                    ve.inc(r.vector.tensor_copy(sw_bc[:, :], ups[:, 1:2]))
                    ve.inc(r.vector.drain())
                r.emit()
                barrier()
            wprep_quant_all()
            r.emit()
        es_pw.close()

        es_c = contextlib.ExitStack()
        xq2 = es_c.enter_context(nc.sbuf_tensor(f"c_xq{s}", [128, 8 * SP2], BF16))
        g1col = es_c.enter_context(nc.sbuf_tensor(f"c_gc{s}", [128, NT], F32))
        cst = es_c.enter_context(nc.sbuf_tensor(f"c_st{s}", [128, 64], F32))
        bnsc = es_c.enter_context(nc.sbuf_tensor(f"c_bn{s}", [128, 48], F32))
        mvc = es_c.enter_context(nc.sbuf_tensor(f"c_mv{s}", [128, 8], F32))
        if True:
            es_cp = contextlib.ExitStack()
            tppC = es_cp.enter_context(
                nc.psum_tensor(f"c_tp{s}", [128, 1024], BF16))
            cxt = es_cp.enter_context(
                nc.sbuf_tensor(f"c_xv{s}", [128, 4 * D], F32, side="right"))
            cxm = es_cp.enter_context(
                nc.sbuf_tensor(f"c_xm{s}", [128, 4 * D], BF16, side="right"))
            cpb = es_cp.enter_context(
                nc.sbuf_tensor(f"c_pb{s}", [128, 192], F32, side="right"))
            mv16 = es_cp.enter_context(
                nc.sbuf_tensor(f"c_mw{s}", [128, 32], F32, side="right"))
            cs2 = es_cp.enter_context(
                nc.sbuf_tensor(f"c_cq{s}", [128, 128], F32, side="right"))
            ve.inc(r.vector.memset(
                xq2.rearrange("p (c t) -> p c t", c=8)[:, :, 0:1], 0.0))
            ve.inc(r.vector.memset(
                xq2.rearrange("p (c t) -> p c t", c=8)[:, :, S + 1:S + 2], 0.0))
            tn_pq = [0] * NT
            ve_pq = [0] * NT
            sc_r2 = [0] * NT
            sc_r1 = [0] * NT
            for u in range(NT):
                ve.inc(r.vector.bn_stats(cpb[:, u * 12:u * 12 + 6],
                                         x2[:, u * D:u * D + 512]))
                ve.inc(r.vector.bn_stats(cpb[:, u * 12 + 6:u * 12 + 12],
                                         x2[:, u * D + 512:(u + 1) * D]))
            ve.inc(r.vector.drain())
            for u in range(NT):
                ve.inc(r.vector.bn_aggr(mv16[:, u * 2:(u + 1) * 2],
                                        cpb[:, u * 12:u * 12 + 12]))
            ve.inc(r.vector.drain())
            var16 = mv16.rearrange("p (j two) -> p two j", two=2)[:, 1:2, :]
            ve.inc(r.vector.tensor_scalar_add(
                cs2.rearrange("p (a b) -> p a b", a=1)[:, :, 0:16], var16,
                EPS))
            r.wait(r.scalar, ve)
            sc.inc(r.scalar.activation(cs2[:, 16:32], cs2[:, 0:16], AF.Sqrt))
            r.wait(r.vector, sc)
            ve.inc(r.vector.reciprocal(cs2[:, 32:48], cs2[:, 16:32]))
            ve.inc(r.vector.drain())
            ve.inc(r.vector.tensor_mul(cs2[:, 48:64], cs2[:, 32:48],
                                       cs2[:, 32:48]))
            ve.inc(r.vector.drain())
            ve.inc(r.vector.tensor_mul(cs2[:, 48:64], cs2[:, 48:64],
                                       cs2[:, 0:16]))
            ve.inc(r.vector.drain())
            ve.inc(r.vector.tensor_scalar(cs2[:, 48:64], cs2[:, 48:64],
                                          -0.5, 1.5, ALU.mult, ALU.add))
            ve.inc(r.vector.drain())
            ve.inc(r.vector.tensor_mul(cs2[:, 64:80], cs2[:, 32:48],
                                       cs2[:, 48:64]))
            ve.inc(r.vector.drain())
            for tq in range(NT // 4):
                q0 = 4 * tq
                if tq >= 1:
                    r.wait(r.vector, tn, tn_pq[q0 - 1])
                    r.wait(r.vector, sc, sc_r1[q0 - 1])
                for j in range(4):
                    u = q0 + j
                    ve.inc(r.vector.tensor_scalar(
                        cxt[:, j * D:(j + 1) * D], x2[:, u * D:(u + 1) * D],
                        mv16[:, 2 * u:2 * u + 1], cs2[:, 64 + u:65 + u],
                        ALU.subtract, ALU.mult))
                ve.inc(r.vector.drain())
                for j in range(4):
                    u = q0 + j
                    ve.inc(r.vector.reduce_max(cs2[:, 80 + u:81 + u],
                                               cxt[:, j * D:(j + 1) * D],
                                               axis=AX.X,
                                               apply_absolute_value=True))
                ve.inc(r.vector.drain())
                ve.inc(r.vector.tensor_scalar_max(
                    cs2[:, 96 + q0:100 + q0], cs2[:, 80 + q0:84 + q0], 1e-5))
                ve.inc(r.vector.drain())
                ve.inc(r.vector.reciprocal(cs2[:, 112 + q0:116 + q0],
                                           cs2[:, 96 + q0:100 + q0]))
                ve.inc(r.vector.drain())
                ve.inc(r.vector.tensor_single_scalar(
                    cs2[:, 112 + q0:116 + q0], cs2[:, 112 + q0:116 + q0],
                    127.0, ALU.mult))
                ve.inc(r.vector.tensor_scalar(
                    g1col[:, q0:q0 + 4], cs2[:, 96 + q0:100 + q0],
                    mw1_bc[:, 0:1], 1.0 / 127.0, ALU.mult, ALU.mult))
                ve_q = ve.inc(r.vector.drain())
                r.wait(r.scalar, ve, ve_q)
                if tq >= 1:
                    r.wait(r.scalar, tn, tn_pq[q0 - 1])
                for pj in range(2):
                    for j in (2 * pj, 2 * pj + 1):
                        u = q0 + j
                        sc_r1[u] = sc.inc(r.scalar.activation(
                            cxt[:, j * D:(j + 1) * D],
                            cxt[:, j * D:(j + 1) * D], AF.Identity,
                            scale=cs2[:, 112 + u:113 + u],
                            bias=magic_bc[:, 0:1]))
                    sc.inc(r.scalar.drain())
                    for j in (2 * pj, 2 * pj + 1):
                        u = q0 + j
                        sc_r2[u] = sc.inc(r.scalar.activation(
                            cxm[:, j * D:(j + 1) * D],
                            cxt[:, j * D:(j + 1) * D], AF.Identity,
                            bias=nmagic_bc[:, 0:1]))
                for j in range(4):
                    u = q0 + j
                    r.wait(r.tensor, sc, sc_r2[u])
                    if not (tq == 0 and j == 0):
                        r.wait(r.tensor, ve, ve_pq[u - 1])
                    for c in range(8):
                        tn.inc(r.tensor.transpose(
                            tppC[:, c * 128:(c + 1) * 128],
                            cxm[:, j * D + c * 128:j * D + (c + 1) * 128],
                            ident_bf[:, :]))
                    tn_pq[u] = tn.v
                    r.wait(r.vector, tn, tn_pq[u])
                    ve_pq[u] = ve.inc(r.vector.tensor_copy(
                        xq2.rearrange("p (c t) -> p c t", c=8)
                        [:, :, 1 + u * 128:1 + (u + 1) * 128],
                        tppC.rearrange("p (c t) -> p c t", c=8)))
            r.emit()
            barrier()
            es_cp.close()

            hps = es_c.enter_context(nc.psum_tensor(f"c_hp{s}",
                                                    [128, 2048], F32))
            ops = es_c.enter_context(nc.psum_tensor(f"c_op{s}",
                                                    [128, 2048], F32))
            Hbq = es_c.enter_context(
                nc.sbuf_tensor(f"c_H{s}", [128, 32 * 514], BF16))
            Hb2 = es_c.enter_context(
                nc.sbuf_tensor(f"c_H2{s}", [128, 32 * 512], BF16))
            G1sb = es_c.enter_context(
                nc.sbuf_tensor(f"c_g1b{s}", [128, 514], F32))
            w1b = es_c.enter_context(
                nc.sbuf_tensor(f"c_w1b{s}", [128, 3 * 2048], BF16))
            w2b = es_c.enter_context(
                nc.sbuf_tensor(f"c_w2b{s}", [128, 2 * 2048], BF16))
            dgb = es_c.enter_context(
                nc.sbuf_tensor(f"c_dg{s}", [128, 3 * 384], BF16))
            ct1 = es_c.enter_context(
                nc.sbuf_tensor(f"c_t1{s}", [128, 2 * 512], F32))
            ct2 = es_c.enter_context(
                nc.sbuf_tensor(f"c_t2{s}", [128, 512], F32))
            mxt = es_c.enter_context(
                nc.sbuf_tensor(f"c_mx{s}", [128, 2 * 512], BF16))
            mxn = es_c.enter_context(
                nc.sbuf_tensor(f"c_mn{s}", [128, 2 * 512], BF16))
            hqg = es_c.enter_context(
                nc.sbuf_tensor(f"c_hq{s}", [128, 32 * 512], BF16))
            S2sb = es_c.enter_context(
                nc.sbuf_tensor(f"c_ss{s}", [128, 512], F32))
            g2buf = es_c.enter_context(
                nc.sbuf_tensor(f"c_g2{s}", [128, 8], F32))
            obuf = es_c.enter_context(
                nc.sbuf_tensor(f"c_obf{s}", [128, D], F32, side="right"))
            ve_h = [0] * 32
            sc_sil = [0] * 32
            ve_hq = [0] * 32
            tn_cv = [0] * 32
            ve_dgb = [0] * 32
            slot_free = [0, 0, 0]
            ve_mx = 0
            ve_ev = 0
            ve_g1 = 0
            gp_ob = 0
            dl_w1 = [0] * 16
            st_ch = {}
            w2_done = {}
            g2s = {}
            w2b_cons = [0, 0]

            def emit_w2_sub(t, half, glob):
                vp = 0 if half == 0 else 2
                wb2 = glob % 2
                g2c = 2 * glob
                r.wait(r.sync, tn, w2b_cons[wb2])
                dlw = dw2[wb2].inc(r.sync.dma_start(
                    out=w2b[:, wb2 * 2048:(wb2 + 1) * 2048].rearrange(
                        "p (l m) -> p l m", l=2),
                    in_=t_w2t[g2c:g2c + 2].rearrange("l p m -> p l m")), 16)
                if glob == 0:
                    r.wait(r.tensor, ve, ve_ev)
                r.wait(r.tensor, ve, ve_hq[2 * glob + 1])
                r.wait(r.tensor, dw2[wb2], dlw)
                for l in range(2):
                    hl2 = g2c + l
                    mv = w2b[:, (wb2 * 2 + l) * 1024:
                             (wb2 * 2 + l + 1) * 1024]
                    first = (glob == 0 and l == 0)
                    last = (glob == 15 and l == 1)
                    for v in (vp, vp + 1):
                        vv = v - vp
                        tn.inc(r.tensor.matmul(
                            ops[:, (2 * vv) * 512:(2 * vv + 1) * 512],
                            hqg[:, hl2 * 512 + v * 128:
                                hl2 * 512 + (v + 1) * 128],
                            mv[:, 0:512], start=first, stop=last,
                            skip_group_check=True))
                        tn.inc(r.tensor.matmul(
                            ops[:, (2 * vv + 1) * 512:(2 * vv + 2) * 512],
                            hqg[:, hl2 * 512 + v * 128:
                                hl2 * 512 + (v + 1) * 128],
                            mv[:, 512:1024], start=first, stop=last,
                            skip_group_check=True))
                w2b_cons[wb2] = tn.v

            def emit_final_pair(t, vp):
                nonlocal ve_ev, gp_ob
                tn_w2 = tn.v
                for vv in range(2):
                    v = vp + vv
                    u = 4 * t + v
                    r.wait(r.vector, tn, tn_w2)
                    r.wait(r.vector, dst, gp_ob)
                    ve.inc(r.vector.scalar_tensor_tensor(
                        obuf[:, 0:512], ops[:, (2 * vv) * 512:
                                            (2 * vv + 1) * 512],
                        g2s[t][:, v:v + 1],
                        x2[:, u * D:u * D + 512], ALU.mult, ALU.add))
                    ve_evl = ve.inc(r.vector.scalar_tensor_tensor(
                        obuf[:, 512:1024], ops[:, (2 * vv + 1) * 512:
                                               (2 * vv + 2) * 512],
                        g2s[t][:, v:v + 1],
                        x2[:, u * D + 512:(u + 1) * D], ALU.mult, ALU.add))
                    ve.inc(r.vector.drain())
                    r.wait(r.gpsimd, ve, ve_evl)
                    gp_ob = dst.inc(r.gpsimd.dma_start(
                        out=t_out[s, u * 128:(u + 1) * 128, :],
                        in_=obuf[:, :]), 16)
                    ve_ev = ve_evl

            def emit_conv_silu(t, hc):
                par = hc % 2
                dbase = (hc % 3) * 384
                r.wait(r.tensor, ve, ve_h[hc])
                r.wait(r.tensor, ve, ve_dgb[hc])
                for j in range(3):
                    tn.inc(r.tensor.matmul(
                        hps[:, par * 1024:par * 1024 + 512],
                        dgb[:, dbase + j * 128:dbase + (j + 1) * 128],
                        Hbq[:, hc * 514 + j:hc * 514 + j + 512],
                        start=(j == 0), stop=(j == 2)))
                tn_cv[hc] = tn.v
                r.wait(r.scalar, tn, tn_cv[hc])
                r.wait(r.scalar, ve, ve_hq[hc])
                sc_sil[hc] = sc.inc(r.scalar.activation(
                    Hb2[:, hc * 512:(hc + 1) * 512],
                    hps[:, par * 1024:par * 1024 + 512],
                    AF.Silu, bias=cb_t[:, hc:hc + 1]))

            def emit_mx(gq):
                r.wait(r.vector, sc, sc_sil[4 * gq + 3])
                for j in range(4):
                    c = 4 * gq + j
                    ja = (j % 2) * 512
                    hsl = Hb2[:, c * 512:(c + 1) * 512]
                    if j == 2:
                        ve.inc(r.vector.drain())
                    if gq == 0 and j < 2:
                        ve.inc(r.vector.tensor_copy(mxt[:, ja:ja + 512], hsl))
                        ve.inc(r.vector.tensor_copy(mxn[:, ja:ja + 512], hsl))
                    else:
                        ve.inc(r.vector.tensor_tensor(
                            mxt[:, ja:ja + 512], mxt[:, ja:ja + 512],
                            hsl, ALU.max))
                        ve.inc(r.vector.tensor_tensor(
                            mxn[:, ja:ja + 512], mxn[:, ja:ja + 512],
                            hsl, ALU.min))
                ve.inc(r.vector.drain())

            def emit_hq_batch(t, c0):
                r.wait(r.vector, sc, sc_sil[c0 + 3])
                for pair in range(2):
                    for k in range(2):
                        c = c0 + 2 * pair + k
                        ve.inc(r.vector.tensor_mul(
                            ct1[:, k * 512:(k + 1) * 512],
                            Hb2[:, c * 512:(c + 1) * 512], S2sb[:, :]))
                    ve.inc(r.vector.drain())
                    for k in range(2):
                        c = c0 + 2 * pair + k
                        ve_hq[c] = ve.inc(r.vector.tensor_scalar(
                            hqg[:, c * 512:(c + 1) * 512],
                            ct1[:, k * 512:(k + 1) * 512],
                            MAGIC, -MAGIC, ALU.add, ALU.add))
                    ve.inc(r.vector.drain())

            def emit_g1(t):
                nonlocal ve_g1
                if t > 0:
                    r.wait(r.tensor, ve, st_ch[t - 1])
                for b in range(4):
                    tn.inc(r.tensor.transpose(
                        hps[0:1, 1024 + b * 128:1024 + (b + 1) * 128],
                        g1col[:, 4 * t + b:4 * t + b + 1], ident_f32[:, :]))
                if t + 1 < NCT:
                    tn.inc(r.tensor.transpose(
                        hps[0:1, 1536:1664],
                        g1col[:, 4 * t + 4:4 * t + 5], ident_f32[:, :]))
                tn_g = tn.v
                r.wait(r.vector, tn, tn_g)
                if t == 0:
                    ve.inc(r.vector.memset(G1sb[0:1, 0:1], 0.0))
                else:
                    ve.inc(r.vector.tensor_copy(G1sb[0:1, 0:1],
                                                G1sb[0:1, 512:513]))
                ve.inc(r.vector.drain())
                ve.inc(r.vector.tensor_copy(G1sb[0:1, 1:513],
                                            hps[0:1, 1024:1536]))
                if t + 1 < NCT:
                    ve.inc(r.vector.tensor_copy(G1sb[0:1, 513:514],
                                                hps[0:1, 1536:1537]))
                else:
                    ve.inc(r.vector.memset(G1sb[0:1, 513:514], 0.0))
                ve_row = ve.inc(r.vector.drain())
                r.wait(r.tensor, ve, ve_row)
                tn.inc(r.tensor.matmul(hps[:, 1024:1536], ones1f[0:1, :],
                                       G1sb[0:1, 0:512], start=True,
                                       stop=True))
                tn.inc(r.tensor.matmul(hps[:, 1536:1538], ones1f[0:1, :],
                                       G1sb[0:1, 512:514], start=True,
                                       stop=True))
                r.wait(r.vector, tn, tn.v)
                ve.inc(r.vector.tensor_copy(G1sb[:, :], hps[:, 1024:1538]))
                ve_g1 = ve.inc(r.vector.drain())

            def emit_g1_w1_conv(t, w2t=None):
                nonlocal ve_mx

                def w1_load(k):
                    sl = k % 3
                    r.wait(r.sync, tn, slot_free[sl])
                    dl_w1[k] = dw1[sl].inc(r.sync.dma_start(
                        out=w1b[:, sl * 2048:(sl + 1) * 2048].rearrange(
                            "p (l f h) -> p l f h", l=2, f=8),
                        in_=t_w1t[2 * k:2 * k + 2].rearrange(
                            "l f p h -> p l f h")), 16)

                def dg_build(hc):
                    r.wait(r.vector, tn, tn_cv[(hc - 3) % 32])
                    for j in range(3):
                        ve_dgb[hc] = ve.inc(r.vector.tensor_scalar(
                            dgb[:, (hc % 3) * 384 + j * 128:
                                (hc % 3) * 384 + (j + 1) * 128],
                            ident_f32[:, :],
                            cw_t[:, j * 32 + hc:j * 32 + hc + 1], 0.0,
                            ALU.mult, ALU.add))

                def w1_chunk(hc, sl):
                    par = hc % 2
                    if t > 0 or hc >= 2:
                        r.wait(r.tensor, sc, sc_sil[(hc - 2) % 32])
                    if t > 0 and hc == 1:
                        r.wait(r.tensor, ve, ve_g1)
                    lbase = sl * 2048 + (hc % 2) * 1024
                    for fc in range(8):
                        wsl = w1b[:, lbase + fc * 128:lbase + (fc + 1) * 128]
                        tn.inc(r.tensor.matmul(
                            hps[:, par * 1024:par * 1024 + 257], wsl,
                            xq2[:, fc * SP2 + t * 512:
                                fc * SP2 + t * 512 + 257],
                            start=(fc == 0), stop=(fc == 7)))
                        tn.inc(r.tensor.matmul(
                            hps[:, par * 1024 + 512:par * 1024 + 769], wsl,
                            xq2[:, fc * SP2 + t * 512 + 257:
                                fc * SP2 + t * 512 + 514],
                            start=(fc == 0), stop=(fc == 7)))
                    tn_h = tn.v
                    r.wait(r.vector, tn, tn_h)
                    ve_h[hc] = ve.inc(r.vector.tensor_mul(
                        Hbq[:, hc * 514:hc * 514 + 514].rearrange(
                            "p (b c) -> p b c", b=2),
                        bass.AP(hps[:, :].tensor, par * 1024,
                                [[2048, 128], [512, 2], [1, 257]]),
                        G1sb.rearrange("p (b c) -> p b c", b=2)))

                dg_build(0)
                w1_load(0)
                w1_load(1)
                for g in range(16):
                    sl = g % 3
                    if g + 2 <= 15:
                        w1_load(g + 2)
                    dg_build(2 * g + 1)
                    if g == 0:
                        dg_build(2)
                    r.wait(r.tensor, dw1[sl], dl_w1[g])
                    if g == 0:
                        r.wait(r.tensor, ve, ve_g1)
                    w1_chunk(2 * g, sl)
                    if g > 0:
                        emit_conv_silu(t, 2 * g - 1)
                        if g < 15:
                            dg_build(2 * g + 2)
                    if w2t is not None:
                        emit_w2_sub(w2t, g // 8, 2 * (g % 8))
                        if g <= 6:
                            emit_hq_batch(w2t, 4 * (g + 1))
                    w1_chunk(2 * g + 1, sl)
                    slot_free[sl] = tn.v
                    emit_conv_silu(t, 2 * g)
                    if w2t is not None:
                        emit_w2_sub(w2t, g // 8, 2 * (g % 8) + 1)
                        if g == 7:
                            emit_final_pair(w2t, 0)
                        if g == 15:
                            emit_final_pair(w2t, 2)
                            w2_done[w2t] = tn.v
                    if g % 2 == 0 and g >= 2:
                        emit_mx(g // 2 - 1)
                emit_conv_silu(t, 31)
                emit_mx(6)
                emit_mx(7)
                ve.inc(r.vector.tensor_tensor(mxt[:, 0:512], mxt[:, 0:512],
                                              mxt[:, 512:1024], ALU.max))
                ve.inc(r.vector.tensor_tensor(mxn[:, 0:512], mxn[:, 0:512],
                                              mxn[:, 512:1024], ALU.min))
                ve.inc(r.vector.drain())
                ve.inc(r.vector.tensor_single_scalar(mxn[:, 0:512],
                                                     mxn[:, 0:512],
                                                     -1.0, ALU.mult))
                ve.inc(r.vector.drain())
                ve.inc(r.vector.tensor_tensor(
                    mxt[:, 0:512], mxt[:, 0:512], mxn[:, 0:512], ALU.max))
                ve.inc(r.vector.drain())
                ve_mx = ve.inc(r.vector.tensor_copy(ct2[:, 0:512],
                                                    mxt[:, 0:512]))
                ve.inc(r.vector.drain())

            def emit_chain(t):
                r.wait(r.tensor, ve, ve_mx)
                for b in range(4):
                    tn.inc(r.tensor.transpose(
                        hps[:, 1024 + b * 128:1024 + (b + 1) * 128],
                        ct2[:, b * 128:(b + 1) * 128], ident_f32[:, :]))
                tn_mx = tn.v
                r.wait(r.vector, tn, tn_mx)
                for b in range(4):
                    ve.inc(r.vector.reduce_max(
                        cst[:, 32 + b:33 + b],
                        hps[:, 1024 + b * 128:1024 + (b + 1) * 128],
                        axis=AX.X))
                ve.inc(r.vector.drain())
                ve.inc(r.vector.tensor_scalar_max(cst[:, 36:40], cst[:, 32:36],
                                                  1e-5))
                ve.inc(r.vector.drain())
                ve.inc(r.vector.reciprocal(cst[:, 40:44], cst[:, 36:40]))
                ve.inc(r.vector.drain())
                ve.inc(r.vector.tensor_single_scalar(cst[:, 44:48],
                                                     cst[:, 40:44],
                                                     127.0, ALU.mult))
                ve_g2 = ve.inc(r.vector.tensor_scalar(
                    g2buf[:, (t % 2) * 4:(t % 2) * 4 + 4], cst[:, 36:40],
                    mw2_bc[:, 0:1], 1.0 / 127.0, ALU.mult, ALU.mult))
                g2s[t] = g2buf[:, (t % 2) * 4:(t % 2) * 4 + 4]
                r.wait(r.tensor, ve, ve_g2)
                for b in range(4):
                    tn.inc(r.tensor.transpose(
                        hps[0:1, 1024 + b * 128:1024 + (b + 1) * 128],
                        cst[:, 44 + b:45 + b], ident_f32[:, :]))
                r.wait(r.vector, tn, tn.v)
                ve_s2 = ve.inc(r.vector.tensor_copy(S2sb[0:1, 0:512],
                                                    hps[0:1, 1024:1536]))
                r.wait(r.tensor, ve, ve_s2)
                tn_s2 = tn.inc(r.tensor.matmul(hps[:, 1024:1536],
                                               ones1f[0:1, :],
                                               S2sb[0:1, 0:512],
                                               start=True, stop=True))
                r.wait(r.vector, tn, tn_s2)
                ve.inc(r.vector.tensor_copy(S2sb[:, :], hps[:, 1024:1536]))
                ve.inc(r.vector.drain())
                st_ch[t] = ve.v

            emit_g1(0)
            for t in range(NCT):
                emit_g1_w1_conv(t, w2t=(t - 1 if t > 0 else None))
                emit_chain(t)
                if t + 1 < NCT:
                    emit_g1(t + 1)
                if t > 0:
                    r.wait(r.vector, tn, w2_done[t - 1])
                emit_hq_batch(t, 0)
            tl = NCT - 1
            for g in range(16):
                emit_w2_sub(tl, g // 8, 2 * (g % 8))
                emit_w2_sub(tl, g // 8, 2 * (g % 8) + 1)
                if g == 7:
                    emit_final_pair(tl, 0)
                if g == 15:
                    emit_final_pair(tl, 2)
                    w2_done[tl] = tn.v
                if g <= 6:
                    emit_hq_batch(tl, 4 * (g + 1))
            r.wait(r.gpsimd, dst, dst.v)
            r.emit()
            barrier()
        es_c.close()
        es_s.close()
        r.emit()

    r.wait(r.gpsimd, dst, dst.v)
    gp.inc(r.gpsimd.engine_nop())
    r.emit()
    es.close()
    nc.finalize()
    return nc


_CACHE = {}


def _get_nc(nseq, S):
    key = (nseq, S)
    if key not in _CACHE:
        _CACHE[key] = build_program(nseq, S)
    return _CACHE[key]


def kernel(L0_attn_x, L0_attn_Q_w, L0_attn_Q_b, L0_attn_K_w, L0_attn_K_b,
           L0_attn_V_w, L0_attn_V_b, L0_attn_O_w, L0_attn_O_b,
           L0_ffn_w1, L0_ffn_conv_w, L0_ffn_conv_b, L0_ffn_w2):
    import ml_dtypes
    x = np.ascontiguousarray(np.asarray(L0_attn_x, dtype=np.float32))
    B, S, _ = x.shape
    ncores = 8
    nseq = B // ncores
    nc = _get_nc(nseq, S)

    tri = np.triu(np.ones((128, 128))).astype(ml_dtypes.bfloat16)
    o8 = np.zeros((8, 128, 8), dtype=ml_dtypes.bfloat16)
    for hp in range(8):
        o8[hp, :, hp] = 1.0
    asf = lambda a: np.ascontiguousarray(np.asarray(a, dtype=np.float32))
    common = {
        "qw": asf(L0_attn_Q_w), "kw": asf(L0_attn_K_w), "vw": asf(L0_attn_V_w),
        "ow": asf(L0_attn_O_w), "qbias": asf(L0_attn_Q_b),
        "kbias": asf(L0_attn_K_b), "vbias": asf(L0_attn_V_b),
        "obias": asf(L0_attn_O_b), "w1": asf(L0_ffn_w1),
        "cw": asf(L0_ffn_conv_w), "cb": asf(L0_ffn_conv_b),
        "w2": asf(L0_ffn_w2),
        "ident_bf": np.eye(128, dtype=ml_dtypes.bfloat16),
        "ident_f32": np.eye(128, dtype=np.float32),
        "trimask": tri,
        "ones8s": o8,
    }
    in_maps = []
    for i in range(ncores):
        m = dict(common)
        m["x"] = np.ascontiguousarray(x[i * nseq:(i + 1) * nseq])
        in_maps.append(m)
    res = run_bass_kernel_spmd(nc, in_maps, core_ids=list(range(ncores)))
    out = np.concatenate([np.asarray(res.results[i]["out"])
                          for i in range(ncores)], axis=0)
    return out.astype(np.float32)
